# version 09 (final)
# speedup vs baseline: 1.7809x; 1.0089x over previous
import numpy as np
import ml_dtypes
import concourse.bass as bass
import concourse.mybir as mybir
from concourse.bass_utils import run_bass_kernel_spmd

BF16_NP = ml_dtypes.bfloat16

F32 = mybir.dt.float32
BF16 = mybir.dt.bfloat16
F16 = mybir.dt.float16
I32 = mybir.dt.int32
AF = mybir.ActivationFunctionType
ALU = mybir.AluOpType
AX = mybir.AxisListType

D = 4096
T = 8192
NCORE = 8
TO = T // NCORE
HALF = 512
NH = 32
HD = 128
KC = D // 128
INTER = 11008
INTER_PAD = 11264
NT_I = INTER_PAD // 128
SM_SCALE = float(128 ** -0.5)


def _pi_perm():
    pos = np.arange(D)
    tau, p = pos // 128, pos % 128
    R, j = tau // 8, tau % 8
    return (1024 * R + 8 * p + j).astype(np.int64)

def _pip_perm():
    pos = np.arange(INTER_PAD)
    tau, p = pos // 128, pos % 128
    Rp, j = tau // 8, tau % 8
    n = 8 * (128 * Rp + p) + j
    valid = n < INTER
    return np.where(valid, n, 0).astype(np.int64), valid

PI = _pi_perm()
PIP, PIP_VALID = _pip_perm()


class Ev:
    __slots__ = ("idx", "engine", "fn", "reads", "writes", "is_dma", "deps",
                 "sem", "incval", "tokval", "tag", "role", "alldeps")

class Prog:
    def __init__(self, nc):
        self.nc = nc
        self.events = []
        self.res = {}
        self.sems = {}
        for en in ("tensor", "vector", "scalar", "gpsimd", "sync"):
            self.sems[(en, False)] = nc.alloc_semaphore("sc_" + en)
        self.dma_sems = {}
        self.dma_last = {}

    def dma_sem(self, role):
        if role not in self.dma_sems:
            self.dma_sems[role] = self.nc.alloc_semaphore("sd_" + role)
        return self.dma_sems[role]

    def emit(self, engine, fn, reads=(), writes=(), deps=(), is_dma=False, tag="",
             role=None):
        ev = Ev()
        ev.idx = len(self.events)
        ev.engine = engine
        ev.fn = fn
        ev.reads = list(reads)
        ev.writes = list(writes)
        ev.is_dma = is_dma
        ev.deps = [d for d in deps if d is not None]
        ev.sem = None
        ev.tag = tag
        if is_dma:
            base = role or engine
            if not hasattr(self, "role_rr"):
                self.role_rr = {}
            n = self.role_rr[base] = (self.role_rr.get(base, -1) + 1) % 3
            ev.role = f"{base}.{n}"
            prev = self.dma_last.get(ev.role)
            if prev is not None:
                ev.deps.append(prev)
            self.dma_last[ev.role] = ev
        for r in ev.reads:
            st = self.res.get(r)
            if st is None:
                continue
            if st["writer"] is not None:
                ev.deps.append(st["writer"])
            st["readers"].append(ev)
        for w in ev.writes:
            st = self.res.setdefault(w, {"writer": None, "readers": []})
            if st["writer"] is not None:
                ev.deps.append(st["writer"])
            for rd in st["readers"]:
                ev.deps.append(rd)
            st["writer"] = ev
            st["readers"] = []
        self.events.append(ev)
        return ev

    def dma(self, engine, dst, src, reads=(), writes=(), deps=(), tag="", role=None, **kw):
        return self.emit(engine, lambda e: e.dma_start(dst, src, **kw),
                         reads=reads, writes=writes, deps=deps, is_dma=True, tag=tag,
                         role=role)

    def barrier(self, engine, reads, tag="end"):
        return self.emit(engine, lambda e: None, reads=reads, tag=tag)

    def claim(self, reads, writes, tag="claim"):
        return self.emit("gpsimd", lambda e: None, reads=reads, writes=writes, tag=tag)

    def buf(self, name, shape, dtype):
        if not hasattr(self, "_bufs"):
            self._bufs = {}
        if name not in self._bufs:
            self._bufs[name] = self.nc.alloc_sbuf_tensor(name, shape, dtype)
        return self._bufs[name]

    def build(self):
        for ev in self.events:
            ev.deps = [d for d in ev.deps if d is not ev]
            ev.alldeps = list(ev.deps)
        needed = set()
        for ev in self.events:
            kept = []
            seen = set()
            for d in ev.deps:
                if d.idx in seen:
                    continue
                seen.add(d.idx)
                if (d.engine == ev.engine) and not d.is_dma:
                    if ev.engine == "tensor" and d.engine == "tensor":
                        continue
                kept.append(d)
                needed.add(d.idx)
            ev.deps = kept
        counts = {}
        for ev in self.events:
            if ev.is_dma:
                ev.sem = self.dma_sem(ev.role)
            elif ev.idx in needed:
                ev.sem = self.sems[(ev.engine, False)]
            if ev.sem is not None:
                ev.incval = 16 if ev.is_dma else 1
                counts[ev.sem] = counts.get(ev.sem, 0) + ev.incval
                ev.tokval = counts[ev.sem]
        events = self.events
        with self.nc.Block() as block:
            def make(engine_name):
                def body(eng):
                    last_wait = {}
                    for ev in events:
                        if ev.engine != engine_name:
                            continue
                        wmax = {}
                        for d in ev.deps:
                            assert d.sem is not None, f"dep without token: {d.tag} <- {ev.tag}"
                            if wmax.get(d.sem, -1) < d.tokval:
                                wmax[d.sem] = d.tokval
                        for sem, val in wmax.items():
                            if last_wait.get(sem, -1) >= val:
                                continue
                            eng.wait_ge(sem, val)
                            last_wait[sem] = val
                        ins = ev.fn(eng)
                        if ev.sem is not None:
                            if ins is None:
                                ins = eng.nop()
                            ins.then_inc(ev.sem, ev.incval)
                return body
            block.tensor(make("tensor"))
            block.vector(make("vector"))
            block.scalar(make("scalar"))
            block.gpsimd(make("gpsimd"))
            block.sync(make("sync"))


class Common:
    def __init__(self, p, want_l1_masks=False, want_masks=True, want_ffn=False,
                 scratch_view=None):
        nc = p.nc
        self.p = p
        self.ones_col_bf = nc.alloc_sbuf_tensor("ones_col_bf", [128, 1], BF16)
        self.ones_row_bf = nc.alloc_sbuf_tensor("ones_row_bf", [1, 128], BF16)
        self.ones_row_f32 = nc.alloc_sbuf_tensor("ones_row_f32", [1, 128], F32)
        self.want_masks = want_masks
        if want_masks:
            self.c_sb = nc.alloc_sbuf_tensor("c_sb", [128, 1], F32)
            self.mask0 = nc.alloc_sbuf_tensor("mask0", [128, 2032], BF16)
            self.iota0 = nc.alloc_sbuf_tensor("iota0", [128, 2032], F32)
        self.want_l1_masks = want_l1_masks
        if want_l1_masks:
            self.delta_sb = nc.alloc_sbuf_tensor("delta_sb", [128, 8], F32)
            self.mask1 = nc.alloc_sbuf_tensor("mask1", [128, 8 * 128], BF16)
        self.want_ffn = want_ffn
        if want_ffn:
            self.e8_f32 = nc.alloc_sbuf_tensor("e8_f32", [8, 128], F32)
            if scratch_view is not None:
                self.scratch = scratch_view
            else:
                self.scratch = nc.alloc_sbuf_tensor("cm_scratch", [128, 20480], BF16)[:]

    def emit_setup_consts(self):
        p = self.p
        p.emit("gpsimd", lambda e: e.memset(self.ones_col_bf[:], 1.0),
               writes=["ones_col"], tag="ones_col")
        p.emit("gpsimd", lambda e: e.memset(self.ones_row_bf[:], 1.0),
               writes=["ones_row_bf"], tag="ones_row_bf")
        p.emit("gpsimd", lambda e: e.memset(self.ones_row_f32[:], 1.0),
               writes=["ones_row_f32"], tag="ones_row_f32")

    def emit_setup(self, c_in, delta_in=None):
        p = self.p
        self.emit_setup_consts()
        p.dma("sync", self.c_sb[:], c_in, writes=["c_sb"], tag="ld_c")
        p.emit("gpsimd", lambda e: e.iota(
            self.iota0[:], [[-8, 2032]], base=4096, channel_multiplier=1,
            allow_small_or_imprecise_dtypes=True), writes=["iota0"], tag="iota0")
        p.emit("vector", lambda e: e.tensor_scalar(
            self.mask0[:], self.iota0[:], 0.0, self.c_sb[:], ALU.add, ALU.is_le),
            reads=["iota0", "c_sb"], writes=["mask0"], tag="mask0")
        if self.want_l1_masks:
            p.dma("sync", self.delta_sb[:], delta_in, writes=["delta"], tag="ld_delta")
            p.emit("gpsimd", lambda e: e.iota(
                self.iota0[:, 0:128], [[-1, 128]], base=0, channel_multiplier=1,
                allow_small_or_imprecise_dtypes=True),
                reads=["mask0"], writes=["iota0"], tag="iota1")
            for r in range(8):
                p.emit("vector", lambda e, r=r: e.tensor_scalar(
                    self.mask1[:, 128 * r:128 * r + 128], self.iota0[:, 0:128],
                    0.0, self.delta_sb[:, r:r + 1], ALU.add, ALU.is_le),
                    reads=["iota0", "delta"], writes=["mask1"], tag=f"mask1_{r}")



class PostLN:
    def __init__(self, p, cm, nm, ps_sv, ps_svv, ps_a, ps_b, rsv, rsvv, rsa, rsb):
        nc = p.nc
        self.p, self.cm, self.nm = p, cm, nm
        self.ps_sv, self.ps_svv, self.ps_a, self.ps_b = ps_sv, ps_svv, ps_a, ps_b
        self.rsv, self.rsvv, self.rsa, self.rsb = rsv, rsvv, rsa, rsb
        self.xres = [nc.alloc_sbuf_tensor(nm + f"_lnxr{i}", [128, HALF], F32) for i in range(2)]
        self.vsum = [nc.alloc_sbuf_tensor(nm + f"_lnvs{i}", [128, HALF], F32) for i in range(2)]
        self.vb = nc.alloc_sbuf_tensor(nm + "_lnvb", [128, HALF], BF16)
        self.sq = nc.alloc_sbuf_tensor(nm + "_lnsq", [128, HALF], BF16)
        self.abc = nc.alloc_sbuf_tensor(nm + "_lnabc", [128, 2 * HALF], F32)
        self.xnew = [nc.alloc_sbuf_tensor(nm + f"_lnxn{i}", [128, HALF], F32) for i in range(2)]

    def add_chunk(self, io, hf, dt, src_ap, src_res, first, last):
        p, cm, nm = self.p, self.cm, self.nm
        vpre = io["x_out_res"] + "_v"
        pbuf = dt % 2
        rr = io.get("res_in_res", "__none__")
        rr = rr if isinstance(rr, list) else [rr]
        p.dma("sync", self.xres[pbuf][:],
              io["res_in"][128 * dt:128 * (dt + 1), HALF * hf:HALF * (hf + 1)],
              reads=rr, writes=[nm + f"lnxr{pbuf}"],
              tag=f"ldxr{hf}_{dt}", role=f"xres{pbuf}")
        p.emit("gpsimd", lambda e: e.tensor_add(
            self.vsum[pbuf][:], src_ap, self.xres[pbuf][:]),
            reads=[src_res, nm + f"lnxr{pbuf}"],
            writes=[nm + f"lnvs{pbuf}"], tag=f"radd{hf}_{dt}")
        p.dma("sync", io["v_dram"][128 * dt:128 * (dt + 1), HALF * hf:HALF * (hf + 1)],
              self.vsum[pbuf][:], reads=[nm + f"lnvs{pbuf}"], writes=[vpre + f"{hf}_{dt}"],
              tag=f"stv{hf}_{dt}", role=f"st{pbuf}")
        p.emit("vector", lambda e: e.tensor_copy(self.vb[:], self.vsum[pbuf][:]),
               reads=[nm + f"lnvs{pbuf}"], writes=[nm + "lnvb"], tag=f"vb{hf}_{dt}")
        p.emit("scalar", lambda e: e.activation(self.sq[:], self.vsum[pbuf][:], AF.Square),
               reads=[nm + f"lnvs{pbuf}"], writes=[nm + "lnsq"], tag=f"sq{hf}_{dt}")
        p.emit("tensor", lambda e, first=first, last=last: e.matmul(
            self.ps_sv[0:1, :], cm.ones_col_bf[:], self.vb[:], start=first, stop=last),
            reads=["ones_col", nm + "lnvb"], writes=[self.rsv], tag=f"sv{hf}_{dt}")
        p.emit("tensor", lambda e, first=first, last=last: e.matmul(
            self.ps_svv[0:1, :], cm.ones_col_bf[:], self.sq[:], start=first, stop=last),
            reads=["ones_col", nm + "lnsq"], writes=[self.rsvv], tag=f"svv{hf}_{dt}")

    def finalize(self, io, hf):
        p, cm, nm = self.p, self.cm, self.nm
        vpre = io["x_out_res"] + "_v"
        R0 = self.xnew[0][0:1, :]
        R1 = self.xnew[0][64:65, :]
        R2 = self.xnew[1][0:1, :]
        R3 = self.xres[0][0:1, :]
        R4 = self.xnew[1][64:65, :]
        rn = [nm + "lnxn0", nm + "lnxn1", nm + "lnxr0"]
        p.emit("vector", lambda e: e.tensor_scalar_mul(R0, self.ps_sv[0:1, :], 1.0 / D),
               reads=[self.rsv] + rn, writes=rn, tag=f"mean{hf}")
        p.emit("vector", lambda e: e.tensor_scalar_mul(R1, self.ps_svv[0:1, :], 1.0 / D),
               reads=[self.rsvv] + rn, writes=rn, tag=f"evv{hf}")
        p.emit("vector", lambda e: e.tensor_mul(R4, R0, R0),
               reads=rn, writes=rn, tag=f"m2{hf}")
        p.emit("vector", lambda e: e.tensor_sub(R1, R1, R4),
               reads=rn, writes=rn, tag=f"var{hf}")
        p.emit("vector", lambda e: e.tensor_scalar_add(R1, R1, 1e-5),
               reads=rn, writes=rn, tag=f"vareps{hf}")
        p.emit("scalar", lambda e: e.activation(R2, R1, AF.Ln),
               reads=rn, writes=rn, tag=f"ln{hf}")
        p.emit("scalar", lambda e: e.activation(R2, R2, AF.Exp, scale=-0.5),
               reads=rn, writes=rn, tag=f"rstd{hf}")
        p.emit("vector", lambda e: e.scalar_tensor_tensor(
            R3, R0, -1.0, R2, ALU.mult, ALU.mult),
            reads=rn, writes=rn, tag=f"brow{hf}")
        p.emit("tensor", lambda e: e.matmul(
            self.ps_a[:], cm.ones_row_f32[:], R2, start=True, stop=True),
            reads=["ones_row_f32"] + rn, writes=[self.rsa], tag=f"bca{hf}")
        p.emit("tensor", lambda e: e.matmul(
            self.ps_b[:], cm.ones_row_f32[:], R3, start=True, stop=True),
            reads=["ones_row_f32"] + rn, writes=[self.rsb], tag=f"bcb{hf}")
        p.emit("scalar", lambda e: e.copy(self.abc[:, 0:HALF], self.ps_a[:]),
               reads=[self.rsa], writes=[nm + "lnabc"], tag=f"ea{hf}")
        p.emit("scalar", lambda e: e.copy(self.abc[:, HALF:2 * HALF], self.ps_b[:]),
               reads=[self.rsb, nm + "lnabc"], writes=[nm + "lnabc"], tag=f"eb{hf}")
        for dt in range(KC):
            pbuf = dt % 2
            p.dma("sync", self.xres[pbuf][:],
                  io["v_dram"][128 * dt:128 * (dt + 1), HALF * hf:HALF * (hf + 1)],
                  reads=[vpre + f"{hf}_{dt}"], writes=[nm + f"lnxr{pbuf}"],
                  tag=f"ldv{hf}_{dt}", role=f"xres{pbuf}")
            p.emit("gpsimd", lambda e, pbuf=pbuf: e.tensor_mul(
                self.xnew[pbuf][:], self.xres[pbuf][:], self.abc[:, 0:HALF]),
                reads=[nm + f"lnxr{pbuf}", nm + "lnabc"], writes=[nm + f"lnxn{pbuf}"],
                tag=f"ap1_{hf}_{dt}")
            p.emit("gpsimd", lambda e, pbuf=pbuf: e.tensor_add(
                self.xnew[pbuf][:], self.xnew[pbuf][:], self.abc[:, HALF:2 * HALF]),
                reads=[nm + f"lnxn{pbuf}", nm + "lnabc"], writes=[nm + f"lnxn{pbuf}"],
                tag=f"ap2_{hf}_{dt}")
            p.dma("sync", io["x_out"][128 * dt:128 * (dt + 1), HALF * hf:HALF * (hf + 1)],
                  self.xnew[pbuf][:], reads=[nm + f"lnxn{pbuf}"],
                  writes=[io["x_out_res"] + f"_{hf}_{dt}"],
                  tag=f"stx{hf}_{dt}", role=f"st{pbuf}")


def emit_attention_layer(p, cm, L, io):
    nc = p.nc
    nm = L["nm"]
    l0 = L["kv_mode"] == "l0"

    ar = io.get("arena")
    if ar is None:
        xn = p.buf(nm + "_xn", [128, KC * HALF], BF16)
        attn = p.buf(nm + "_attn", [128, NH * HALF], BF16)
        k_sb = p.buf(nm + "_k", [128, T], BF16)
        v_sb = p.buf(nm + "_v", [128, T], BF16)
        wq_sb = [p.buf(nm + f"_wq{i}", [128, KC * 128], BF16) for i in range(2)]
        wo_sb = [p.buf(nm + f"_wo{i}", [128, NH * 128], BF16) for i in range(2)]
    else:
        xn, attn, k_sb, v_sb = ar["xn"], ar["attn"], ar["k"], ar["v"]
        wq_sb, wo_sb = ar["wq"], ar["wo"]
    q_sb = [p.buf(nm + f"_q{i}", [128, HALF], BF16) for i in range(2)]
    exp_sb = [p.buf(nm + f"_exp{i}", [128, HALF], BF16) for i in range(4)]
    av_sb = p.buf(nm + "_av", [128, HALF], BF16)
    rec_sb = p.buf(nm + "_rec", [1, HALF], F32)
    if io.get("proj_t") is not None:
        proj_sb = io["proj_t"]
        proj_res = io["proj_res"]
    else:
        proj_sb = [p.buf(nm + f"_proj{i}", [128, HALF], F32)[:] for i in range(2)]
        proj_res = [nm + f"proj{i}" for i in range(2)]
    if l0:
        qb_sb = p.buf(nm + "_qb", [128, NH], F32)
        kb_sb = p.buf(nm + "_kb", [128, 1], F32)
        vbias_sb = p.buf(nm + "_vbias", [1, 128], F32)
        vbc_sb = p.buf(nm + "_vbc", [128, 128], F32)
        ob_sb = p.buf(nm + "_ob", [128, KC], F32)
        wkv_sb = io.get("wkv_t")
        if wkv_sb is None:
            wkv_sb = p.buf(nm + "_wkv", [128, 2 * KC * 128], BF16)[:]
        xs_sb = [xn, attn]
        xs_res = [nm + "xn", nm + "attn"]

    if io.get("pb") is not None:
        pb = io["pb"]
    else:
        pb = [nc.alloc_psum_tensor(nm + f"_pb{i}", [128, 512], F32) for i in range(8)]
        io["pb"] = pb
        io["pb_res"] = [nm + f"_pb{i}" for i in range(8)]
    ps_sc = [pb[0], pb[1], pb[6], pb[4]]
    sc_R = [0, 1, 6, 4]
    ps_av, ps_sum, ps_q, ps_bc, ps_proj, ps_misc = pb[2], pb[3], pb[4], pb[5], pb[6], pb[7]
    R = {i: io["pb_res"][i] for i in range(8)}
    pln = io.get("_pln") or PostLN(p, cm, nm, ps_sum, ps_misc, ps_bc, ps_q,
                                   R[3], R[7], R[5], R[4])
    io["_xn_t"] = xn
    io["_pln"] = pln
    io["_wq"] = wq_sb
    io["_wo"] = wo_sb
    io["_attn_t"] = attn
    io["_pb"] = pb
    io["_pb_res"] = [R[i] for i in range(8)]

    if l0:
        p.dma("sync", qb_sb[:], io["qb"], writes=[nm + "qb"], tag="ld_qb")
        p.dma("sync", kb_sb[:], io["kb"], writes=[nm + "kb"], tag="ld_kb")
        p.dma("sync", vbias_sb[:], io["vbias"], writes=[nm + "vbias"], tag="ld_vbias")
        p.dma("sync", ob_sb[:], io["ob"], writes=[nm + "ob"], tag="ld_ob")
        for idx, nmw in enumerate(("wk", "wv")):
            dst = wkv_sb[:, idx * KC * 128:(idx + 1) * KC * 128]
            p.dma("gpsimd", dst, io[nmw], writes=[nm + "wkv"], tag="ld_" + nmw, role="wq")
        p.emit("tensor", lambda e: e.matmul(
            ps_bc[:, 0:128], cm.ones_row_f32[:], vbias_sb[:], start=True, stop=True),
            reads=["ones_row_f32", nm + "vbias"], writes=[R[5]], tag="vb_bc")
        p.emit("scalar", lambda e: e.copy(vbc_sb[:], ps_bc[:, 0:128]),
               reads=[R[5]], writes=[nm + "vbc"], tag="vb_evac")

    if l0:
        NS = T // HALF
        for ss in range(NS):
            buf = ss % 2
            dst = xs_sb[buf][:, 0:KC * HALF]
            src = io["xT"][:, KC * HALF * ss:KC * HALF * (ss + 1)]
            p.dma("gpsimd", dst, src, writes=[xs_res[buf]], tag=f"xs{ss}", role="xs")
            for kc in range(KC):
                p.emit("tensor", lambda e, kc=kc, buf=buf: e.matmul(
                    ps_misc[:], wkv_sb[:, 128 * kc:128 * kc + 128],
                    xs_sb[buf][:, HALF * kc:HALF * (kc + 1)],
                    start=(kc == 0), stop=(kc == KC - 1)),
                    reads=[nm + "wkv", xs_res[buf]], writes=[R[7]], tag=f"kmm{ss}_{kc}")
            p.emit("scalar", lambda e, ss=ss: e.add(
                k_sb[:, HALF * ss:HALF * (ss + 1)], ps_misc[:], kb_sb[:, 0:1]),
                reads=[R[7], nm + "kb"], writes=[nm + "k"], tag=f"kev{ss}")
            for st in range(4):
                for kc in range(KC):
                    p.emit("tensor", lambda e, kc=kc, st=st, buf=buf: e.matmul(
                        ps_q[:, 0:128],
                        xs_sb[buf][:, HALF * kc + 128 * st: HALF * kc + 128 * st + 128],
                        wkv_sb[:, KC * 128 + 128 * kc: KC * 128 + 128 * kc + 128],
                        start=(kc == 0), stop=(kc == KC - 1)),
                        reads=[nm + "wkv", xs_res[buf]], writes=[R[4]],
                        tag=f"vmm{ss}_{st}_{kc}")
                col = 128 * (4 * ss + st)
                p.emit("vector", lambda e, col=col: e.tensor_add(
                    v_sb[:, col:col + 128], ps_q[:, 0:128], vbc_sb[:]),
                    reads=[R[4], nm + "vbc"], writes=[nm + "v"], tag=f"vev{ss}_{st}")

    def tile_cfg_l0(K, hf):
        gK8 = 16 * K - HALF * hf
        i0 = max(0, -(-(8 * gK8 - 7) // 8))
        if i0 >= HALF:
            return None
        i1 = min(HALF, max(i0, -(-(8 * gK8 + 128) // 8)))
        off = 512 - gK8 + i0
        return (i0, i1, lambda w: cm.mask0[:, off:off + w])

    def tile_cfg_l1(r, B, hf):
        w0 = 128 * B - HALF * hf
        i0 = max(0, w0)
        if i0 >= HALF:
            return None
        i1 = min(HALF, w0 + 128)
        if i1 <= i0:
            return (i0, i0, None)
        moff = 128 * r + (i0 - w0)
        return (i0, i1, lambda w: cm.mask1[:, moff:moff + w])

    for hf in range(2):
        if io.get("x_in_c") is not None:
            p.dma("gpsimd", xn[:], io["x_in_c"][:, KC * HALF * hf:KC * HALF * (hf + 1)],
                  reads=io.get("x_in_res", []), writes=[nm + "xn"],
                  tag=f"xn{hf}", role="xs")
        else:
            dst = xn[:].rearrange("p (a s) -> p a s", a=KC)
            src = io["x_in"][:, HALF * hf:HALF * (hf + 1)].rearrange("(a p) s -> p a s", p=128)
            p.dma("gpsimd", dst, src, reads=io.get("x_in_res", []), writes=[nm + "xn"],
                  tag=f"xn{hf}", role="xs")

        def make_tail(h, hf=hf):
            def tail():
                p.emit("vector", lambda e: e.reciprocal(rec_sb[:], ps_sum[0:1, :]),
                       reads=[R[3]], writes=[nm + "rec"], tag=f"rec{hf}_{h}")
                p.emit("tensor", lambda e: e.matmul(
                    ps_bc[:], cm.ones_row_f32[:], rec_sb[:], start=True, stop=True),
                    reads=["ones_row_f32", nm + "rec"], writes=[R[5]], tag=f"rbc{hf}_{h}")
                p.emit("scalar", lambda e: e.copy(av_sb[:], ps_av[:]),
                       reads=[R[2]], writes=[nm + "av"], tag=f"avev{hf}_{h}")
                p.emit("vector", lambda e, h=h: e.tensor_mul(
                    attn[:, HALF * h:HALF * (h + 1)], av_sb[:], ps_bc[:]),
                    reads=[nm + "av", R[5]], writes=[nm + "attn"], tag=f"at{hf}_{h}")
            return tail

        pend_tail = None
        for h in range(NH):
            wbuf = h % 2
            if hf == 0 or True:
                pass
            if h == 0 and hf == 0:
                p.dma("gpsimd", wq_sb[0][:], io["wq"][0],
                      writes=[nm + "wq0"], tag=f"ldwq0_{hf}", role="wq")
            nh_ = h + 1 if h + 1 < NH else (0 if hf == 0 else None)
            if nh_ is not None:
                nbuf = nh_ % 2
                p.dma("gpsimd", wq_sb[nbuf][:], io["wq"][nh_],
                      writes=[nm + f"wq{nbuf}"], tag=f"ldwq{nh_}_{hf}", role="wq")
            if (not l0) and h % 4 == 0:
                kh = h // 4
                for r in range(8):
                    p.dma("sync", k_sb[:, 1024 * r:1024 * (r + 1)],
                          io["gathered"][2048 * r + 128 * kh: 2048 * r + 128 * kh + 128, :],
                          reads=[io["gathered_res"]], writes=[nm + f"k{r}"], tag=f"ldk{kh}_{r}", role="kv")
                    dstv = v_sb[:, 1024 * r:1024 * (r + 1)].rearrange("p (b m) -> p b m", b=8)
                    srcv = io["gathered"][2048 * r + 1024: 2048 * (r + 1),
                                          128 * kh:128 * kh + 128].rearrange(
                        "(b p) m -> p b m", p=128)
                    p.dma("sync", dstv, srcv, reads=[io["gathered_res"]],
                          writes=[nm + f"v{r}"], tag=f"ldv{kh}_{r}", role="kv")
            for kc in range(KC):
                p.emit("tensor", lambda e, kc=kc, wbuf=wbuf: e.matmul(
                    ps_q[:], wq_sb[wbuf][:, 128 * kc:128 * kc + 128],
                    xn[:, HALF * kc:HALF * (kc + 1)],
                    start=(kc == 0), stop=(kc == KC - 1)),
                    reads=[nm + f"wq{wbuf}", nm + "xn"], writes=[R[4]],
                    tag=f"qmm{hf}_{h}_{kc}")
            qbuf = h % 2
            if l0:
                p.emit("scalar", lambda e, qbuf=qbuf, h=h: e.add(
                    q_sb[qbuf][:], ps_q[:], qb_sb[:, h:h + 1]),
                    reads=[R[4], nm + "qb"], writes=[nm + f"q{qbuf}"], tag=f"qev{hf}_{h}")
            else:
                p.emit("scalar", lambda e, qbuf=qbuf: e.copy(q_sb[qbuf][:], ps_q[:]),
                       reads=[R[4]], writes=[nm + f"q{qbuf}"], tag=f"qev{hf}_{h}")
            if pend_tail is not None:
                pend_tail()
                pend_tail = None

            if l0:
                tiles = []
                for K in range(T // 128):
                    cfg = tile_cfg_l0(K, hf)
                    if cfg is not None:
                        tiles.append((k_sb[:, 128 * K:128 * K + 128], nm + "k",
                                      v_sb[:, 128 * K:128 * K + 128], nm + "v", cfg))
            else:
                tiles = []
                for B in range(8):
                    for r in range(8):
                        cfg = tile_cfg_l1(r, B, hf)
                        if cfg is not None:
                            o = 1024 * r + 128 * B
                            tiles.append((k_sb[:, o:o + 128], nm + f"k{r}",
                                          v_sb[:, o:o + 128], nm + f"v{r}", cfg))
            n_tiles = len(tiles)

            def rest_stage(idx, h=h, hf=hf, qbuf=qbuf, tiles=tiles, n_tiles=n_tiles):
                k_ap, k_res, v_ap, v_res, cfg = tiles[idx]
                i0, i1, mfn = cfg
                sb = idx % 4
                rb = sc_R[sb]
                first = idx == 0
                last = idx == n_tiles - 1
                p.emit("scalar", lambda e, sb=sb, i0=i0: e.activation(
                    exp_sb[sb][:, i0:HALF], ps_sc[sb][:, i0:HALF], AF.Exp,
                    scale=SM_SCALE),
                    reads=[R[rb]], writes=[nm + f"exp{sb}"], tag=f"ex{hf}_{h}_{idx}")
                if i1 > i0 and mfn is not None:
                    p.emit("vector", lambda e, sb=sb, i0=i0, i1=i1, mfn=mfn: e.tensor_mul(
                        exp_sb[sb][:, i0:i1], exp_sb[sb][:, i0:i1], mfn(i1 - i0)),
                        reads=["mask0" if l0 else "mask1", nm + f"exp{sb}"],
                        writes=[nm + f"exp{sb}"], tag=f"mk{hf}_{h}_{idx}")
                p.emit("tensor", lambda e, sb=sb, i0=i0, v_ap=v_ap, first=first, last=last: e.matmul(
                    ps_av[:, i0:HALF], v_ap, exp_sb[sb][:, i0:HALF],
                    start=first, stop=last),
                    reads=[v_res, nm + f"exp{sb}"], writes=[R[2]],
                    tag=f"av{hf}_{h}_{idx}")
                p.emit("tensor", lambda e, sb=sb, i0=i0, first=first, last=last: e.matmul(
                    ps_sum[0:1, i0:HALF], cm.ones_col_bf[:], exp_sb[sb][:, i0:HALF],
                    start=first, stop=last),
                    reads=["ones_col", nm + f"exp{sb}"], writes=[R[3]],
                    tag=f"su{hf}_{h}_{idx}")

            LA = 2
            for idx, (k_ap, k_res, v_ap, v_res, cfg) in enumerate(tiles):
                i0 = cfg[0]
                sb = idx % 4
                rb = sc_R[sb]
                p.emit("tensor", lambda e, sb=sb, k_ap=k_ap, i0=i0, qbuf=qbuf: e.matmul(
                    ps_sc[sb][:, i0:HALF], k_ap, q_sb[qbuf][:, i0:HALF],
                    start=True, stop=True),
                    reads=[k_res, nm + f"q{qbuf}"], writes=[R[rb]],
                    tag=f"sc{hf}_{h}_{idx}")
                if idx >= LA:
                    rest_stage(idx - LA)
            for j in range(max(0, n_tiles - LA), n_tiles):
                rest_stage(j)
            pend_tail = make_tail(h)
        if pend_tail is not None:
            pend_tail()
            pend_tail = None

        if hf == 0:
            p.dma("gpsimd", wo_sb[0][:], io["wo"][0],
                  writes=[nm + "wo0"], tag=f"ldwo0_{hf}", role="wo")
        pending = []
        for dt in range(KC):
            wbuf = dt % 2
            ndt = dt + 1 if dt + 1 < KC else (0 if hf == 0 else None)
            if ndt is not None:
                nbuf = ndt % 2
                p.dma("gpsimd", wo_sb[nbuf][:], io["wo"][ndt],
                      writes=[nm + f"wo{nbuf}"], tag=f"ldwo{ndt}_{hf}", role="wo")
            pbank = ps_proj if dt % 2 == 0 else ps_av
            pR = R[6] if dt % 2 == 0 else R[2]
            for h in range(NH):
                p.emit("tensor", lambda e, h=h, wbuf=wbuf, pbank=pbank: e.matmul(
                    pbank[:], wo_sb[wbuf][:, 128 * h:128 * h + 128],
                    attn[:, HALF * h:HALF * (h + 1)],
                    start=(h == 0), stop=(h == NH - 1)),
                    reads=[nm + f"wo{wbuf}", nm + "attn"], writes=[pR],
                    tag=f"pj{hf}_{dt}_{h}")
            pbuf = dt % 2
            if l0:
                p.emit("scalar", lambda e, pbuf=pbuf, dt=dt, pbank=pbank: e.add(
                    proj_sb[pbuf], pbank[:], ob_sb[:, dt:dt + 1]),
                    reads=[pR, nm + "ob"], writes=[proj_res[pbuf]],
                    tag=f"pjev{hf}_{dt}")
            else:
                p.emit("scalar", lambda e, pbuf=pbuf, pbank=pbank: e.copy(
                    proj_sb[pbuf], pbank[:]),
                    reads=[pR], writes=[proj_res[pbuf]], tag=f"pjev{hf}_{dt}")
            pending.append((dt, pbuf))
            if len(pending) > 1:
                d0, pb0_ = pending.pop(0)
                pln.add_chunk(io, hf, d0, proj_sb[pb0_], proj_res[pb0_],
                              first=(d0 == 0), last=(d0 == KC - 1))
        for d0, pb0_ in pending:
            pln.add_chunk(io, hf, d0, proj_sb[pb0_], proj_res[pb0_],
                          first=(d0 == 0), last=(d0 == KC - 1))
        pln.finalize(io, hf)



def emit_ffn_setup(p, cm, io):
    nc = p.nc
    e8_bf = nc.alloc_sbuf_tensor("e8_bf", [8, 128], BF16)
    p.dma("sync", cm.e8_f32[:], io["e8"], writes=["e8f"], tag="ld_e8", role="misc")
    p.emit("vector", lambda e: e.tensor_copy(e8_bf[:], cm.e8_f32[:]),
           reads=["e8f"], writes=["e8"], tag="cv_e8")
    ps = io["pb"][0]
    zsb = cm.scratch.bitcast(I32)[0:96, 0:1376]
    tmp = cm.scratch.bitcast(I32)[0:96, 1376:2752]
    zunp = cm.scratch[0:96, 5504:5504 + 1408]
    zrow = cm.scratch[0:8, 6912:6912 + INTER_PAD]
    stage = cm.scratch[0:128, 18176:18176 + 2816]
    SCR = "scratch"
    specs = [
        ("g", io["gate_zeros"], 32, 1376, 10, io["z1g"], INTER_PAD, 4, io["zbc_g"]),
        ("u", io["up_zeros"], 32, 1376, 10, io["z1u"], INTER_PAD, 4, io["zbc_u"]),
        ("d", io["down_zeros"], 86, 512, 4, io["z1d"], D, 11, io["zbc_d"]),
    ]
    for (mn, zin, G, KW, NB, z1d, W, RC, zbcd) in specs:
        p.dma("sync", zsb[0:G, 0:KW], zin, reads=[SCR], writes=[SCR],
              tag=f"ldz_{mn}", role="misc")
        p.emit("gpsimd", lambda e, G=G, KW=KW: e.memset(zunp[0:G, KW:KW + 32], 0.0),
               reads=[SCR], writes=[SCR], tag=f"zpad_{mn}")
        for j2 in range(8):
            p.emit("vector", lambda e, j2=j2, G=G, KW=KW: e.tensor_scalar(
                tmp[0:G, 0:KW], zsb[0:G, 0:KW], 4 * j2, 15,
                ALU.logical_shift_right, ALU.bitwise_and),
                reads=[SCR], writes=[SCR], tag=f"zs_{mn}_{j2}")
            p.emit("vector", lambda e, G=G, KW=KW: e.tensor_scalar_add(
                tmp[0:G, 0:KW], tmp[0:G, 0:KW], 1),
                reads=[SCR], writes=[SCR], tag=f"z1_{mn}_{j2}")
            p.emit("vector", lambda e, j2=j2, G=G, KW=KW: e.tensor_copy(
                zunp[0:G, 0:KW], tmp[0:G, 0:KW]),
                reads=[SCR], writes=[SCR], tag=f"zc_{mn}_{j2}")
            nfull = (KW // 128) * 128
            from concourse.ap import AP as _AP
            if nfull > 0:
                dst = _AP(z1d.tensor, 128 * j2,
                          [[W, G], [1024, nfull // 128], [1, 128]])
                p.dma("sync", dst, zunp[0:G, 0:nfull].rearrange(
                    "g (a k) -> g a k", k=128),
                    reads=[SCR], writes=[f"z1_{mn}"], tag=f"zst_{mn}_{j2}", role="misc")
            if KW > nfull:
                dst = _AP(z1d.tensor, 128 * j2 + 1024 * (KW // 128),
                          [[W, G], [1, KW - nfull + 32]])
                p.dma("sync", dst, zunp[0:G, nfull:KW + 32],
                    reads=[SCR], writes=[f"z1_{mn}"], tag=f"zstt_{mn}_{j2}", role="misc")
        for Rr in range(RC):
            gr = min(8, G - 8 * Rr)
            p.dma("sync", zrow[0:gr, 0:W], z1d[8 * Rr:8 * Rr + gr, :],
                  reads=[f"z1_{mn}", SCR], writes=[SCR], tag=f"zrow_{mn}_{Rr}", role="misc")
            nseg = (W + 2815) // 2816
            for seg in range(nseg):
                w0 = 2816 * seg
                w1 = min(W, w0 + 2816)
                for ns in range(w0, w1, 512):
                    nw = min(512, w1 - ns)
                    p.emit("tensor", lambda e, gr=gr, ns=ns, nw=nw: e.matmul(
                        ps[:, 0:nw], e8_bf[0:gr, :], zrow[0:gr, ns:ns + nw],
                        start=True, stop=True),
                        reads=["e8", SCR], writes=[io["pb_res"][0]], tag=f"zbm_{mn}_{Rr}_{ns}")
                    p.emit("scalar", lambda e, ns=ns, w0=w0, nw=nw: e.copy(
                        stage[:, ns - w0:ns - w0 + nw], ps[:, 0:nw]),
                        reads=[io["pb_res"][0]], writes=["stage"], tag=f"zbe_{mn}_{Rr}_{ns}")
                p.dma("sync", zbcd[Rr][:, w0:w1], stage[:, 0:w1 - w0],
                      reads=["stage"], writes=[f"zbc_{mn}"], tag=f"zbs_{mn}_{Rr}_{seg}",
                      role="misc") if ns + 512 >= w1 else None
    return


def emit_ffn_layer(p, cm, L, io):
    nc = p.nc
    nm = L["nm"]
    if io.get("xn_t") is not None:
        xn = io["xn_t"]
        xn_res = io["xn_res"]
    else:
        xn = nc.alloc_sbuf_tensor(nm + "_xn", [128, KC * HALF], BF16)
        xn_res = nm + "xn"
    io["_ffn_xn"] = xn
    io["_ffn_xn_res"] = xn_res
    inter = io["inter_t"]
    p.emit("vector", lambda e: None, reads=[],
           writes=[nm + "inter"] + io.get("inter_claims", ["scratch", "stage"]),
           tag="inter_claim")
    ea = io.get("ea")
    if ea is None:
        qw = {m: [nc.alloc_sbuf_tensor(nm + f"_qw{m}{i}", [128, 512], I32)[:] for i in range(2)]
              for m in "gud"}
        sbc = {m: [nc.alloc_sbuf_tensor(nm + f"_sbc{m}{i}", [128, 512], BF16)[:] for i in range(2)]
               for m in "gud"}
        zbc = {m: [nc.alloc_sbuf_tensor(nm + f"_zbc{m}{i}", [128, 512], BF16)[:] for i in range(2)]
               for m in "gud"}
        qi = {m: [nc.alloc_sbuf_tensor(nm + f"_qi{m}{i}", [128, 512], I32)[:] for i in range(2)]
              for m in "gu"}
        qbf = {m: [nc.alloc_sbuf_tensor(nm + f"_qbf{m}{i}", [128, 512], BF16)[:] for i in range(2)]
               for m in "gu"}
        sg_sb = nc.alloc_sbuf_tensor(nm + "_sg", [128, HALF], BF16)[:]
        f16_sb = nc.alloc_sbuf_tensor(nm + "_f16", [128, HALF], F16)[:]
        fsub = [nc.alloc_sbuf_tensor(nm + f"_fsub{i}", [128, HALF], F32)[:] for i in range(2)]
    else:
        eai = ea.bitcast(I32)
        eaf = ea.bitcast(F32)
        eah = ea.bitcast(F16)
        qw = {m: [eai[:, 512 * (2 * mi + i):512 * (2 * mi + i) + 512] for i in range(2)]
              for mi, m in enumerate("gud")}
        qi = {m: [eai[:, 3072 + 512 * i:3584 + 512 * i] for i in range(2)] for m in "gu"}
        sbc = {m: [ea[:, 8192 + 512 * (2 * mi + i):8192 + 512 * (2 * mi + i) + 512]
                   for i in range(2)] for mi, m in enumerate("gu")}
        zbc = {m: [ea[:, 10240 + 512 * (2 * mi + i):10240 + 512 * (2 * mi + i) + 512]
                   for i in range(2)] for mi, m in enumerate("gu")}
        sbc["d"] = sbc["g"]
        zbc["d"] = zbc["g"]
        qbf = {"g": [ea[:, 12288 + 512 * i:12800 + 512 * i] for i in range(2)],
               "u": [ea[:, 13312 + 512 * i:13824 + 512 * i] for i in range(2)]}
        sg_sb = ea[:, 14336:14848]
        fsub = [eaf[:, 7936 + 512 * i:8448 + 512 * i] for i in range(2)]
        f16_sb = eah[:, 14848:15360]
    qi["d"] = qi["g"]
    qbf["d"] = qbf["g"]
    wt = qbf
    mres = {"g": "g", "u": "u", "d": "g"}
    if ea is None:
        sres = {"g": "g", "u": "u", "d": "d"}
        qires = {"g": "g", "u": "u", "d": "g"}
    else:
        sres = {"g": "g", "u": "u", "d": "g"}
        qires = {"g": "g", "u": "g", "d": "g"}

    pb = io["pb"]
    R = {i: io["pb_res"][i] for i in range(8)}
    if io.get("swap_banks"):
        gate_ps, up_ps = pb[4:8], pb[0:4]
        gate_R = [4, 5, 6, 7]
        up_R = [0, 1, 2, 3]
    else:
        gate_ps, up_ps = pb[0:4], pb[4:8]
        gate_R = [0, 1, 2, 3]
        up_R = [4, 5, 6, 7]
    down_ps = [pb[0], pb[1], pb[2], pb[6]]
    down_R = [R[0], R[1], R[2], R[6]]
    pln = io.get("_pln") or PostLN(p, cm, nm, pb[3], pb[7], pb[5], pb[4],
                                   R[3], R[7], R[5], R[4])

    def dequant(m, par, j, Rr, kcols, tagp):
        jb = j % 2
        mr = mres[m]
        sr = sres[m]
        qr = qires[m]
        p.emit("vector", lambda e: e.tensor_scalar(
            qi[m][jb][0:kcols, :], qw[m][par][0:kcols, :], 4 * j, 15,
            ALU.logical_shift_right, ALU.bitwise_and),
            reads=[nm + f"qw{m}{par}"], writes=[nm + f"qi{qr}{jb}"], tag=f"dq1{tagp}")
        p.emit("scalar", lambda e: e.copy(
            qbf[m][jb][0:kcols, :], qi[m][jb][0:kcols, :]),
            reads=[nm + f"qi{qr}{jb}"], writes=[nm + f"qbf{mr}{jb}"], tag=f"dq2{tagp}")
        p.emit("vector", lambda e: e.tensor_sub(
            qbf[m][jb][0:kcols, :], qbf[m][jb][0:kcols, :], zbc[m][par][0:kcols, :]),
            reads=[nm + f"qbf{mr}{jb}", nm + f"zbc{sr}{par}"],
            writes=[nm + f"qbf{mr}{jb}"], tag=f"dq3{tagp}")
        p.emit("vector", lambda e: e.tensor_mul(
            qbf[m][jb][0:kcols, :], qbf[m][jb][0:kcols, :], sbc[m][par][0:kcols, :]),
            reads=[nm + f"qbf{mr}{jb}", nm + f"sbc{sr}{par}"],
            writes=[nm + f"qbf{mr}{jb}"], tag=f"dq4{tagp}")
        return qbf[m][jb], nm + f"qbf{mr}{jb}"

    def load_tiles(m, par, Rr, cols, c0, kcols, tagp):
        p.dma("sync", qw[m][par][0:kcols, :], io[f"qw_{m}"][128 * Rr:128 * Rr + kcols,
                                                           c0:c0 + cols],
              reads=[], writes=[nm + f"qw{m}{par}"], tag=f"lq{tagp}", role="qw")
        sr = sres[m]
        p.dma("gpsimd", sbc[m][par][:, 0:cols], io[f"sbc_{m}"][Rr][:, c0:c0 + cols],
              reads=[], writes=[nm + f"sbc{sr}{par}"], tag=f"ls{tagp}", role="sbc")
        p.dma("sync", zbc[m][par][:, 0:cols], io[f"zbc_{m}"][Rr][:, c0:c0 + cols],
              reads=[f"zbc_{m}", "__none"], writes=[nm + f"zbc{sr}{par}"], tag=f"lz{tagp}",
              role="zbc")

    for hf in range(2):
        dst = xn[:].rearrange("p (a s) -> p a s", a=KC)
        srcx = io["x_in"][:, HALF * hf:HALF * (hf + 1)].rearrange("(a p) s -> p a s", p=128)
        p.dma("gpsimd", dst, srcx, reads=io.get("x_in_res", []), writes=[xn_res],
              tag=f"fxn{hf}", role="xs")
        for ns in range(INTER_PAD // 512):
            for Rr in range(4):
                par = Rr % 2
                load_tiles("g", par, Rr, 512, 512 * ns, 128, f"g{hf}_{ns}_{Rr}")
                load_tiles("u", par, Rr, 512, 512 * ns, 128, f"u{hf}_{ns}_{Rr}")
                for j in range(8):
                    wg, wg_res = dequant("g", par, j, Rr, 128, f"g{hf}_{ns}_{Rr}_{j}")
                    NTO = io.get("nt_order", [0, 1, 2, 3])
                    if io.get("dbg_wg") is not None and hf == 0 and ns == 0 and Rr == 0 and j == 1:
                        p.dma("sync", io["dbg_wg"], wg, reads=[wg_res],
                              writes=["dbgwg"], tag="dbgwg", role="misc")
                        p.dma("sync", io["dbg_sbc"], sbc["g"][par][:, 0:512], reads=[nm + "sbcg" + str(par)],
                              writes=["dbgsbc"], tag="dbgsbc", role="misc")
                        p.dma("sync", io["dbg_zbc"], zbc["g"][par][:, 0:512], reads=[nm + "zbcg" + str(par)],
                              writes=["dbgzbc"], tag="dbgzbc", role="misc")
                    tau = 8 * Rr + j
                    for nt in NTO:
                        p.emit("tensor", lambda e, nt=nt, wg=wg, tau=tau, Rr=Rr, j=j: e.matmul(
                            gate_ps[nt][:], wg[:, 128 * nt:128 * nt + 128],
                            xn[:, HALF * tau:HALF * (tau + 1)],
                            start=(Rr == 0 and j == 0), stop=(Rr == 3 and j == 7)),
                            reads=[wg_res, xn_res], writes=[R[gate_R[nt]]],
                            tag=f"gm{hf}_{ns}_{Rr}_{j}_{nt}")
                    wu, wu_res = dequant("u", par, j, Rr, 128, f"u{hf}_{ns}_{Rr}_{j}")
                    for nt in NTO:
                        p.emit("tensor", lambda e, nt=nt, wu=wu, tau=tau, Rr=Rr, j=j: e.matmul(
                            up_ps[nt][:], wu[:, 128 * nt:128 * nt + 128],
                            xn[:, HALF * tau:HALF * (tau + 1)],
                            start=(Rr == 0 and j == 0), stop=(Rr == 3 and j == 7)),
                            reads=[wu_res, xn_res], writes=[R[up_R[nt]]],
                            tag=f"um{hf}_{ns}_{Rr}_{j}_{nt}")
            if io.get("dbg_ps") is not None and hf == 0 and ns == 0:
                p.emit("scalar", lambda e: e.copy(fsub[0], gate_ps[3][:]),
                       reads=[R[gate_R[3]]], writes=[nm + "fsub0"], tag="dbgpse")
                p.dma("sync", io["dbg_ps"], fsub[0], reads=[nm + "fsub0"],
                      writes=["dbgps"], tag="dbgps", role="misc")
                p.emit("scalar", lambda e: e.copy(fsub[0], up_ps[3][:]),
                       reads=[R[up_R[3]], "dbgps"], writes=[nm + "fsub0"], tag="dbgpse2")
                p.dma("sync", io["dbg_ps2"], fsub[0], reads=[nm + "fsub0"],
                      writes=["dbgps2"], tag="dbgps2", role="misc")
            for nt in range(4):
                p.emit("scalar", lambda e, nt=nt: e.activation(
                    sg_sb, gate_ps[nt][:], AF.Silu),
                    reads=[R[gate_R[nt]]], writes=[nm + "sg"], tag=f"sl{hf}_{ns}_{nt}")
                slot = 4 * ns + nt
                p.emit("vector", lambda e, nt=nt, slot=slot: e.tensor_mul(
                    inter[:, HALF * slot:HALF * (slot + 1)], sg_sb, up_ps[nt][:]),
                    reads=[nm + "sg", R[up_R[nt]]], writes=[nm + "inter"],
                    tag=f"im{hf}_{ns}_{nt}")
        if io.get("dbg_inter") is not None and hf == 0:
            p.dma("sync", io["dbg_inter"], inter[:, 0:4096], reads=[nm + "inter"],
                  writes=["dbgi"], tag="dbgi", role="misc")
            p.dma("sync", io["dbg_xn"], xn[:, 0:4096], reads=[xn_res],
                  writes=["dbgx"], tag="dbgx", role="misc")
        dpending = []
        for ds in range(8):
            for Rr in range(11):
                par = Rr % 2
                kcols = 96 if Rr == 10 else 128
                load_tiles("d", par, Rr, 512, 512 * ds, kcols, f"d{hf}_{ds}_{Rr}")
                for j in range(8):
                    wd, wd_res = dequant("d", par, j, Rr, kcols, f"d{hf}_{ds}_{Rr}_{j}")
                    tau = 8 * Rr + j
                    for dtl in range(4):
                        p.emit("tensor", lambda e, dtl=dtl, wd=wd, tau=tau, kcols=kcols, Rr=Rr, j=j: e.matmul(
                            down_ps[dtl][:], wd[0:kcols, 128 * dtl:128 * dtl + 128],
                            inter[0:kcols, HALF * tau:HALF * (tau + 1)],
                            start=(Rr == 0 and j == 0), stop=(Rr == 10 and j == 7)),
                            reads=[wd_res, nm + "inter"], writes=[down_R[dtl]],
                            tag=f"dm{hf}_{ds}_{Rr}_{j}_{dtl}")
            for dtl in range(4):
                dt = 4 * ds + dtl
                pbuf = dt % 2
                p.emit("scalar", lambda e, dtl=dtl: e.copy(f16_sb, down_ps[dtl][:]),
                       reads=[down_R[dtl]], writes=[nm + "f16"], tag=f"de1{hf}_{dt}")
                p.emit("scalar", lambda e, pbuf=pbuf: e.copy(fsub[pbuf], f16_sb),
                       reads=[nm + "f16"], writes=[nm + f"fsub{pbuf}"], tag=f"de2{hf}_{dt}")
                dpending.append((dt, pbuf))
                if len(dpending) > 1:
                    d0, pb0_ = dpending.pop(0)
                    pln.add_chunk(io, hf, d0, fsub[pb0_], nm + f"fsub{pb0_}",
                                  first=(d0 == 0), last=(d0 == KC - 1))
        for d0, pb0_ in dpending:
            pln.add_chunk(io, hf, d0, fsub[pb0_], nm + f"fsub{pb0_}",
                          first=(d0 == 0), last=(d0 == KC - 1))
        pln.finalize(io, hf)


def emit_contrib(p, cm, nm, io):
    nc = p.nc
    xn = io["cxn"]
    wkv = io["cwkv"]
    wkv_res = io["cwkv_res"]
    kb = io["kvbuf"]
    kb_res = io["kvbuf_res"]
    ps = io["pb"]
    R = io["pb_res"]
    for hf in range(2):
        dst = xn[:].rearrange("p (a s) -> p a s", a=KC)
        srcx = io["x2"][:, HALF * hf:HALF * (hf + 1)].rearrange("(a p) s -> p a s", p=128)
        p.dma("gpsimd", dst, srcx, reads=io["x2_res"], writes=[io["cxn_res"]],
              tag=f"cxn{hf}", role="xs")
        for mt in range(8):
            par = mt % 2
            p.dma("gpsimd", wkv[par], io["wk_t"][mt],
                  writes=[wkv_res[par]], tag=f"cwk{hf}_{mt}", role="wq")
            for kc in range(KC):
                p.emit("tensor", lambda e, kc=kc, par=par: e.matmul(
                    ps[0][:], wkv[par][:, 128 * kc:128 * kc + 128],
                    xn[:, HALF * kc:HALF * (kc + 1)],
                    start=(kc == 0), stop=(kc == KC - 1)),
                    reads=[wkv_res[par], io["cxn_res"]], writes=[R[0]],
                    tag=f"ck{hf}_{mt}_{kc}")
            p.emit("scalar", lambda e, mt=mt, hf=hf: e.copy(
                kb[:, 1024 * (mt % 4) + 0:1024 * (mt % 4) + HALF] if False else
                kb[:, 1024 * (mt % 4) + HALF * hf:1024 * (mt % 4) + HALF * hf + HALF],
                ps[0][:]),
                reads=[R[0]], writes=[kb_res], tag=f"cke{hf}_{mt}")
            if mt % 4 == 3 or True:
                pass
            p.dma("sync", io["contrib"][128 * mt:128 * (mt + 1), HALF * hf:HALF * (hf + 1)],
                  kb[:, 1024 * (mt % 4) + HALF * hf:1024 * (mt % 4) + HALF * hf + HALF],
                  reads=[kb_res], writes=[f"ctr_k{mt}_{hf}"], tag=f"cks{hf}_{mt}",
                  role="st0")
        for mt in range(8):
            par = mt % 2
            p.dma("gpsimd", wkv[par], io["wv_t"][mt],
                  writes=[wkv_res[par]], tag=f"cwv{hf}_{mt}", role="wq")
            for tt in range(4):
                for kc in range(KC):
                    p.emit("tensor", lambda e, kc=kc, par=par, tt=tt: e.matmul(
                        ps[1][:, 0:128], xn[:, HALF * kc + 128 * tt:HALF * kc + 128 * tt + 128],
                        wkv[par][:, 128 * kc:128 * kc + 128],
                        start=(kc == 0), stop=(kc == KC - 1)),
                        reads=[wkv_res[par], io["cxn_res"]], writes=[R[1]],
                        tag=f"cv{hf}_{mt}_{tt}_{kc}")
                p.emit("scalar", lambda e, tt=tt, mt=mt: e.copy(
                    kb[:, 4096 + 1024 * tt + 128 * mt: 4096 + 1024 * tt + 128 * mt + 128],
                    ps[1][:, 0:128]),
                    reads=[R[1]], writes=[kb_res], tag=f"cve{hf}_{mt}_{tt}")
        for tt in range(4):
            gt = 128 * (4 * hf + tt)
            p.dma("sync", io["contrib"][1024 + gt:1024 + gt + 128, :],
                  kb[:, 4096 + 1024 * tt: 4096 + 1024 * tt + 1024],
                  reads=[kb_res], writes=[f"ctr_v{hf}_{tt}"], tag=f"cvs{hf}_{tt}",
                  role="st1")


def emit_moe_layer(p, cm, nm, io):
    nc = p.nc
    xn = io["moe_xn"]
    xn_res = io["moe_xn_res"]
    scr = io["moe_scr"]
    scr_res = io["moe_scr_res"]
    pb = io["pb"]
    R = io["pb_res"]
    pln = io.get("_pln") or PostLN(p, cm, nm, pb[3], pb[7], pb[5], pb[4],
                                   R[3], R[7], R[5], R[4])

    if io.get("wr_view") is not None:
        wr_sb = io["wr_view"]
    else:
        wr_sb = nc.alloc_sbuf_tensor(nm + "_wr", [128, KC * 4], BF16)[:]
    amat = cm.iota0[:].bitcast(BF16)[0:128, 0:4 * KC * 16]
    wq0f = io["ar_wq0"][:].bitcast(F32)
    wq1b = io["ar_wq1"][:]
    wo0f = io["ar_wo0"][:].bitcast(F32)
    wo1f = io["ar_wo1"][:].bitcast(F32)
    RQ0, RQ1, RO0, RO1 = io["ar_res"]
    identf = wq0f[0:128, 64:192]
    mid_sb = wq0f[0:16, 192:704]
    dsub = [wq0f[0:128, 704:1216], wq0f[0:128, 1216:1728]]
    ms_sb = wq1b[0:16, 0:4 * HALF]
    ls_sb = wo0f[0:4, 0:TO]
    g_sb = wo0f[0:4, TO:2 * TO]
    lt_sb = wo1f[0:128, 0:32]
    rowm = wo1f[0:128, 32:40]
    tmp4 = wo1f[0:128, 40:56]
    ge_sb = wo1f[0:1, 56:56 + HALF]

    p.emit("gpsimd", lambda e: e.memset(identf, 1.0), writes=[RQ0], tag="midf0")
    p.emit("gpsimd", lambda e: e.affine_select(
        identf, identf, [[-1, 128]], ALU.is_equal, 0.0, base=0, channel_multiplier=1),
        reads=[RQ0], writes=[RQ0], tag="midf")
    p.dma("gpsimd", wr_sb.rearrange("p (a m) -> p a m", a=KC),
          io["wr_t"].rearrange("(a p) m -> p a m", p=128), writes=[nm + "wr"],
          tag="ldwr", role="wq")
    p.dma("gpsimd", amat, io["a_t"], writes=["iota0"], tag="lda", role="wq")
    bmat = scr[0:16, 0:4 * D]
    p.dma("gpsimd", bmat.rearrange("p (e m) -> p e m", e=4),
          io["b_t"].rearrange("e r m -> r e m"),
          reads=[scr_res], writes=[scr_res], tag="ldb", role="wq")

    if io.get("dbg_ls") is not None:
        pass
    for hf in range(2):
        dst = xn[:].rearrange("p (a s) -> p a s", a=KC)
        srcx = io["x_in"][:, HALF * hf:HALF * (hf + 1)].rearrange("(a p) s -> p a s", p=128)
        p.dma("gpsimd", dst, srcx, reads=io.get("x_in_res", []), writes=[xn_res],
              tag=f"mxn{hf}", role="xs")
        for kc in range(KC):
            p.emit("tensor", lambda e, kc=kc: e.matmul(
                pb[0][0:4, :], wr_sb[:, 4 * kc:4 * kc + 4],
                xn[:, HALF * kc:HALF * (kc + 1)],
                start=(kc == 0), stop=(kc == KC - 1)),
                reads=[nm + "wr", xn_res], writes=[R[0]], tag=f"lg{hf}_{kc}")
        p.emit("scalar", lambda e, hf=hf: e.copy(ls_sb[0:4, HALF * hf:HALF * (hf + 1)],
                                                 pb[0][0:4, :]),
               reads=[R[0]], writes=[RO0], tag=f"lge{hf}")
        for tt in range(4):
            t0 = HALF * hf + 128 * tt
            p.emit("tensor", lambda e, t0=t0: e.transpose(
                pb[1][:].bitcast(F32)[0:128, 0:4], ls_sb[0:4, t0:t0 + 128], identf[0:4, 0:4]),
                reads=[RO0, RQ0], writes=[R[1]], tag=f"gt{hf}_{tt}")
            p.emit("scalar", lambda e, tt=tt: e.copy(lt_sb[:, 4 * tt:4 * tt + 4],
                                                     pb[1][0:128, 0:4]),
                   reads=[R[1]], writes=[RO1], tag=f"gte{hf}_{tt}")
            L4 = lambda tt=tt: lt_sb[:, 4 * tt:4 * tt + 4]
            p.emit("vector", lambda e, tt=tt: e.reduce_max(
                rowm[:, 0:1], lt_sb[:, 4 * tt:4 * tt + 4], AX.X, negate=True),
                reads=[RO1], writes=[RO1], tag=f"m1{hf}_{tt}")
            p.emit("vector", lambda e, tt=tt: e.reduce_max(
                rowm[:, 4:5], lt_sb[:, 4 * tt:4 * tt + 4], AX.X),
                reads=[RO1], writes=[RO1], tag=f"m1p{hf}_{tt}")
            p.emit("vector", lambda e, tt=tt: e.tensor_scalar(
                tmp4[:, 0:4], lt_sb[:, 4 * tt:4 * tt + 4], 0.0, rowm[:, 4:5],
                ALU.add, ALU.is_ge),
                reads=[RO1, RO1], writes=[RO1], tag=f"im{hf}_{tt}")
            p.emit("vector", lambda e, tt=tt: e.scalar_tensor_tensor(
                tmp4[:, 4:8], tmp4[:, 0:4], -1e30, lt_sb[:, 4 * tt:4 * tt + 4],
                ALU.mult, ALU.add),
                reads=[RO1, RO1], writes=[RO1], tag=f"msk{hf}_{tt}")
            p.emit("vector", lambda e: e.reduce_max(
                rowm[:, 1:2], tmp4[:, 4:8], AX.X),
                reads=[RO1], writes=[RO1], tag=f"m2{hf}_{tt}")
            p.emit("vector", lambda e, tt=tt: e.tensor_scalar(
                tmp4[:, 8:12], lt_sb[:, 4 * tt:4 * tt + 4], 0.0, rowm[:, 1:2],
                ALU.add, ALU.is_ge),
                reads=[RO1, RO1], writes=[RO1], tag=f"kp{hf}_{tt}")
            p.emit("scalar", lambda e, tt=tt: e.activation(
                tmp4[:, 12:16], lt_sb[:, 4 * tt:4 * tt + 4], AF.Exp, bias=rowm[:, 0:1]),
                reads=[RO1, RO1], writes=[RO1], tag=f"ex{hf}_{tt}")
            p.emit("vector", lambda e: e.tensor_mul(
                tmp4[:, 12:16], tmp4[:, 12:16], tmp4[:, 8:12]),
                reads=[RO1], writes=[RO1], tag=f"ek{hf}_{tt}")
            p.emit("vector", lambda e: e.reduce_sum(
                rowm[:, 2:3], tmp4[:, 12:16], AX.X),
                reads=[RO1], writes=[RO1], tag=f"sm{hf}_{tt}")
            p.emit("vector", lambda e: e.reciprocal(rowm[:, 3:4], rowm[:, 2:3]),
                   reads=[RO1], writes=[RO1], tag=f"rc{hf}_{tt}")
            p.emit("vector", lambda e: e.tensor_scalar(
                tmp4[:, 12:16], tmp4[:, 12:16], rowm[:, 3:4], 4.0,
                ALU.mult, ALU.mult),
                reads=[RO1, RO1], writes=[RO1], tag=f"gs{hf}_{tt}")
            p.emit("tensor", lambda e: e.transpose(
                pb[1][:].bitcast(F32)[0:4, 0:128], tmp4[:, 12:16], identf),
                reads=[RO1, RQ0], writes=[R[1]], tag=f"gtb{hf}_{tt}")
            p.emit("scalar", lambda e, t0=t0: e.copy(g_sb[0:4, t0:t0 + 128],
                                                     pb[1][0:4, 0:128]),
                   reads=[R[1]], writes=[RO0], tag=f"gbe{hf}_{tt}")
    if io.get("dbg_ls") is not None:
        p.dma("sync", io["dbg_ls"], ls_sb[0:4, :], reads=[RO0], writes=["dbgls"],
              tag="dbgls", role="misc")
        p.dma("sync", io["dbg_g"], g_sb[0:4, :], reads=[RO0], writes=["dbgg"],
              tag="dbgg", role="misc")
    for hf in range(2):
        dst = xn[:].rearrange("p (a s) -> p a s", a=KC)
        srcx = io["x_in"][:, HALF * hf:HALF * (hf + 1)].rearrange("(a p) s -> p a s", p=128)
        p.dma("gpsimd", dst, srcx, reads=io.get("x_in_res", []), writes=[xn_res],
              tag=f"mxn2_{hf}", role="xs")
        for e_ in range(4):
            for kc in range(KC):
                p.emit("tensor", lambda e, e_=e_, kc=kc: e.matmul(
                    pb[0][0:16, :], amat[:, 16 * (KC * e_ + kc):16 * (KC * e_ + kc) + 16],
                    xn[:, HALF * kc:HALF * (kc + 1)],
                    start=(kc == 0), stop=(kc == KC - 1)),
                    reads=["iota0", xn_res], writes=[R[0]], tag=f"md{hf}_{e_}_{kc}")
            p.emit("scalar", lambda e: e.copy(mid_sb, pb[0][0:16, :]),
                   reads=[R[0]], writes=[RQ0], tag=f"mde{hf}_{e_}")
            p.dma("sync", ge_sb, g_sb[e_:e_ + 1, HALF * hf:HALF * (hf + 1)],
                  reads=[RO0], writes=[RO1], tag=f"gex{hf}_{e_}", role="misc")
            p.emit("tensor", lambda e, e_=e_, hf=hf: e.matmul(
                pb[1][0:16, :], cm.ones_row_f32[0:1, 0:16],
                ge_sb, start=True, stop=True),
                reads=["ones_row_f32", RO1], writes=[R[1]], tag=f"gb{hf}_{e_}")
            p.emit("vector", lambda e, e_=e_: e.tensor_mul(
                ms_sb[0:16, HALF * e_:HALF * (e_ + 1)], mid_sb, pb[1][0:16, :]),
                reads=[RQ0, R[1]], writes=[RQ1], tag=f"msx{hf}_{e_}")
        mpending = []
        for dt in range(KC):
            dbank = pb[2] if dt % 2 == 0 else pb[0]
            dR = R[2] if dt % 2 == 0 else R[0]
            for e_ in range(4):
                p.emit("tensor", lambda e, e_=e_, dt=dt, dbank=dbank: e.matmul(
                    dbank[:], bmat[0:16, D * e_ + 128 * dt: D * e_ + 128 * dt + 128],
                    ms_sb[0:16, HALF * e_:HALF * (e_ + 1)],
                    start=(e_ == 0), stop=(e_ == 3)),
                    reads=[scr_res, RQ1], writes=[dR], tag=f"dl{hf}_{dt}_{e_}")
            pbuf = dt % 2
            p.emit("scalar", lambda e, pbuf=pbuf, dbank=dbank: e.copy(dsub[pbuf], dbank[:]),
                   reads=[dR], writes=[RQ0], tag=f"dle{hf}_{dt}")
            mpending.append((dt, pbuf))
            if len(mpending) > 1:
                d0, pb0_ = mpending.pop(0)
                pln.add_chunk(io, hf, d0, dsub[pb0_], RQ0,
                              first=(d0 == 0), last=(d0 == KC - 1))
        for d0, pb0_ in mpending:
            pln.add_chunk(io, hf, d0, dsub[pb0_], RQ0,
                          first=(d0 == 0), last=(d0 == KC - 1))
        pln.finalize(io, hf)


def build_launch3(debug_x3=False):
    nc = bass.Bass("TRN2", target_bir_lowering=False, detect_race_conditions=False)
    x2 = nc.dram_tensor("x2", [D, TO], F32, kind="ExternalInput").ap()
    gathered = nc.dram_tensor("gathered", [2 * TO * NCORE, TO], BF16, kind="ExternalInput").ap()
    wq = nc.dram_tensor("wq", [NH, D, 128], F32, kind="ExternalInput").ap()
    wo = nc.dram_tensor("wo", [KC, D, 128], F32, kind="ExternalInput").ap()
    wr_t = nc.dram_tensor("wr_t", [D, 4], F32, kind="ExternalInput").ap()
    a_t = nc.dram_tensor("a_t", [128, 4 * KC * 16], F32, kind="ExternalInput").ap()
    b_t = nc.dram_tensor("b_t", [4, 16, D], F32, kind="ExternalInput").ap()
    c_in = nc.dram_tensor("c_in", [128, 1], F32, kind="ExternalInput").ap()
    delta_in = nc.dram_tensor("delta_in", [128, 8], F32, kind="ExternalInput").ap()
    out_d = nc.dram_tensor("out_d", [D, TO], F32, kind="ExternalOutput").ap()
    x3_dram = nc.dram_tensor("x3_dram", [D, TO], F32,
                             kind="ExternalOutput" if debug_x3 else "Internal").ap()
    v_dram = nc.dram_tensor("v_dram", [D, TO], F32, kind="Internal").ap()

    p = Prog(nc)
    cm = Common(p, want_l1_masks=True)
    cm.emit_setup(c_in, delta_in)
    io = dict(x_in=x2, res_in=x2, gathered=gathered, gathered_res="__in_g",
              wq=[wq[h] for h in range(NH)], wo=[wo[i] for i in range(KC)],
              v_dram=v_dram, x_out=x3_dram, x_out_res="x3d")
    L = dict(kv_mode="l1", nm="at")
    emit_attention_layer(p, cm, L, io)
    at_xn = None
    dbg_ls = dbg_g = None
    if debug_x3:
        dbg_ls = nc.dram_tensor("dbg_ls", [4, TO], F32, kind="ExternalOutput").ap()
        dbg_g = nc.dram_tensor("dbg_g", [4, TO], F32, kind="ExternalOutput").ap()
    x3res = [f"x3d_{hf}_{dt}" for hf in range(2) for dt in range(KC)]
    moe_io = dict(x_in=x3_dram, x_in_res=x3res, res_in=x3_dram, res_in_res=x3res,
                  dbg_ls=dbg_ls, dbg_g=dbg_g,
                  v_dram=v_dram, x_out=out_d, x_out_res="out",
                  wr_t=wr_t, a_t=a_t, b_t=b_t,
                  moe_xn=io["_xn_t"], moe_xn_res="atxn",
                  moe_scr=io["_attn_t"], moe_scr_res="atattn",
                  ar_wq0=io["_wq"][0], ar_wq1=io["_wq"][1],
                  ar_wo0=io["_wo"][0], ar_wo1=io["_wo"][1],
                  ar_res=["atwq0", "atwq1", "atwo0", "atwo1"],
                  pb=io["_pb"], pb_res=io["_pb_res"], _pln=io["_pln"])
    emit_moe_layer(p, cm, "moe", moe_io)
    p.barrier("sync", [f"out_{hf}_{dt}" for hf in range(2) for dt in range(KC)])
    p.build()
    nc._prog = p
    return nc


def host_prep_l1(inputs):
    wq_l = np.asarray(inputs["L1_attn_wq"], dtype=np.float32)[PI]
    wq_t = np.ascontiguousarray(
        wq_l.reshape(KC, 128, NH, 128).transpose(2, 1, 0, 3)
        .reshape(NH, 128, KC * 128)).astype(BF16_NP)
    wo_l = np.asarray(inputs["L1_attn_wo"], dtype=np.float32)[:, PI]
    wo_t = np.ascontiguousarray(
        wo_l.reshape(NH, 128, KC, 128).transpose(2, 1, 0, 3)
        .reshape(KC, 128, NH * 128)).astype(BF16_NP)
    wk_l = np.asarray(inputs["L1_attn_wk"], dtype=np.float32)[PI]
    wk_t = np.ascontiguousarray(
        wk_l.reshape(KC, 128, 8, 128).transpose(2, 1, 0, 3)
        .reshape(8, 128, KC * 128)).astype(BF16_NP)
    wv_l = np.asarray(inputs["L1_attn_wv"], dtype=np.float32)[PI]
    wv_t = np.ascontiguousarray(
        wv_l.reshape(KC, 128, 8, 128).transpose(2, 1, 0, 3)
        .reshape(8, 128, KC * 128)).astype(BF16_NP)
    wr_t = np.ascontiguousarray(np.asarray(inputs["L1_ffn_Wr"], np.float32).T[PI])
    a_l = np.asarray(inputs["L1_ffn_A"], np.float32).transpose(0, 2, 1)[:, PI, :]
    a_t = np.ascontiguousarray(a_l.reshape(4, KC, 128, 16).transpose(2, 0, 1, 3).reshape(128, 4 * KC * 16))
    b_t = np.ascontiguousarray(np.asarray(inputs["L1_ffn_B"], np.float32)[:, PI, :].transpose(0, 2, 1))
    per_core = []
    for c in range(NCORE):
        delta = np.zeros((128, 8), np.float32)
        for r in range(8):
            delta[:, r] = 0.0 if r <= c else -1.0
        per_core.append(dict(c_in=np.full((128, 1), float(c), dtype=np.float32),
                             delta_in=delta))
    shared = dict(wq=wq_t, wo=wo_t, wr_t=wr_t, a_t=a_t, b_t=b_t)
    return shared, per_core, wk_t, wv_t

def build_launch1():
    nc = bass.Bass("TRN2", target_bir_lowering=False, detect_race_conditions=False)
    xT = nc.dram_tensor("xT", [D, T], F32, kind="ExternalInput").ap()
    xTown = nc.dram_tensor("xTown", [D, TO], F32, kind="ExternalInput").ap()
    wq = nc.dram_tensor("wq", [NH, D, 128], F32, kind="ExternalInput").ap()
    wo = nc.dram_tensor("wo", [KC, D, 128], F32, kind="ExternalInput").ap()
    wk = nc.dram_tensor("wk", [D, 128], F32, kind="ExternalInput").ap()
    wv = nc.dram_tensor("wv", [D, 128], F32, kind="ExternalInput").ap()
    qb = nc.dram_tensor("qb", [128, NH], F32, kind="ExternalInput").ap()
    kb = nc.dram_tensor("kb", [128, 1], F32, kind="ExternalInput").ap()
    vbias = nc.dram_tensor("vbias", [1, 128], F32, kind="ExternalInput").ap()
    ob = nc.dram_tensor("ob", [128, KC], F32, kind="ExternalInput").ap()
    c_in = nc.dram_tensor("c_in", [128, 1], F32, kind="ExternalInput").ap()
    x1_out = nc.dram_tensor("x1_out", [D, TO], F32, kind="ExternalOutput").ap()
    v_dram = nc.dram_tensor("v_dram", [D, TO], F32, kind="Internal").ap()

    p = Prog(nc)
    cm = Common(p, scratch_view=None)
    cm.emit_setup(c_in)
    io = dict(xT=xT, x_in=xTown, res_in=xTown, wq=[wq[h] for h in range(NH)],
              wo=[wo[i] for i in range(KC)], wk=wk, wv=wv, qb=qb, kb=kb,
              vbias=vbias, ob=ob, v_dram=v_dram, x_out=x1_out, x_out_res="x1_out")
    emit_attention_layer(p, cm, dict(kv_mode="l0", nm="l0a"), io)
    p.barrier("sync", [f"x1_out_{hf}_{dt}" for hf in range(2) for dt in range(KC)])
    p.build()
    return nc


def host_prep_l0attn(inputs):
    x = np.asarray(inputs["L0_attn_x"], dtype=np.float32).reshape(T, D)
    xT = np.ascontiguousarray(x.T[PI])
    NS = T // HALF
    xTs = np.ascontiguousarray(
        xT.reshape(KC, 128, NS, HALF).transpose(1, 2, 0, 3)
        .reshape(128, NS * KC * HALF)).astype(BF16_NP)
    wq_l = np.asarray(inputs["L0_attn_wq_w"], dtype=np.float32).T[PI]
    wq_t = np.ascontiguousarray(
        wq_l.reshape(KC, 128, NH, 128).transpose(2, 1, 0, 3)
        .reshape(NH, 128, KC * 128)).astype(BF16_NP)
    wo_l = np.asarray(inputs["L0_attn_wo_w"], dtype=np.float32).T[:, PI]
    wo_t = np.ascontiguousarray(
        wo_l.reshape(NH, 128, KC, 128).transpose(2, 1, 0, 3)
        .reshape(KC, 128, NH * 128)).astype(BF16_NP)
    wk_t = np.ascontiguousarray(
        np.asarray(inputs["L0_attn_wk_w"], dtype=np.float32).T[PI]
        .reshape(KC, 128, 128).transpose(1, 0, 2).reshape(128, KC * 128)).astype(BF16_NP)
    wv_t = np.ascontiguousarray(
        np.asarray(inputs["L0_attn_wv_w"], dtype=np.float32).T[PI]
        .reshape(KC, 128, 128).transpose(1, 0, 2).reshape(128, KC * 128)).astype(BF16_NP)
    qb = np.ascontiguousarray(np.asarray(inputs["L0_attn_wq_b"], np.float32).reshape(NH, 128).T)
    kb = np.asarray(inputs["L0_attn_wk_b"], np.float32).reshape(128, 1).copy()
    vbias = np.asarray(inputs["L0_attn_wv_b"], np.float32).reshape(1, 128).copy()
    ob = np.ascontiguousarray(np.asarray(inputs["L0_attn_wo_b"], np.float32)[PI].reshape(KC, 128).T)
    shared = dict(xT=xTs, wq=wq_t, wo=wo_t, wk=wk_t, wv=wv_t, qb=qb, kb=kb,
                  vbias=vbias, ob=ob)
    per_core = []
    for c in range(NCORE):
        xTown = np.ascontiguousarray(xT[:, c::8])
        xno = np.ascontiguousarray(
            xTown.reshape(KC, 128, 2, HALF).transpose(1, 2, 0, 3)
            .reshape(128, 2 * KC * HALF)).astype(BF16_NP)
        per_core.append(dict(xTown=xTown, xno=xno,
                             c_in=np.full((128, 1), float(c), dtype=np.float32)))
    return shared, per_core



def build_launch2():
    nc = bass.Bass("TRN2", target_bir_lowering=False, detect_race_conditions=False)
    x1 = nc.dram_tensor("x1", [D, TO], F32, kind="ExternalInput").ap()
    qw_g = nc.dram_tensor("qw_g", [512, INTER_PAD], I32, kind="ExternalInput").ap()
    qw_u = nc.dram_tensor("qw_u", [512, INTER_PAD], I32, kind="ExternalInput").ap()
    qw_d = nc.dram_tensor("qw_d", [1376, D], I32, kind="ExternalInput").ap()
    sbc_g = nc.dram_tensor("sbc_g", [4, 128, INTER_PAD], F16, kind="ExternalInput").ap()
    sbc_u = nc.dram_tensor("sbc_u", [4, 128, INTER_PAD], F16, kind="ExternalInput").ap()
    sbc_d = nc.dram_tensor("sbc_d", [11, 128, D], F16, kind="ExternalInput").ap()
    gate_zeros = nc.dram_tensor("gate_zeros", [32, 1376], I32, kind="ExternalInput").ap()
    up_zeros = nc.dram_tensor("up_zeros", [32, 1376], I32, kind="ExternalInput").ap()
    down_zeros = nc.dram_tensor("down_zeros", [86, 512], I32, kind="ExternalInput").ap()
    e8 = nc.dram_tensor("e8", [8, 128], F32, kind="ExternalInput").ap()
    wk_t = nc.dram_tensor("wk_t", [8, D, 128], F32, kind="ExternalInput").ap()
    wv_t = nc.dram_tensor("wv_t", [8, D, 128], F32, kind="ExternalInput").ap()
    x2_out = nc.dram_tensor("x2_out", [D, TO], F32, kind="ExternalOutput").ap()
    contrib_out = nc.dram_tensor("contrib_out", [2 * TO, TO], BF16, kind="ExternalOutput").ap()
    v_dram = nc.dram_tensor("v_dram", [D, TO], F32, kind="Internal").ap()
    z1g = nc.dram_tensor("z1g", [32, INTER_PAD], BF16, kind="Internal").ap()
    z1u = nc.dram_tensor("z1u", [32, INTER_PAD], BF16, kind="Internal").ap()
    z1d = nc.dram_tensor("z1d", [86, D], BF16, kind="Internal").ap()
    zbc_g = nc.dram_tensor("zbc_g", [4, 128, INTER_PAD], BF16, kind="Internal").ap()
    zbc_u = nc.dram_tensor("zbc_u", [4, 128, INTER_PAD], BF16, kind="Internal").ap()
    zbc_d = nc.dram_tensor("zbc_d", [11, 128, D], BF16, kind="Internal").ap()

    p = Prog(nc)
    pb = [nc.alloc_psum_tensor(f"ffn_pb{i}", [128, 512], F32) for i in range(8)]
    inter_arena = nc.alloc_sbuf_tensor("inter_arena", [128, NT_I * HALF], BF16)
    cm = Common(p, want_masks=False, want_ffn=True, scratch_view=inter_arena[:, 0:20992])
    cm.emit_setup_consts()
    io = dict(x_in=x1, res_in=x1, x_out=x2_out, x_out_res="x2_out", v_dram=v_dram,
              inter_t=inter_arena, pb=pb, pb_res=[f"ffn_pb{i}" for i in range(8)],
              qw_g=qw_g, qw_u=qw_u, qw_d=qw_d,
              sbc_g=[sbc_g[i] for i in range(4)], sbc_u=[sbc_u[i] for i in range(4)],
              sbc_d=[sbc_d[i] for i in range(11)],
              gate_zeros=gate_zeros, up_zeros=up_zeros, down_zeros=down_zeros,
              e8=e8, z1g=z1g, z1u=z1u, z1d=z1d,
              zbc_g=[zbc_g[i] for i in range(4)], zbc_u=[zbc_u[i] for i in range(4)],
              zbc_d=[zbc_d[i] for i in range(11)])
    emit_ffn_setup(p, cm, io)
    emit_ffn_layer(p, cm, dict(nm="l0f"), io)
    cio = dict(x2=x2_out, x2_res=[f"x2_out_{hf}_{dt}" for hf in range(2) for dt in range(KC)],
               wk_t=[wk_t[i] for i in range(8)], wv_t=[wv_t[i] for i in range(8)],
               contrib=contrib_out, kvbuf=inter_arena[:, 0:8192], kvbuf_res="l0finter",
               cxn=io["_ffn_xn"], cxn_res=io["_ffn_xn_res"],
               cwkv=[inter_arena[:, 8192 + 4096 * i:8192 + 4096 * (i + 1)] for i in range(2)],
               cwkv_res=["l0finter", "l0finter"],
               pb=pb, pb_res=[f"ffn_pb{i}" for i in range(8)])
    emit_contrib(p, cm, "ctr", cio)
    p.barrier("sync", [f"x2_out_{hf}_{dt}" for hf in range(2) for dt in range(KC)]
              + [f"ctr_k{mt}_{hf}" for mt in range(8) for hf in range(2)]
              + [f"ctr_v{hf}_{tt}" for hf in range(2) for tt in range(4)])
    p.build()
    return nc


def host_prep_l0ffn(inputs):
    gq = np.asarray(inputs["L0_ffn_gate_qweight"])
    uq = np.asarray(inputs["L0_ffn_up_qweight"])
    dq = np.asarray(inputs["L0_ffn_down_qweight"])
    gs = np.asarray(inputs["L0_ffn_gate_scales"])
    us = np.asarray(inputs["L0_ffn_up_scales"])
    dsc = np.asarray(inputs["L0_ffn_down_scales"])
    gz = np.asarray(inputs["L0_ffn_gate_zeros"])
    uz = np.asarray(inputs["L0_ffn_up_zeros"])
    dz = np.asarray(inputs["L0_ffn_down_zeros"])
    qw_g = np.ascontiguousarray(gq[:, PIP])
    qw_u = np.ascontiguousarray(uq[:, PIP])
    qw_d = np.ascontiguousarray(dq[:, PI])
    def sbc_table(s, RC, W, perm):
        sp = s[:, perm]
        out = np.empty((RC, 128, W), dtype=s.dtype)
        for R in range(RC):
            g0 = 8 * R
            gr = min(8, sp.shape[0] - g0)
            blk = np.repeat(sp[g0:g0 + gr], 16, axis=0)
            out[R, :16 * gr] = blk
            if 16 * gr < 128:
                out[R, 16 * gr:] = 0
        return np.ascontiguousarray(out)
    sbc_g = sbc_table(gs, 4, INTER_PAD, PIP)
    sbc_u = sbc_table(us, 4, INTER_PAD, PIP)
    sbc_d = sbc_table(dsc, 11, D, PI)
    e8 = np.zeros((8, 128), dtype=np.float32)
    for g in range(8):
        e8[g, 16 * g:16 * g + 16] = 1.0
    shared = dict(qw_g=qw_g, qw_u=qw_u, qw_d=qw_d, sbc_g=sbc_g, sbc_u=sbc_u,
                  sbc_d=sbc_d, gate_zeros=gz, up_zeros=uz, down_zeros=dz, e8=e8)
    return shared




def build_fused(stop_after=None, no_ea_qbf=False, dbg=False):
    nc = bass.Bass("TRN2", target_bir_lowering=False, detect_race_conditions=False)
    xT = nc.dram_tensor("xT", [128, (T // HALF) * KC * HALF], BF16, kind="ExternalInput").ap()
    xTown = nc.dram_tensor("xTown", [D, TO], F32, kind="ExternalInput").ap()
    xno = nc.dram_tensor("xno", [128, 2 * KC * HALF], BF16, kind="ExternalInput").ap()
    wq0 = nc.dram_tensor("wq0", [NH, 128, KC * 128], BF16, kind="ExternalInput").ap()
    wo0 = nc.dram_tensor("wo0", [KC, 128, NH * 128], BF16, kind="ExternalInput").ap()
    wk0 = nc.dram_tensor("wk0", [128, KC * 128], BF16, kind="ExternalInput").ap()
    wv0 = nc.dram_tensor("wv0", [128, KC * 128], BF16, kind="ExternalInput").ap()
    qb = nc.dram_tensor("qb", [128, NH], F32, kind="ExternalInput").ap()
    kb = nc.dram_tensor("kb", [128, 1], F32, kind="ExternalInput").ap()
    vbias = nc.dram_tensor("vbias", [1, 128], F32, kind="ExternalInput").ap()
    ob = nc.dram_tensor("ob", [128, KC], F32, kind="ExternalInput").ap()
    c_in = nc.dram_tensor("c_in", [128, 1], F32, kind="ExternalInput").ap()
    delta_in = nc.dram_tensor("delta_in", [128, 8], F32, kind="ExternalInput").ap()
    qw_g = nc.dram_tensor("qw_g", [512, INTER_PAD], I32, kind="ExternalInput").ap()
    qw_u = nc.dram_tensor("qw_u", [512, INTER_PAD], I32, kind="ExternalInput").ap()
    qw_d = nc.dram_tensor("qw_d", [1376, D], I32, kind="ExternalInput").ap()
    sbc_g = nc.dram_tensor("sbc_g", [4, 128, INTER_PAD], F16, kind="ExternalInput").ap()
    sbc_u = nc.dram_tensor("sbc_u", [4, 128, INTER_PAD], F16, kind="ExternalInput").ap()
    sbc_d = nc.dram_tensor("sbc_d", [11, 128, D], F16, kind="ExternalInput").ap()
    gate_zeros = nc.dram_tensor("gate_zeros", [32, 1376], I32, kind="ExternalInput").ap()
    up_zeros = nc.dram_tensor("up_zeros", [32, 1376], I32, kind="ExternalInput").ap()
    down_zeros = nc.dram_tensor("down_zeros", [86, 512], I32, kind="ExternalInput").ap()
    e8 = nc.dram_tensor("e8", [8, 128], F32, kind="ExternalInput").ap()
    wk_t = nc.dram_tensor("wk_t", [8, 128, KC * 128], BF16, kind="ExternalInput").ap()
    wv_t = nc.dram_tensor("wv_t", [8, 128, KC * 128], BF16, kind="ExternalInput").ap()
    wq1 = nc.dram_tensor("wq1", [NH, 128, KC * 128], BF16, kind="ExternalInput").ap()
    wo1 = nc.dram_tensor("wo1", [KC, 128, NH * 128], BF16, kind="ExternalInput").ap()
    wr_t = nc.dram_tensor("wr_t", [D, 4], F32, kind="ExternalInput").ap()
    a_t = nc.dram_tensor("a_t", [128, 4 * KC * 16], F32, kind="ExternalInput").ap()
    b_t = nc.dram_tensor("b_t", [4, 16, D], F32, kind="ExternalInput").ap()
    out_d = nc.dram_tensor("out_d", [D, TO], F32, kind="ExternalOutput").ap()
    KI = "ExternalOutput" if dbg else "Internal"
    dbg_inter = nc.dram_tensor("dbg_inter", [128, 4096], BF16, kind=KI).ap()
    dbg_ps = nc.dram_tensor("dbg_ps", [128, 512], F32, kind=KI).ap()
    dbg_ps2 = nc.dram_tensor("dbg_ps2", [128, 512], F32, kind=KI).ap()
    dbg_wg = nc.dram_tensor("dbg_wg", [128, 512], BF16, kind=KI).ap()
    dbg_sbc = nc.dram_tensor("dbg_sbc", [128, 512], BF16, kind=KI).ap()
    dbg_zbc = nc.dram_tensor("dbg_zbc", [128, 512], BF16, kind=KI).ap()
    dbg_xn = nc.dram_tensor("dbg_xn", [128, 4096], BF16, kind=KI).ap()
    x1_dram = nc.dram_tensor("x1_dram", [D, TO], F32, kind=KI).ap()
    x2_dram = nc.dram_tensor("x2_dram", [D, TO], F32, kind=KI).ap()
    x3_dram = nc.dram_tensor("x3_dram", [D, TO], F32, kind="Internal").ap()
    v_dram = nc.dram_tensor("v_dram", [D, TO], F32, kind="Internal").ap()
    z1g = nc.dram_tensor("z1g", [32, INTER_PAD], BF16, kind="Internal").ap()
    z1u = nc.dram_tensor("z1u", [32, INTER_PAD], BF16, kind="Internal").ap()
    z1d = nc.dram_tensor("z1d", [86, D], BF16, kind="Internal").ap()
    zbc_g = nc.dram_tensor("zbc_g", [4, 128, INTER_PAD], BF16, kind="Internal").ap()
    zbc_u = nc.dram_tensor("zbc_u", [4, 128, INTER_PAD], BF16, kind="Internal").ap()
    zbc_d = nc.dram_tensor("zbc_d", [11, 128, D], BF16, kind="Internal").ap()
    contrib_d = nc.dram_tensor("contrib_d", [2 * TO, TO], BF16, kind="Internal").ap()
    gathered_d = nc.dram_tensor("gathered_d", [2 * TO * NCORE, TO], BF16, kind="Internal").ap()

    p = Prog(nc)
    pb = [nc.alloc_psum_tensor(f"pb{i}", [128, 512], F32) for i in range(8)]
    pb_res = [f"pb{i}" for i in range(8)]
    AR1 = nc.alloc_sbuf_tensor("AR1", [128, 16384], BF16)
    AR2 = nc.alloc_sbuf_tensor("AR2", [128, 45056], BF16)
    WO1 = nc.alloc_sbuf_tensor("WO1", [128, 4096], BF16)
    EA = nc.alloc_sbuf_tensor("EA", [128, 18432], BF16)
    arena = dict(
        xn=AR1[:],
        attn=AR2[:, 0:16384],
        k=AR2[:, 16384:24576],
        v=AR2[:, 24576:32768],
        wq=[AR2[:, 32768:36864], AR2[:, 36864:40960]],
        wo=[AR2[:, 40960:45056], WO1[:]],
    )
    at_res = ["atxn", "atattn", "atk", "atv", "atwq0", "atwq1", "atwo0", "atwo1",
              "atwkv"] + [f"atk{r}" for r in range(8)] + [f"atv{r}" for r in range(8)]
    ar2_res = [r for r in at_res if r not in ("atxn", "atwo1", "atwkv")]

    cm = Common(p, want_l1_masks=True, want_masks=True, want_ffn=True,
                scratch_view=AR2[:, 0:20992])
    cm.emit_setup(c_in, delta_in)

    ffn_io = dict(x_in=x1_dram, res_in=x1_dram,
                  x_in_res=[f"x1_{hf}_{dt}" for hf in range(2) for dt in range(KC)],
                  res_in_res=[f"x1_{hf}_{dt}" for hf in range(2) for dt in range(KC)],
                  x_out=x2_dram, x_out_res="x2", v_dram=v_dram,
                  qw_g=qw_g, qw_u=qw_u, qw_d=qw_d,
                  sbc_g=[sbc_g[i] for i in range(4)], sbc_u=[sbc_u[i] for i in range(4)],
                  sbc_d=[sbc_d[i] for i in range(11)],
                  gate_zeros=gate_zeros, up_zeros=up_zeros, down_zeros=down_zeros,
                  e8=e8, z1g=z1g, z1u=z1u, z1d=z1d,
                  zbc_g=[zbc_g[i] for i in range(4)], zbc_u=[zbc_u[i] for i in range(4)],
                  zbc_d=[zbc_d[i] for i in range(11)],
                  pb=pb, pb_res=pb_res,
                  xn_t=AR1[:], xn_res="atxn",
                  inter_t=AR2[:],
                  inter_claims=ar2_res + ["scratch", "stage"],
                  ea=EA[:], dbg_inter=dbg_inter, dbg_xn=dbg_xn, dbg_ps=dbg_ps,
                  dbg_ps2=dbg_ps2,
                  dbg_wg=dbg_wg, dbg_sbc=dbg_sbc, dbg_zbc=dbg_zbc)
    emit_ffn_setup(p, cm, ffn_io)

    p.claim(reads=[], writes=ar2_res + ["scratch", "stage"], tag="cl_setup_at")

    io0 = dict(xT=xT, x_in=xTown, x_in_c=xno, res_in=xTown,
               wq=[wq0[h] for h in range(NH)], wo=[wo0[i] for i in range(KC)],
               wk=wk0, wv=wv0, qb=qb, kb=kb, vbias=vbias, ob=ob,
               v_dram=v_dram, x_out=x1_dram, x_out_res="x1",
               arena=arena, pb=pb, pb_res=pb_res, wkv_t=EA[:, 0:8192],
               proj_t=[EA[:].bitcast(F32)[:, 7936 + 512 * i:8448 + 512 * i] for i in range(2)],
               proj_res=["l0ffsub0", "l0ffsub1"])
    emit_attention_layer(p, cm, dict(kv_mode="l0", nm="at"), io0)
    pln = io0["_pln"]

    p.claim(reads=[], writes=["atwkv", "l0fqwg0", "l0fqwg1", "l0fqwu0", "l0fqwu1",
                              "l0fqwd0", "l0fqwd1", "l0fqig0", "l0fqig1"],
            tag="cl_wkv_ea")
    ffn_io["_pln"] = pln
    zrow = p.buf("at_rec", [1, HALF], F32)
    p.emit("gpsimd", lambda e: e.memset(zrow[:], 0.0), writes=["zrow_s", "atrec"],
           tag="zrow_s")
    for i in range(8):
        p.emit("tensor", lambda e, i=i: e.matmul(
            pb[i][:], cm.ones_row_f32[:], zrow[0:1, 0:512], start=True, stop=True),
            reads=["ones_row_f32", "zrow_s", "atrec"], writes=[pb_res[i]], tag=f"scrub{i}")
    if no_ea_qbf:
        ffn_io["no_ea_qbf"] = True
    import os
    if os.environ.get("SWAP_BANKS"):
        ffn_io["swap_banks"] = True
    if os.environ.get("NT_REV"):
        ffn_io["nt_order"] = [3, 2, 1, 0]
    emit_ffn_layer(p, cm, dict(nm="l0f"), ffn_io)
    if stop_after == "ffn":
        p.barrier("sync", [f"x2_{hf}_{dt}" for hf in range(2) for dt in range(KC)])
        p.build()
        nc._prog = p
        return nc

    x2res = [f"x2_{hf}_{dt}" for hf in range(2) for dt in range(KC)]
    cio = dict(x2=x2_dram, x2_res=x2res,
               wk_t=[wk_t[i] for i in range(8)], wv_t=[wv_t[i] for i in range(8)],
               contrib=contrib_d, kvbuf=AR2[:, 0:8192], kvbuf_res="l0finter",
               cxn=AR1[:], cxn_res="atxn",
               cwkv=[AR2[:, 8192 + 4096 * i:8192 + 4096 * (i + 1)] for i in range(2)],
               cwkv_res=["l0finter", "l0finter"],
               pb=pb, pb_res=pb_res)
    emit_contrib(p, cm, "ctr", cio)
    ctr_res = [f"ctr_k{mt}_{hf}" for mt in range(8) for hf in range(2)] + \
              [f"ctr_v{hf}_{tt}" for hf in range(2) for tt in range(4)]
    p.emit("gpsimd", lambda e: e.collective_compute(
        "AllGather", ALU.bypass, replica_groups=[list(range(NCORE))],
        ins=[contrib_d.opt()], outs=[gathered_d.opt()]),
        reads=ctr_res, writes=["gathered"], tag="allgather")

    p.claim(reads=[], writes=ar2_res + ["l0finter"], tag="cl_ffn_at")

    io1 = dict(x_in=x2_dram, x_in_res=x2res, res_in=x2_dram, res_in_res=x2res,
               gathered=gathered_d, gathered_res="gathered",
               wq=[wq1[h] for h in range(NH)], wo=[wo1[i] for i in range(KC)],
               v_dram=v_dram, x_out=x3_dram, x_out_res="x3",
               arena=arena, pb=pb, pb_res=pb_res, _pln=pln,
               proj_t=[EA[:].bitcast(F32)[:, 7936 + 512 * i:8448 + 512 * i] for i in range(2)],
               proj_res=["l0ffsub0", "l0ffsub1"])
    emit_attention_layer(p, cm, dict(kv_mode="l1", nm="at"), io1)

    x3res = [f"x3_{hf}_{dt}" for hf in range(2) for dt in range(KC)]
    moe_io = dict(x_in=x3_dram, x_in_res=x3res, res_in=x3_dram, res_in_res=x3res,
                  v_dram=v_dram, x_out=out_d, x_out_res="out",
                  wr_t=wr_t, a_t=a_t, b_t=b_t,
                  moe_xn=AR1, moe_xn_res="atxn",
                  moe_scr=AR2[:, 0:16384], moe_scr_res="atattn",
                  ar_wq0=arena["wq"][0], ar_wq1=arena["wq"][1],
                  ar_wo0=arena["wo"][0], ar_wo1=arena["wo"][1],
                  ar_res=["atwq0", "atwq1", "atwo0", "atwo1"],
                  wr_view=EA[:, 15360:15488],
                  pb=pb, pb_res=pb_res, _pln=pln)
    emit_moe_layer(p, cm, "moe", moe_io)

    p.barrier("sync", [f"out_{hf}_{dt}" for hf in range(2) for dt in range(KC)])
    p.build()
    nc._prog = p
    return nc

def _reassemble(outs):
    out = np.empty((T, D), dtype=np.float32)
    for c in range(NCORE):
        nat = np.empty((D, TO), np.float32)
        nat[PI] = outs[c]
        out[c::8, :] = nat.T
    return out.reshape(1, T, D)


KERNEL_MODE = "fused"


def kernel(**inputs):
    if KERNEL_MODE == "fused":
        return kernel_fused(**inputs)
    return kernel_3launch(**inputs)


def kernel_fused(**inputs):
    shared1, per_core1 = host_prep_l0attn(inputs)
    shared2 = host_prep_l0ffn(inputs)
    shared3, per_core3, wk_t, wv_t = host_prep_l1(inputs)
    nc = build_fused()
    in_maps = []
    for c in range(NCORE):
        m = dict(shared2, wk_t=wk_t, wv_t=wv_t)
        m.update(xT=shared1["xT"], wq0=shared1["wq"], wo0=shared1["wo"],
                 wk0=shared1["wk"], wv0=shared1["wv"], qb=shared1["qb"],
                 kb=shared1["kb"], vbias=shared1["vbias"], ob=shared1["ob"],
                 wq1=shared3["wq"], wo1=shared3["wo"], wr_t=shared3["wr_t"],
                 a_t=shared3["a_t"], b_t=shared3["b_t"])
        m.update(per_core1[c])
        m.update(per_core3[c])
        in_maps.append(m)
    res = run_bass_kernel_spmd(nc, in_maps, core_ids=list(range(NCORE)))
    return _reassemble([res.results[c]["out_d"] for c in range(NCORE)])


def kernel_3launch(**inputs):
    shared1, per_core1 = host_prep_l0attn(inputs)
    shared2 = host_prep_l0ffn(inputs)
    shared3, per_core3, wk_t, wv_t = host_prep_l1(inputs)

    nc1 = build_launch1()
    in_maps = [dict(shared1, **per_core1[c]) for c in range(NCORE)]
    res1 = run_bass_kernel_spmd(nc1, in_maps, core_ids=list(range(NCORE)))
    x1s = [res1.results[c]["x1_out"] for c in range(NCORE)]

    nc2 = build_launch2()
    in_maps = [dict(shared2, x1=x1s[c], wk_t=wk_t, wv_t=wv_t) for c in range(NCORE)]
    res2 = run_bass_kernel_spmd(nc2, in_maps, core_ids=list(range(NCORE)))
    x2s = [res2.results[c]["x2_out"] for c in range(NCORE)]
    gathered = np.concatenate([res2.results[c]["contrib_out"] for c in range(NCORE)],
                              axis=0)

    nc3 = build_launch3()
    in_maps = [dict(shared3, **per_core3[c], gathered=gathered, x2=x2s[c])
               for c in range(NCORE)]
    res3 = run_bass_kernel_spmd(nc3, in_maps, core_ids=list(range(NCORE)))
    return _reassemble([res3.results[c]["out_d"] for c in range(NCORE)])


# version 10
# speedup vs baseline: 1.7917x; 1.0060x over previous
import numpy as np
import ml_dtypes
import concourse.bass as bass
import concourse.mybir as mybir
from concourse.bass_utils import run_bass_kernel_spmd

BF16_NP = ml_dtypes.bfloat16

F32 = mybir.dt.float32
BF16 = mybir.dt.bfloat16
F16 = mybir.dt.float16
I32 = mybir.dt.int32
AF = mybir.ActivationFunctionType
ALU = mybir.AluOpType
AX = mybir.AxisListType

D = 4096
T = 8192
NCORE = 8
TO = T // NCORE
HALF = 512
NH = 32
HD = 128
KC = D // 128
INTER = 11008
INTER_PAD = 11264
NT_I = INTER_PAD // 128
SM_SCALE = float(128 ** -0.5)


def _pi_perm():
    pos = np.arange(D)
    tau, p = pos // 128, pos % 128
    R, j = tau // 8, tau % 8
    return (1024 * R + 8 * p + j).astype(np.int64)

def _pip_perm():
    pos = np.arange(INTER_PAD)
    tau, p = pos // 128, pos % 128
    Rp, j = tau // 8, tau % 8
    n = 8 * (128 * Rp + p) + j
    valid = n < INTER
    return np.where(valid, n, 0).astype(np.int64), valid

PI = _pi_perm()
PIP, PIP_VALID = _pip_perm()


class Ev:
    __slots__ = ("idx", "engine", "fn", "reads", "writes", "is_dma", "deps",
                 "sem", "incval", "tokval", "tag", "role", "alldeps")

class Prog:
    def __init__(self, nc):
        self.nc = nc
        self.events = []
        self.res = {}
        self.sems = {}
        for en in ("tensor", "vector", "scalar", "gpsimd", "sync"):
            self.sems[(en, False)] = nc.alloc_semaphore("sc_" + en)
        self.dma_sems = {}
        self.dma_last = {}

    def dma_sem(self, role):
        if role not in self.dma_sems:
            self.dma_sems[role] = self.nc.alloc_semaphore("sd_" + role)
        return self.dma_sems[role]

    def emit(self, engine, fn, reads=(), writes=(), deps=(), is_dma=False, tag="",
             role=None):
        ev = Ev()
        ev.idx = len(self.events)
        ev.engine = engine
        ev.fn = fn
        ev.reads = list(reads)
        ev.writes = list(writes)
        ev.is_dma = is_dma
        ev.deps = [d for d in deps if d is not None]
        ev.sem = None
        ev.tag = tag
        if is_dma:
            base = role or engine
            if not hasattr(self, "role_rr"):
                self.role_rr = {}
            n = self.role_rr[base] = (self.role_rr.get(base, -1) + 1) % 6
            ev.role = f"{base}.{n}"
            prev = self.dma_last.get(ev.role)
            if prev is not None:
                ev.deps.append(prev)
            self.dma_last[ev.role] = ev
        for r in ev.reads:
            st = self.res.get(r)
            if st is None:
                continue
            if st["writer"] is not None:
                ev.deps.append(st["writer"])
            st["readers"].append(ev)
        for w in ev.writes:
            st = self.res.setdefault(w, {"writer": None, "readers": []})
            if st["writer"] is not None:
                ev.deps.append(st["writer"])
            for rd in st["readers"]:
                ev.deps.append(rd)
            st["writer"] = ev
            st["readers"] = []
        self.events.append(ev)
        return ev

    def dma(self, engine, dst, src, reads=(), writes=(), deps=(), tag="", role=None, **kw):
        return self.emit(engine, lambda e: e.dma_start(dst, src, **kw),
                         reads=reads, writes=writes, deps=deps, is_dma=True, tag=tag,
                         role=role)

    def barrier(self, engine, reads, tag="end"):
        return self.emit(engine, lambda e: None, reads=reads, tag=tag)

    def claim(self, reads, writes, tag="claim"):
        return self.emit("gpsimd", lambda e: None, reads=reads, writes=writes, tag=tag)

    def buf(self, name, shape, dtype):
        if not hasattr(self, "_bufs"):
            self._bufs = {}
        if name not in self._bufs:
            self._bufs[name] = self.nc.alloc_sbuf_tensor(name, shape, dtype)
        return self._bufs[name]

    def build(self):
        for ev in self.events:
            ev.deps = [d for d in ev.deps if d is not ev]
            ev.alldeps = list(ev.deps)
        needed = set()
        for ev in self.events:
            kept = []
            seen = set()
            for d in ev.deps:
                if d.idx in seen:
                    continue
                seen.add(d.idx)
                if (d.engine == ev.engine) and not d.is_dma:
                    if ev.engine == "tensor" and d.engine == "tensor":
                        continue
                kept.append(d)
                needed.add(d.idx)
            ev.deps = kept
        counts = {}
        for ev in self.events:
            if ev.is_dma:
                ev.sem = self.dma_sem(ev.role)
            elif ev.idx in needed:
                ev.sem = self.sems[(ev.engine, False)]
            if ev.sem is not None:
                ev.incval = 16 if ev.is_dma else 1
                counts[ev.sem] = counts.get(ev.sem, 0) + ev.incval
                ev.tokval = counts[ev.sem]
        events = self.events
        with self.nc.Block() as block:
            def make(engine_name):
                def body(eng):
                    last_wait = {}
                    for ev in events:
                        if ev.engine != engine_name:
                            continue
                        wmax = {}
                        for d in ev.deps:
                            assert d.sem is not None, f"dep without token: {d.tag} <- {ev.tag}"
                            if wmax.get(d.sem, -1) < d.tokval:
                                wmax[d.sem] = d.tokval
                        for sem, val in wmax.items():
                            if last_wait.get(sem, -1) >= val:
                                continue
                            eng.wait_ge(sem, val)
                            last_wait[sem] = val
                        ins = ev.fn(eng)
                        if ev.sem is not None:
                            if ins is None:
                                ins = eng.nop()
                            ins.then_inc(ev.sem, ev.incval)
                return body
            block.tensor(make("tensor"))
            block.vector(make("vector"))
            block.scalar(make("scalar"))
            block.gpsimd(make("gpsimd"))
            block.sync(make("sync"))


class Common:
    def __init__(self, p, want_l1_masks=False, want_masks=True, want_ffn=False,
                 scratch_view=None):
        nc = p.nc
        self.p = p
        self.ones_col_bf = nc.alloc_sbuf_tensor("ones_col_bf", [128, 1], BF16)
        self.ones_row_bf = nc.alloc_sbuf_tensor("ones_row_bf", [1, 128], BF16)
        self.ones_row_f32 = nc.alloc_sbuf_tensor("ones_row_f32", [1, 128], F32)
        self.want_masks = want_masks
        if want_masks:
            self.c_sb = nc.alloc_sbuf_tensor("c_sb", [128, 1], F32)
            self.mask0 = nc.alloc_sbuf_tensor("mask0", [128, 2032], BF16)
            self.iota0 = nc.alloc_sbuf_tensor("iota0", [128, 2032], F32)
        self.want_l1_masks = want_l1_masks
        if want_l1_masks:
            self.delta_sb = nc.alloc_sbuf_tensor("delta_sb", [128, 8], F32)
            self.mask1 = nc.alloc_sbuf_tensor("mask1", [128, 8 * 128], BF16)
        self.want_ffn = want_ffn
        if want_ffn:
            self.e8_f32 = nc.alloc_sbuf_tensor("e8_f32", [8, 128], F32)
            if scratch_view is not None:
                self.scratch = scratch_view
            else:
                self.scratch = nc.alloc_sbuf_tensor("cm_scratch", [128, 20480], BF16)[:]

    def emit_setup_consts(self):
        p = self.p
        p.emit("gpsimd", lambda e: e.memset(self.ones_col_bf[:], 1.0),
               writes=["ones_col"], tag="ones_col")
        p.emit("gpsimd", lambda e: e.memset(self.ones_row_bf[:], 1.0),
               writes=["ones_row_bf"], tag="ones_row_bf")
        p.emit("gpsimd", lambda e: e.memset(self.ones_row_f32[:], 1.0),
               writes=["ones_row_f32"], tag="ones_row_f32")

    def emit_setup(self, c_in, delta_in=None):
        p = self.p
        self.emit_setup_consts()
        p.dma("sync", self.c_sb[:], c_in, writes=["c_sb"], tag="ld_c")
        p.emit("gpsimd", lambda e: e.iota(
            self.iota0[:], [[-8, 2032]], base=4096, channel_multiplier=1,
            allow_small_or_imprecise_dtypes=True), writes=["iota0"], tag="iota0")
        p.emit("vector", lambda e: e.tensor_scalar(
            self.mask0[:], self.iota0[:], 0.0, self.c_sb[:], ALU.add, ALU.is_le),
            reads=["iota0", "c_sb"], writes=["mask0"], tag="mask0")
        if self.want_l1_masks:
            p.dma("sync", self.delta_sb[:], delta_in, writes=["delta"], tag="ld_delta")
            p.emit("gpsimd", lambda e: e.iota(
                self.iota0[:, 0:128], [[-1, 128]], base=0, channel_multiplier=1,
                allow_small_or_imprecise_dtypes=True),
                reads=["mask0"], writes=["iota0"], tag="iota1")
            for r in range(8):
                p.emit("vector", lambda e, r=r: e.tensor_scalar(
                    self.mask1[:, 128 * r:128 * r + 128], self.iota0[:, 0:128],
                    0.0, self.delta_sb[:, r:r + 1], ALU.add, ALU.is_le),
                    reads=["iota0", "delta"], writes=["mask1"], tag=f"mask1_{r}")



class PostLN:
    def __init__(self, p, cm, nm, ps_sv, ps_svv, ps_a, ps_b, rsv, rsvv, rsa, rsb):
        nc = p.nc
        self.p, self.cm, self.nm = p, cm, nm
        self.ps_sv, self.ps_svv, self.ps_a, self.ps_b = ps_sv, ps_svv, ps_a, ps_b
        self.rsv, self.rsvv, self.rsa, self.rsb = rsv, rsvv, rsa, rsb
        self.xres = [nc.alloc_sbuf_tensor(nm + f"_lnxr{i}", [128, HALF], F32) for i in range(2)]
        self.vsum = [nc.alloc_sbuf_tensor(nm + f"_lnvs{i}", [128, HALF], F32) for i in range(2)]
        self.vb = nc.alloc_sbuf_tensor(nm + "_lnvb", [128, HALF], BF16)
        self.sq = nc.alloc_sbuf_tensor(nm + "_lnsq", [128, HALF], BF16)
        self.abc = nc.alloc_sbuf_tensor(nm + "_lnabc", [128, 2 * HALF], F32)
        self.xnew = [nc.alloc_sbuf_tensor(nm + f"_lnxn{i}", [128, HALF], F32) for i in range(2)]

    def add_chunk(self, io, hf, dt, src_ap, src_res, first, last):
        p, cm, nm = self.p, self.cm, self.nm
        vpre = io["x_out_res"] + "_v"
        pbuf = dt % 2
        rr = io.get("res_in_res", "__none__")
        rr = rr if isinstance(rr, list) else [rr]
        p.dma("sync", self.xres[pbuf][:],
              io["res_in"][128 * dt:128 * (dt + 1), HALF * hf:HALF * (hf + 1)],
              reads=rr, writes=[nm + f"lnxr{pbuf}"],
              tag=f"ldxr{hf}_{dt}", role=f"xres{pbuf}")
        p.emit("gpsimd", lambda e: e.tensor_add(
            self.vsum[pbuf][:], src_ap, self.xres[pbuf][:]),
            reads=[src_res, nm + f"lnxr{pbuf}"],
            writes=[nm + f"lnvs{pbuf}"], tag=f"radd{hf}_{dt}")
        p.dma("sync", io["v_dram"][128 * dt:128 * (dt + 1), HALF * hf:HALF * (hf + 1)],
              self.vsum[pbuf][:], reads=[nm + f"lnvs{pbuf}"], writes=[vpre + f"{hf}_{dt}"],
              tag=f"stv{hf}_{dt}", role=f"st{pbuf}")
        p.emit("vector", lambda e: e.tensor_copy(self.vb[:], self.vsum[pbuf][:]),
               reads=[nm + f"lnvs{pbuf}"], writes=[nm + "lnvb"], tag=f"vb{hf}_{dt}")
        p.emit("scalar", lambda e: e.activation(self.sq[:], self.vsum[pbuf][:], AF.Square),
               reads=[nm + f"lnvs{pbuf}"], writes=[nm + "lnsq"], tag=f"sq{hf}_{dt}")
        p.emit("tensor", lambda e, first=first, last=last: e.matmul(
            self.ps_sv[0:1, :], cm.ones_col_bf[:], self.vb[:], start=first, stop=last),
            reads=["ones_col", nm + "lnvb"], writes=[self.rsv], tag=f"sv{hf}_{dt}")
        p.emit("tensor", lambda e, first=first, last=last: e.matmul(
            self.ps_svv[0:1, :], cm.ones_col_bf[:], self.sq[:], start=first, stop=last),
            reads=["ones_col", nm + "lnsq"], writes=[self.rsvv], tag=f"svv{hf}_{dt}")

    def finalize(self, io, hf):
        p, cm, nm = self.p, self.cm, self.nm
        vpre = io["x_out_res"] + "_v"
        R0 = self.xnew[0][0:1, :]
        R1 = self.xnew[0][64:65, :]
        R2 = self.xnew[1][0:1, :]
        R3 = self.xres[0][0:1, :]
        R4 = self.xnew[1][64:65, :]
        rn = [nm + "lnxn0", nm + "lnxn1", nm + "lnxr0"]
        p.emit("vector", lambda e: e.tensor_scalar_mul(R0, self.ps_sv[0:1, :], 1.0 / D),
               reads=[self.rsv] + rn, writes=rn, tag=f"mean{hf}")
        p.emit("vector", lambda e: e.tensor_scalar_mul(R1, self.ps_svv[0:1, :], 1.0 / D),
               reads=[self.rsvv] + rn, writes=rn, tag=f"evv{hf}")
        p.emit("vector", lambda e: e.tensor_mul(R4, R0, R0),
               reads=rn, writes=rn, tag=f"m2{hf}")
        p.emit("vector", lambda e: e.tensor_sub(R1, R1, R4),
               reads=rn, writes=rn, tag=f"var{hf}")
        p.emit("vector", lambda e: e.tensor_scalar_add(R1, R1, 1e-5),
               reads=rn, writes=rn, tag=f"vareps{hf}")
        p.emit("scalar", lambda e: e.activation(R2, R1, AF.Ln),
               reads=rn, writes=rn, tag=f"ln{hf}")
        p.emit("scalar", lambda e: e.activation(R2, R2, AF.Exp, scale=-0.5),
               reads=rn, writes=rn, tag=f"rstd{hf}")
        p.emit("vector", lambda e: e.scalar_tensor_tensor(
            R3, R0, -1.0, R2, ALU.mult, ALU.mult),
            reads=rn, writes=rn, tag=f"brow{hf}")
        p.emit("tensor", lambda e: e.matmul(
            self.ps_a[:], cm.ones_row_f32[:], R2, start=True, stop=True),
            reads=["ones_row_f32"] + rn, writes=[self.rsa], tag=f"bca{hf}")
        p.emit("tensor", lambda e: e.matmul(
            self.ps_b[:], cm.ones_row_f32[:], R3, start=True, stop=True),
            reads=["ones_row_f32"] + rn, writes=[self.rsb], tag=f"bcb{hf}")
        p.emit("scalar", lambda e: e.copy(self.abc[:, 0:HALF], self.ps_a[:]),
               reads=[self.rsa], writes=[nm + "lnabc"], tag=f"ea{hf}")
        p.emit("scalar", lambda e: e.copy(self.abc[:, HALF:2 * HALF], self.ps_b[:]),
               reads=[self.rsb, nm + "lnabc"], writes=[nm + "lnabc"], tag=f"eb{hf}")
        for dt in range(KC):
            pbuf = dt % 2
            p.dma("sync", self.xres[pbuf][:],
                  io["v_dram"][128 * dt:128 * (dt + 1), HALF * hf:HALF * (hf + 1)],
                  reads=[vpre + f"{hf}_{dt}"], writes=[nm + f"lnxr{pbuf}"],
                  tag=f"ldv{hf}_{dt}", role=f"xres{pbuf}")
            p.emit("gpsimd", lambda e, pbuf=pbuf: e.tensor_mul(
                self.xnew[pbuf][:], self.xres[pbuf][:], self.abc[:, 0:HALF]),
                reads=[nm + f"lnxr{pbuf}", nm + "lnabc"], writes=[nm + f"lnxn{pbuf}"],
                tag=f"ap1_{hf}_{dt}")
            p.emit("gpsimd", lambda e, pbuf=pbuf: e.tensor_add(
                self.xnew[pbuf][:], self.xnew[pbuf][:], self.abc[:, HALF:2 * HALF]),
                reads=[nm + f"lnxn{pbuf}", nm + "lnabc"], writes=[nm + f"lnxn{pbuf}"],
                tag=f"ap2_{hf}_{dt}")
            p.dma("sync", io["x_out"][128 * dt:128 * (dt + 1), HALF * hf:HALF * (hf + 1)],
                  self.xnew[pbuf][:], reads=[nm + f"lnxn{pbuf}"],
                  writes=[io["x_out_res"] + f"_{hf}_{dt}"],
                  tag=f"stx{hf}_{dt}", role=f"st{pbuf}")


def emit_attention_layer(p, cm, L, io):
    nc = p.nc
    nm = L["nm"]
    l0 = L["kv_mode"] == "l0"

    ar = io.get("arena")
    if ar is None:
        xn = p.buf(nm + "_xn", [128, KC * HALF], BF16)
        attn = p.buf(nm + "_attn", [128, NH * HALF], BF16)
        k_sb = p.buf(nm + "_k", [128, T], BF16)
        v_sb = p.buf(nm + "_v", [128, T], BF16)
        wq_sb = [p.buf(nm + f"_wq{i}", [128, KC * 128], BF16) for i in range(2)]
        wo_sb = [p.buf(nm + f"_wo{i}", [128, NH * 128], BF16) for i in range(2)]
    else:
        xn, attn, k_sb, v_sb = ar["xn"], ar["attn"], ar["k"], ar["v"]
        wq_sb, wo_sb = ar["wq"], ar["wo"]
    q_sb = [p.buf(nm + f"_q{i}", [128, HALF], BF16) for i in range(2)]
    exp_sb = [p.buf(nm + f"_exp{i}", [128, HALF], BF16) for i in range(4)]
    av_sb = p.buf(nm + "_av", [128, HALF], BF16)
    rec_sb = p.buf(nm + "_rec", [1, HALF], F32)
    if io.get("proj_t") is not None:
        proj_sb = io["proj_t"]
        proj_res = io["proj_res"]
    else:
        proj_sb = [p.buf(nm + f"_proj{i}", [128, HALF], F32)[:] for i in range(2)]
        proj_res = [nm + f"proj{i}" for i in range(2)]
    if l0:
        qb_sb = p.buf(nm + "_qb", [128, NH], F32)
        kb_sb = p.buf(nm + "_kb", [128, 1], F32)
        vbias_sb = p.buf(nm + "_vbias", [1, 128], F32)
        vbc_sb = p.buf(nm + "_vbc", [128, 128], F32)
        ob_sb = p.buf(nm + "_ob", [128, KC], F32)
        wkv_sb = io.get("wkv_t")
        if wkv_sb is None:
            wkv_sb = p.buf(nm + "_wkv", [128, 2 * KC * 128], BF16)[:]
        xs_sb = [xn, attn]
        xs_res = [nm + "xn", nm + "attn"]

    if io.get("pb") is not None:
        pb = io["pb"]
    else:
        pb = [nc.alloc_psum_tensor(nm + f"_pb{i}", [128, 512], F32) for i in range(8)]
        io["pb"] = pb
        io["pb_res"] = [nm + f"_pb{i}" for i in range(8)]
    ps_sc = [pb[0], pb[1], pb[6], pb[4]]
    sc_R = [0, 1, 6, 4]
    ps_av, ps_sum, ps_q, ps_bc, ps_proj, ps_misc = pb[2], pb[3], pb[4], pb[5], pb[6], pb[7]
    R = {i: io["pb_res"][i] for i in range(8)}
    pln = io.get("_pln") or PostLN(p, cm, nm, ps_sum, ps_misc, ps_bc, ps_q,
                                   R[3], R[7], R[5], R[4])
    io["_xn_t"] = xn
    io["_pln"] = pln
    io["_wq"] = wq_sb
    io["_wo"] = wo_sb
    io["_attn_t"] = attn
    io["_pb"] = pb
    io["_pb_res"] = [R[i] for i in range(8)]

    if l0:
        p.dma("sync", qb_sb[:], io["qb"], writes=[nm + "qb"], tag="ld_qb")
        p.dma("sync", kb_sb[:], io["kb"], writes=[nm + "kb"], tag="ld_kb")
        p.dma("sync", vbias_sb[:], io["vbias"], writes=[nm + "vbias"], tag="ld_vbias")
        p.dma("sync", ob_sb[:], io["ob"], writes=[nm + "ob"], tag="ld_ob")
        for idx, nmw in enumerate(("wk", "wv")):
            dst = wkv_sb[:, idx * KC * 128:(idx + 1) * KC * 128]
            p.dma("gpsimd", dst, io[nmw], writes=[nm + "wkv"], tag="ld_" + nmw, role="wq")
        p.emit("tensor", lambda e: e.matmul(
            ps_bc[:, 0:128], cm.ones_row_f32[:], vbias_sb[:], start=True, stop=True),
            reads=["ones_row_f32", nm + "vbias"], writes=[R[5]], tag="vb_bc")
        p.emit("scalar", lambda e: e.copy(vbc_sb[:], ps_bc[:, 0:128]),
               reads=[R[5]], writes=[nm + "vbc"], tag="vb_evac")

    if l0:
        NS = T // HALF
        for ss in range(NS):
            buf = ss % 2
            dst = xs_sb[buf][:, 0:KC * HALF]
            src = io["xT"][:, KC * HALF * ss:KC * HALF * (ss + 1)]
            p.dma("gpsimd", dst, src, writes=[xs_res[buf]], tag=f"xs{ss}", role="xs")
            for kc in range(KC):
                p.emit("tensor", lambda e, kc=kc, buf=buf: e.matmul(
                    ps_misc[:], wkv_sb[:, 128 * kc:128 * kc + 128],
                    xs_sb[buf][:, HALF * kc:HALF * (kc + 1)],
                    start=(kc == 0), stop=(kc == KC - 1)),
                    reads=[nm + "wkv", xs_res[buf]], writes=[R[7]], tag=f"kmm{ss}_{kc}")
            p.emit("scalar", lambda e, ss=ss: e.add(
                k_sb[:, HALF * ss:HALF * (ss + 1)], ps_misc[:], kb_sb[:, 0:1]),
                reads=[R[7], nm + "kb"], writes=[nm + "k"], tag=f"kev{ss}")
            for st in range(4):
                for kc in range(KC):
                    p.emit("tensor", lambda e, kc=kc, st=st, buf=buf: e.matmul(
                        ps_q[:, 0:128],
                        xs_sb[buf][:, HALF * kc + 128 * st: HALF * kc + 128 * st + 128],
                        wkv_sb[:, KC * 128 + 128 * kc: KC * 128 + 128 * kc + 128],
                        start=(kc == 0), stop=(kc == KC - 1)),
                        reads=[nm + "wkv", xs_res[buf]], writes=[R[4]],
                        tag=f"vmm{ss}_{st}_{kc}")
                col = 128 * (4 * ss + st)
                p.emit("vector", lambda e, col=col: e.tensor_add(
                    v_sb[:, col:col + 128], ps_q[:, 0:128], vbc_sb[:]),
                    reads=[R[4], nm + "vbc"], writes=[nm + "v"], tag=f"vev{ss}_{st}")

    def tile_cfg_l0(K, hf):
        gK8 = 16 * K - HALF * hf
        i0 = max(0, -(-(8 * gK8 - 7) // 8))
        if i0 >= HALF:
            return None
        i1 = min(HALF, max(i0, -(-(8 * gK8 + 128) // 8)))
        off = 512 - gK8 + i0
        return (i0, i1, lambda w: cm.mask0[:, off:off + w])

    def tile_cfg_l1(r, B, hf):
        w0 = 128 * B - HALF * hf
        i0 = max(0, w0)
        if i0 >= HALF:
            return None
        i1 = min(HALF, w0 + 128)
        if i1 <= i0:
            return (i0, i0, None)
        moff = 128 * r + (i0 - w0)
        return (i0, i1, lambda w: cm.mask1[:, moff:moff + w])

    for hf in range(2):
        if io.get("x_in_c") is not None:
            p.dma("gpsimd", xn[:], io["x_in_c"][:, KC * HALF * hf:KC * HALF * (hf + 1)],
                  reads=io.get("x_in_res", []), writes=[nm + "xn"],
                  tag=f"xn{hf}", role="xs")
        else:
            dst = xn[:].rearrange("p (a s) -> p a s", a=KC)
            src = io["x_in"][:, HALF * hf:HALF * (hf + 1)].rearrange("(a p) s -> p a s", p=128)
            p.dma("gpsimd", dst, src, reads=io.get("x_in_res", []), writes=[nm + "xn"],
                  tag=f"xn{hf}", role="xs")

        def make_tail(h, hf=hf):
            def tail():
                p.emit("vector", lambda e: e.reciprocal(rec_sb[:], ps_sum[0:1, :]),
                       reads=[R[3]], writes=[nm + "rec"], tag=f"rec{hf}_{h}")
                p.emit("tensor", lambda e: e.matmul(
                    ps_bc[:], cm.ones_row_f32[:], rec_sb[:], start=True, stop=True),
                    reads=["ones_row_f32", nm + "rec"], writes=[R[5]], tag=f"rbc{hf}_{h}")
                p.emit("scalar", lambda e: e.copy(av_sb[:], ps_av[:]),
                       reads=[R[2]], writes=[nm + "av"], tag=f"avev{hf}_{h}")
                p.emit("vector", lambda e, h=h: e.tensor_mul(
                    attn[:, HALF * h:HALF * (h + 1)], av_sb[:], ps_bc[:]),
                    reads=[nm + "av", R[5]], writes=[nm + "attn"], tag=f"at{hf}_{h}")
            return tail

        pend_tail = None
        for h in range(NH):
            wbuf = h % 2
            if hf == 0 or True:
                pass
            if h == 0 and hf == 0:
                p.dma("gpsimd", wq_sb[0][:], io["wq"][0],
                      writes=[nm + "wq0"], tag=f"ldwq0_{hf}", role="wq")
            nh_ = h + 1 if h + 1 < NH else (0 if hf == 0 else None)
            if nh_ is not None:
                nbuf = nh_ % 2
                p.dma("gpsimd", wq_sb[nbuf][:], io["wq"][nh_],
                      writes=[nm + f"wq{nbuf}"], tag=f"ldwq{nh_}_{hf}", role="wq")
            if (not l0) and h % 4 == 0:
                kh = h // 4
                for r in range(8):
                    p.dma("sync", k_sb[:, 1024 * r:1024 * (r + 1)],
                          io["gathered"][2048 * r + 128 * kh: 2048 * r + 128 * kh + 128, :],
                          reads=[io["gathered_res"]], writes=[nm + f"k{r}"], tag=f"ldk{kh}_{r}", role="kv")
                    dstv = v_sb[:, 1024 * r:1024 * (r + 1)].rearrange("p (b m) -> p b m", b=8)
                    srcv = io["gathered"][2048 * r + 1024: 2048 * (r + 1),
                                          128 * kh:128 * kh + 128].rearrange(
                        "(b p) m -> p b m", p=128)
                    p.dma("sync", dstv, srcv, reads=[io["gathered_res"]],
                          writes=[nm + f"v{r}"], tag=f"ldv{kh}_{r}", role="kv")
            for kc in range(KC):
                p.emit("tensor", lambda e, kc=kc, wbuf=wbuf: e.matmul(
                    ps_q[:], wq_sb[wbuf][:, 128 * kc:128 * kc + 128],
                    xn[:, HALF * kc:HALF * (kc + 1)],
                    start=(kc == 0), stop=(kc == KC - 1)),
                    reads=[nm + f"wq{wbuf}", nm + "xn"], writes=[R[4]],
                    tag=f"qmm{hf}_{h}_{kc}")
            qbuf = h % 2
            if l0:
                p.emit("scalar", lambda e, qbuf=qbuf, h=h: e.add(
                    q_sb[qbuf][:], ps_q[:], qb_sb[:, h:h + 1]),
                    reads=[R[4], nm + "qb"], writes=[nm + f"q{qbuf}"], tag=f"qev{hf}_{h}")
            else:
                p.emit("scalar", lambda e, qbuf=qbuf: e.copy(q_sb[qbuf][:], ps_q[:]),
                       reads=[R[4]], writes=[nm + f"q{qbuf}"], tag=f"qev{hf}_{h}")
            if pend_tail is not None:
                pend_tail()
                pend_tail = None

            if l0:
                tiles = []
                for K in range(T // 128):
                    cfg = tile_cfg_l0(K, hf)
                    if cfg is not None:
                        tiles.append((k_sb[:, 128 * K:128 * K + 128], nm + "k",
                                      v_sb[:, 128 * K:128 * K + 128], nm + "v", cfg))
            else:
                tiles = []
                for B in range(8):
                    for r in range(8):
                        cfg = tile_cfg_l1(r, B, hf)
                        if cfg is not None:
                            o = 1024 * r + 128 * B
                            tiles.append((k_sb[:, o:o + 128], nm + f"k{r}",
                                          v_sb[:, o:o + 128], nm + f"v{r}", cfg))
            n_tiles = len(tiles)

            def rest_stage(idx, h=h, hf=hf, qbuf=qbuf, tiles=tiles, n_tiles=n_tiles):
                k_ap, k_res, v_ap, v_res, cfg = tiles[idx]
                i0, i1, mfn = cfg
                sb = idx % 4
                rb = sc_R[sb]
                first = idx == 0
                last = idx == n_tiles - 1
                p.emit("scalar", lambda e, sb=sb, i0=i0: e.activation(
                    exp_sb[sb][:, i0:HALF], ps_sc[sb][:, i0:HALF], AF.Exp,
                    scale=SM_SCALE),
                    reads=[R[rb]], writes=[nm + f"exp{sb}"], tag=f"ex{hf}_{h}_{idx}")
                if i1 > i0 and mfn is not None:
                    p.emit("vector", lambda e, sb=sb, i0=i0, i1=i1, mfn=mfn: e.tensor_mul(
                        exp_sb[sb][:, i0:i1], exp_sb[sb][:, i0:i1], mfn(i1 - i0)),
                        reads=["mask0" if l0 else "mask1", nm + f"exp{sb}"],
                        writes=[nm + f"exp{sb}"], tag=f"mk{hf}_{h}_{idx}")
                p.emit("tensor", lambda e, sb=sb, i0=i0, v_ap=v_ap, first=first, last=last: e.matmul(
                    ps_av[:, i0:HALF], v_ap, exp_sb[sb][:, i0:HALF],
                    start=first, stop=last),
                    reads=[v_res, nm + f"exp{sb}"], writes=[R[2]],
                    tag=f"av{hf}_{h}_{idx}")
                p.emit("tensor", lambda e, sb=sb, i0=i0, first=first, last=last: e.matmul(
                    ps_sum[0:1, i0:HALF], cm.ones_col_bf[:], exp_sb[sb][:, i0:HALF],
                    start=first, stop=last),
                    reads=["ones_col", nm + f"exp{sb}"], writes=[R[3]],
                    tag=f"su{hf}_{h}_{idx}")

            LA = 2
            for idx, (k_ap, k_res, v_ap, v_res, cfg) in enumerate(tiles):
                i0 = cfg[0]
                sb = idx % 4
                rb = sc_R[sb]
                p.emit("tensor", lambda e, sb=sb, k_ap=k_ap, i0=i0, qbuf=qbuf: e.matmul(
                    ps_sc[sb][:, i0:HALF], k_ap, q_sb[qbuf][:, i0:HALF],
                    start=True, stop=True),
                    reads=[k_res, nm + f"q{qbuf}"], writes=[R[rb]],
                    tag=f"sc{hf}_{h}_{idx}")
                if idx >= LA:
                    rest_stage(idx - LA)
            for j in range(max(0, n_tiles - LA), n_tiles):
                rest_stage(j)
            pend_tail = make_tail(h)
        if pend_tail is not None:
            pend_tail()
            pend_tail = None

        if hf == 0:
            p.dma("gpsimd", wo_sb[0][:], io["wo"][0],
                  writes=[nm + "wo0"], tag=f"ldwo0_{hf}", role="wo")
        pending = []
        for dt in range(KC):
            wbuf = dt % 2
            ndt = dt + 1 if dt + 1 < KC else (0 if hf == 0 else None)
            if ndt is not None:
                nbuf = ndt % 2
                p.dma("gpsimd", wo_sb[nbuf][:], io["wo"][ndt],
                      writes=[nm + f"wo{nbuf}"], tag=f"ldwo{ndt}_{hf}", role="wo")
            pbank = ps_proj if dt % 2 == 0 else ps_av
            pR = R[6] if dt % 2 == 0 else R[2]
            for h in range(NH):
                p.emit("tensor", lambda e, h=h, wbuf=wbuf, pbank=pbank: e.matmul(
                    pbank[:], wo_sb[wbuf][:, 128 * h:128 * h + 128],
                    attn[:, HALF * h:HALF * (h + 1)],
                    start=(h == 0), stop=(h == NH - 1)),
                    reads=[nm + f"wo{wbuf}", nm + "attn"], writes=[pR],
                    tag=f"pj{hf}_{dt}_{h}")
            pbuf = dt % 2
            if l0:
                p.emit("scalar", lambda e, pbuf=pbuf, dt=dt, pbank=pbank: e.add(
                    proj_sb[pbuf], pbank[:], ob_sb[:, dt:dt + 1]),
                    reads=[pR, nm + "ob"], writes=[proj_res[pbuf]],
                    tag=f"pjev{hf}_{dt}")
            else:
                p.emit("scalar", lambda e, pbuf=pbuf, pbank=pbank: e.copy(
                    proj_sb[pbuf], pbank[:]),
                    reads=[pR], writes=[proj_res[pbuf]], tag=f"pjev{hf}_{dt}")
            pending.append((dt, pbuf))
            if len(pending) > 1:
                d0, pb0_ = pending.pop(0)
                pln.add_chunk(io, hf, d0, proj_sb[pb0_], proj_res[pb0_],
                              first=(d0 == 0), last=(d0 == KC - 1))
        for d0, pb0_ in pending:
            pln.add_chunk(io, hf, d0, proj_sb[pb0_], proj_res[pb0_],
                          first=(d0 == 0), last=(d0 == KC - 1))
        pln.finalize(io, hf)



def emit_ffn_setup(p, cm, io):
    nc = p.nc
    e8_bf = nc.alloc_sbuf_tensor("e8_bf", [8, 128], BF16)
    p.dma("sync", cm.e8_f32[:], io["e8"], writes=["e8f"], tag="ld_e8", role="misc")
    p.emit("vector", lambda e: e.tensor_copy(e8_bf[:], cm.e8_f32[:]),
           reads=["e8f"], writes=["e8"], tag="cv_e8")
    ps = io["pb"][0]
    zsb = cm.scratch.bitcast(I32)[0:96, 0:1376]
    tmp = cm.scratch.bitcast(I32)[0:96, 1376:2752]
    zunp = cm.scratch[0:96, 5504:5504 + 1408]
    zrow = cm.scratch[0:8, 6912:6912 + INTER_PAD]
    stage = cm.scratch[0:128, 18176:18176 + 2816]
    SCR = "scratch"
    specs = [
        ("g", io["gate_zeros"], 32, 1376, 10, io["z1g"], INTER_PAD, 4, io["zbc_g"]),
        ("u", io["up_zeros"], 32, 1376, 10, io["z1u"], INTER_PAD, 4, io["zbc_u"]),
        ("d", io["down_zeros"], 86, 512, 4, io["z1d"], D, 11, io["zbc_d"]),
    ]
    for (mn, zin, G, KW, NB, z1d, W, RC, zbcd) in specs:
        p.dma("sync", zsb[0:G, 0:KW], zin, reads=[SCR], writes=[SCR],
              tag=f"ldz_{mn}", role="misc")
        p.emit("gpsimd", lambda e, G=G, KW=KW: e.memset(zunp[0:G, KW:KW + 32], 0.0),
               reads=[SCR], writes=[SCR], tag=f"zpad_{mn}")
        for j2 in range(8):
            p.emit("vector", lambda e, j2=j2, G=G, KW=KW: e.tensor_scalar(
                tmp[0:G, 0:KW], zsb[0:G, 0:KW], 4 * j2, 15,
                ALU.logical_shift_right, ALU.bitwise_and),
                reads=[SCR], writes=[SCR], tag=f"zs_{mn}_{j2}")
            p.emit("vector", lambda e, G=G, KW=KW: e.tensor_scalar_add(
                tmp[0:G, 0:KW], tmp[0:G, 0:KW], 1),
                reads=[SCR], writes=[SCR], tag=f"z1_{mn}_{j2}")
            p.emit("vector", lambda e, j2=j2, G=G, KW=KW: e.tensor_copy(
                zunp[0:G, 0:KW], tmp[0:G, 0:KW]),
                reads=[SCR], writes=[SCR], tag=f"zc_{mn}_{j2}")
            nfull = (KW // 128) * 128
            from concourse.ap import AP as _AP
            if nfull > 0:
                dst = _AP(z1d.tensor, 128 * j2,
                          [[W, G], [1024, nfull // 128], [1, 128]])
                p.dma("sync", dst, zunp[0:G, 0:nfull].rearrange(
                    "g (a k) -> g a k", k=128),
                    reads=[SCR], writes=[f"z1_{mn}"], tag=f"zst_{mn}_{j2}", role="misc")
            if KW > nfull:
                dst = _AP(z1d.tensor, 128 * j2 + 1024 * (KW // 128),
                          [[W, G], [1, KW - nfull + 32]])
                p.dma("sync", dst, zunp[0:G, nfull:KW + 32],
                    reads=[SCR], writes=[f"z1_{mn}"], tag=f"zstt_{mn}_{j2}", role="misc")
        for Rr in range(RC):
            gr = min(8, G - 8 * Rr)
            p.dma("sync", zrow[0:gr, 0:W], z1d[8 * Rr:8 * Rr + gr, :],
                  reads=[f"z1_{mn}", SCR], writes=[SCR], tag=f"zrow_{mn}_{Rr}", role="misc")
            nseg = (W + 2815) // 2816
            for seg in range(nseg):
                w0 = 2816 * seg
                w1 = min(W, w0 + 2816)
                for ns in range(w0, w1, 512):
                    nw = min(512, w1 - ns)
                    p.emit("tensor", lambda e, gr=gr, ns=ns, nw=nw: e.matmul(
                        ps[:, 0:nw], e8_bf[0:gr, :], zrow[0:gr, ns:ns + nw],
                        start=True, stop=True),
                        reads=["e8", SCR], writes=[io["pb_res"][0]], tag=f"zbm_{mn}_{Rr}_{ns}")
                    p.emit("scalar", lambda e, ns=ns, w0=w0, nw=nw: e.copy(
                        stage[:, ns - w0:ns - w0 + nw], ps[:, 0:nw]),
                        reads=[io["pb_res"][0]], writes=["stage"], tag=f"zbe_{mn}_{Rr}_{ns}")
                p.dma("sync", zbcd[Rr][:, w0:w1], stage[:, 0:w1 - w0],
                      reads=["stage"], writes=[f"zbc_{mn}"], tag=f"zbs_{mn}_{Rr}_{seg}",
                      role="misc") if ns + 512 >= w1 else None
    return


def emit_ffn_layer(p, cm, L, io):
    nc = p.nc
    nm = L["nm"]
    if io.get("xn_t") is not None:
        xn = io["xn_t"]
        xn_res = io["xn_res"]
    else:
        xn = nc.alloc_sbuf_tensor(nm + "_xn", [128, KC * HALF], BF16)
        xn_res = nm + "xn"
    io["_ffn_xn"] = xn
    io["_ffn_xn_res"] = xn_res
    inter = io["inter_t"]
    p.emit("vector", lambda e: None, reads=[],
           writes=[nm + "inter"] + io.get("inter_claims", ["scratch", "stage"]),
           tag="inter_claim")
    ea = io.get("ea")
    if ea is None:
        qw = {m: [nc.alloc_sbuf_tensor(nm + f"_qw{m}{i}", [128, 512], I32)[:] for i in range(2)]
              for m in "gud"}
        sbc = {m: [nc.alloc_sbuf_tensor(nm + f"_sbc{m}{i}", [128, 512], BF16)[:] for i in range(2)]
               for m in "gud"}
        zbc = {m: [nc.alloc_sbuf_tensor(nm + f"_zbc{m}{i}", [128, 512], BF16)[:] for i in range(2)]
               for m in "gud"}
        qi = {m: [nc.alloc_sbuf_tensor(nm + f"_qi{m}{i}", [128, 512], I32)[:] for i in range(2)]
              for m in "gu"}
        qbf = {m: [nc.alloc_sbuf_tensor(nm + f"_qbf{m}{i}", [128, 512], BF16)[:] for i in range(2)]
               for m in "gu"}
        sg_sb = nc.alloc_sbuf_tensor(nm + "_sg", [128, HALF], BF16)[:]
        f16_sb = nc.alloc_sbuf_tensor(nm + "_f16", [128, HALF], F16)[:]
        fsub = [nc.alloc_sbuf_tensor(nm + f"_fsub{i}", [128, HALF], F32)[:] for i in range(2)]
    else:
        eai = ea.bitcast(I32)
        eaf = ea.bitcast(F32)
        eah = ea.bitcast(F16)
        qw = {m: [eai[:, 512 * (2 * mi + i):512 * (2 * mi + i) + 512] for i in range(2)]
              for mi, m in enumerate("gud")}
        qi = {m: [eai[:, 3072 + 512 * i:3584 + 512 * i] for i in range(2)] for m in "gu"}
        sbc = {m: [ea[:, 8192 + 512 * (2 * mi + i):8192 + 512 * (2 * mi + i) + 512]
                   for i in range(2)] for mi, m in enumerate("gu")}
        zbc = {m: [ea[:, 10240 + 512 * (2 * mi + i):10240 + 512 * (2 * mi + i) + 512]
                   for i in range(2)] for mi, m in enumerate("gu")}
        sbc["d"] = sbc["g"]
        zbc["d"] = zbc["g"]
        qbf = {"g": [ea[:, 12288 + 512 * i:12800 + 512 * i] for i in range(2)],
               "u": [ea[:, 13312 + 512 * i:13824 + 512 * i] for i in range(2)]}
        sg_sb = ea[:, 14336:14848]
        fsub = [eaf[:, 7936 + 512 * i:8448 + 512 * i] for i in range(2)]
        f16_sb = eah[:, 14848:15360]
    qi["d"] = qi["g"]
    qbf["d"] = qbf["g"]
    wt = qbf
    mres = {"g": "g", "u": "u", "d": "g"}
    if ea is None:
        sres = {"g": "g", "u": "u", "d": "d"}
        qires = {"g": "g", "u": "u", "d": "g"}
    else:
        sres = {"g": "g", "u": "u", "d": "g"}
        qires = {"g": "g", "u": "g", "d": "g"}

    pb = io["pb"]
    R = {i: io["pb_res"][i] for i in range(8)}
    if io.get("swap_banks"):
        gate_ps, up_ps = pb[4:8], pb[0:4]
        gate_R = [4, 5, 6, 7]
        up_R = [0, 1, 2, 3]
    else:
        gate_ps, up_ps = pb[0:4], pb[4:8]
        gate_R = [0, 1, 2, 3]
        up_R = [4, 5, 6, 7]
    down_ps = [pb[0], pb[1], pb[2], pb[6]]
    down_R = [R[0], R[1], R[2], R[6]]
    pln = io.get("_pln") or PostLN(p, cm, nm, pb[3], pb[7], pb[5], pb[4],
                                   R[3], R[7], R[5], R[4])

    def dequant(m, par, j, Rr, kcols, tagp):
        jb = j % 2
        mr = mres[m]
        sr = sres[m]
        qr = qires[m]
        p.emit("vector", lambda e: e.tensor_scalar(
            qi[m][jb][0:kcols, :], qw[m][par][0:kcols, :], 4 * j, 15,
            ALU.logical_shift_right, ALU.bitwise_and),
            reads=[nm + f"qw{m}{par}"], writes=[nm + f"qi{qr}{jb}"], tag=f"dq1{tagp}")
        p.emit("scalar", lambda e: e.copy(
            qbf[m][jb][0:kcols, :], qi[m][jb][0:kcols, :]),
            reads=[nm + f"qi{qr}{jb}"], writes=[nm + f"qbf{mr}{jb}"], tag=f"dq2{tagp}")
        p.emit("vector", lambda e: e.tensor_sub(
            qbf[m][jb][0:kcols, :], qbf[m][jb][0:kcols, :], zbc[m][par][0:kcols, :]),
            reads=[nm + f"qbf{mr}{jb}", nm + f"zbc{sr}{par}"],
            writes=[nm + f"qbf{mr}{jb}"], tag=f"dq3{tagp}")
        p.emit("vector", lambda e: e.tensor_mul(
            qbf[m][jb][0:kcols, :], qbf[m][jb][0:kcols, :], sbc[m][par][0:kcols, :]),
            reads=[nm + f"qbf{mr}{jb}", nm + f"sbc{sr}{par}"],
            writes=[nm + f"qbf{mr}{jb}"], tag=f"dq4{tagp}")
        return qbf[m][jb], nm + f"qbf{mr}{jb}"

    def load_tiles(m, par, Rr, cols, c0, kcols, tagp):
        p.dma("sync", qw[m][par][0:kcols, :], io[f"qw_{m}"][128 * Rr:128 * Rr + kcols,
                                                           c0:c0 + cols],
              reads=[], writes=[nm + f"qw{m}{par}"], tag=f"lq{tagp}", role="qw")
        sr = sres[m]
        p.dma("gpsimd", sbc[m][par][:, 0:cols], io[f"sbc_{m}"][Rr][:, c0:c0 + cols],
              reads=[], writes=[nm + f"sbc{sr}{par}"], tag=f"ls{tagp}", role="sbc")
        p.dma("sync", zbc[m][par][:, 0:cols], io[f"zbc_{m}"][Rr][:, c0:c0 + cols],
              reads=[f"zbc_{m}", "__none"], writes=[nm + f"zbc{sr}{par}"], tag=f"lz{tagp}",
              role="zbc")

    for hf in range(2):
        dst = xn[:].rearrange("p (a s) -> p a s", a=KC)
        srcx = io["x_in"][:, HALF * hf:HALF * (hf + 1)].rearrange("(a p) s -> p a s", p=128)
        p.dma("gpsimd", dst, srcx, reads=io.get("x_in_res", []), writes=[xn_res],
              tag=f"fxn{hf}", role="xs")
        for ns in range(INTER_PAD // 512):
            for Rr in range(4):
                par = Rr % 2
                load_tiles("g", par, Rr, 512, 512 * ns, 128, f"g{hf}_{ns}_{Rr}")
                load_tiles("u", par, Rr, 512, 512 * ns, 128, f"u{hf}_{ns}_{Rr}")
                for j in range(8):
                    wg, wg_res = dequant("g", par, j, Rr, 128, f"g{hf}_{ns}_{Rr}_{j}")
                    NTO = io.get("nt_order", [0, 1, 2, 3])
                    if io.get("dbg_wg") is not None and hf == 0 and ns == 0 and Rr == 0 and j == 1:
                        p.dma("sync", io["dbg_wg"], wg, reads=[wg_res],
                              writes=["dbgwg"], tag="dbgwg", role="misc")
                        p.dma("sync", io["dbg_sbc"], sbc["g"][par][:, 0:512], reads=[nm + "sbcg" + str(par)],
                              writes=["dbgsbc"], tag="dbgsbc", role="misc")
                        p.dma("sync", io["dbg_zbc"], zbc["g"][par][:, 0:512], reads=[nm + "zbcg" + str(par)],
                              writes=["dbgzbc"], tag="dbgzbc", role="misc")
                    tau = 8 * Rr + j
                    for nt in NTO:
                        p.emit("tensor", lambda e, nt=nt, wg=wg, tau=tau, Rr=Rr, j=j: e.matmul(
                            gate_ps[nt][:], wg[:, 128 * nt:128 * nt + 128],
                            xn[:, HALF * tau:HALF * (tau + 1)],
                            start=(Rr == 0 and j == 0), stop=(Rr == 3 and j == 7)),
                            reads=[wg_res, xn_res], writes=[R[gate_R[nt]]],
                            tag=f"gm{hf}_{ns}_{Rr}_{j}_{nt}")
                    wu, wu_res = dequant("u", par, j, Rr, 128, f"u{hf}_{ns}_{Rr}_{j}")
                    for nt in NTO:
                        p.emit("tensor", lambda e, nt=nt, wu=wu, tau=tau, Rr=Rr, j=j: e.matmul(
                            up_ps[nt][:], wu[:, 128 * nt:128 * nt + 128],
                            xn[:, HALF * tau:HALF * (tau + 1)],
                            start=(Rr == 0 and j == 0), stop=(Rr == 3 and j == 7)),
                            reads=[wu_res, xn_res], writes=[R[up_R[nt]]],
                            tag=f"um{hf}_{ns}_{Rr}_{j}_{nt}")
            if io.get("dbg_ps") is not None and hf == 0 and ns == 0:
                p.emit("scalar", lambda e: e.copy(fsub[0], gate_ps[3][:]),
                       reads=[R[gate_R[3]]], writes=[nm + "fsub0"], tag="dbgpse")
                p.dma("sync", io["dbg_ps"], fsub[0], reads=[nm + "fsub0"],
                      writes=["dbgps"], tag="dbgps", role="misc")
                p.emit("scalar", lambda e: e.copy(fsub[0], up_ps[3][:]),
                       reads=[R[up_R[3]], "dbgps"], writes=[nm + "fsub0"], tag="dbgpse2")
                p.dma("sync", io["dbg_ps2"], fsub[0], reads=[nm + "fsub0"],
                      writes=["dbgps2"], tag="dbgps2", role="misc")
            for nt in range(4):
                p.emit("scalar", lambda e, nt=nt: e.activation(
                    sg_sb, gate_ps[nt][:], AF.Silu),
                    reads=[R[gate_R[nt]]], writes=[nm + "sg"], tag=f"sl{hf}_{ns}_{nt}")
                slot = 4 * ns + nt
                p.emit("vector", lambda e, nt=nt, slot=slot: e.tensor_mul(
                    inter[:, HALF * slot:HALF * (slot + 1)], sg_sb, up_ps[nt][:]),
                    reads=[nm + "sg", R[up_R[nt]]], writes=[nm + "inter"],
                    tag=f"im{hf}_{ns}_{nt}")
        if io.get("dbg_inter") is not None and hf == 0:
            p.dma("sync", io["dbg_inter"], inter[:, 0:4096], reads=[nm + "inter"],
                  writes=["dbgi"], tag="dbgi", role="misc")
            p.dma("sync", io["dbg_xn"], xn[:, 0:4096], reads=[xn_res],
                  writes=["dbgx"], tag="dbgx", role="misc")
        dpending = []
        for ds in range(8):
            for Rr in range(11):
                par = Rr % 2
                kcols = 96 if Rr == 10 else 128
                load_tiles("d", par, Rr, 512, 512 * ds, kcols, f"d{hf}_{ds}_{Rr}")
                for j in range(8):
                    wd, wd_res = dequant("d", par, j, Rr, kcols, f"d{hf}_{ds}_{Rr}_{j}")
                    tau = 8 * Rr + j
                    for dtl in range(4):
                        p.emit("tensor", lambda e, dtl=dtl, wd=wd, tau=tau, kcols=kcols, Rr=Rr, j=j: e.matmul(
                            down_ps[dtl][:], wd[0:kcols, 128 * dtl:128 * dtl + 128],
                            inter[0:kcols, HALF * tau:HALF * (tau + 1)],
                            start=(Rr == 0 and j == 0), stop=(Rr == 10 and j == 7)),
                            reads=[wd_res, nm + "inter"], writes=[down_R[dtl]],
                            tag=f"dm{hf}_{ds}_{Rr}_{j}_{dtl}")
            for dtl in range(4):
                dt = 4 * ds + dtl
                pbuf = dt % 2
                p.emit("scalar", lambda e, dtl=dtl: e.copy(f16_sb, down_ps[dtl][:]),
                       reads=[down_R[dtl]], writes=[nm + "f16"], tag=f"de1{hf}_{dt}")
                p.emit("scalar", lambda e, pbuf=pbuf: e.copy(fsub[pbuf], f16_sb),
                       reads=[nm + "f16"], writes=[nm + f"fsub{pbuf}"], tag=f"de2{hf}_{dt}")
                dpending.append((dt, pbuf))
                if len(dpending) > 1:
                    d0, pb0_ = dpending.pop(0)
                    pln.add_chunk(io, hf, d0, fsub[pb0_], nm + f"fsub{pb0_}",
                                  first=(d0 == 0), last=(d0 == KC - 1))
        for d0, pb0_ in dpending:
            pln.add_chunk(io, hf, d0, fsub[pb0_], nm + f"fsub{pb0_}",
                          first=(d0 == 0), last=(d0 == KC - 1))
        pln.finalize(io, hf)


def emit_contrib(p, cm, nm, io):
    nc = p.nc
    xn = io["cxn"]
    wkv = io["cwkv"]
    wkv_res = io["cwkv_res"]
    kb = io["kvbuf"]
    kb_res = io["kvbuf_res"]
    ps = io["pb"]
    R = io["pb_res"]
    for hf in range(2):
        dst = xn[:].rearrange("p (a s) -> p a s", a=KC)
        srcx = io["x2"][:, HALF * hf:HALF * (hf + 1)].rearrange("(a p) s -> p a s", p=128)
        p.dma("gpsimd", dst, srcx, reads=io["x2_res"], writes=[io["cxn_res"]],
              tag=f"cxn{hf}", role="xs")
        for mt in range(8):
            par = mt % 2
            p.dma("gpsimd", wkv[par], io["wk_t"][mt],
                  writes=[wkv_res[par]], tag=f"cwk{hf}_{mt}", role="wq")
            for kc in range(KC):
                p.emit("tensor", lambda e, kc=kc, par=par: e.matmul(
                    ps[0][:], wkv[par][:, 128 * kc:128 * kc + 128],
                    xn[:, HALF * kc:HALF * (kc + 1)],
                    start=(kc == 0), stop=(kc == KC - 1)),
                    reads=[wkv_res[par], io["cxn_res"]], writes=[R[0]],
                    tag=f"ck{hf}_{mt}_{kc}")
            p.emit("scalar", lambda e, mt=mt, hf=hf: e.copy(
                kb[:, 1024 * (mt % 4) + 0:1024 * (mt % 4) + HALF] if False else
                kb[:, 1024 * (mt % 4) + HALF * hf:1024 * (mt % 4) + HALF * hf + HALF],
                ps[0][:]),
                reads=[R[0]], writes=[kb_res], tag=f"cke{hf}_{mt}")
            if mt % 4 == 3 or True:
                pass
            p.dma("sync", io["contrib"][128 * mt:128 * (mt + 1), HALF * hf:HALF * (hf + 1)],
                  kb[:, 1024 * (mt % 4) + HALF * hf:1024 * (mt % 4) + HALF * hf + HALF],
                  reads=[kb_res], writes=[f"ctr_k{mt}_{hf}"], tag=f"cks{hf}_{mt}",
                  role="st0")
        for mt in range(8):
            par = mt % 2
            p.dma("gpsimd", wkv[par], io["wv_t"][mt],
                  writes=[wkv_res[par]], tag=f"cwv{hf}_{mt}", role="wq")
            for tt in range(4):
                for kc in range(KC):
                    p.emit("tensor", lambda e, kc=kc, par=par, tt=tt: e.matmul(
                        ps[1][:, 0:128], xn[:, HALF * kc + 128 * tt:HALF * kc + 128 * tt + 128],
                        wkv[par][:, 128 * kc:128 * kc + 128],
                        start=(kc == 0), stop=(kc == KC - 1)),
                        reads=[wkv_res[par], io["cxn_res"]], writes=[R[1]],
                        tag=f"cv{hf}_{mt}_{tt}_{kc}")
                p.emit("scalar", lambda e, tt=tt, mt=mt: e.copy(
                    kb[:, 4096 + 1024 * tt + 128 * mt: 4096 + 1024 * tt + 128 * mt + 128],
                    ps[1][:, 0:128]),
                    reads=[R[1]], writes=[kb_res], tag=f"cve{hf}_{mt}_{tt}")
        for tt in range(4):
            gt = 128 * (4 * hf + tt)
            p.dma("sync", io["contrib"][1024 + gt:1024 + gt + 128, :],
                  kb[:, 4096 + 1024 * tt: 4096 + 1024 * tt + 1024],
                  reads=[kb_res], writes=[f"ctr_v{hf}_{tt}"], tag=f"cvs{hf}_{tt}",
                  role="st1")


def emit_moe_layer(p, cm, nm, io):
    nc = p.nc
    xn = io["moe_xn"]
    xn_res = io["moe_xn_res"]
    scr = io["moe_scr"]
    scr_res = io["moe_scr_res"]
    pb = io["pb"]
    R = io["pb_res"]
    pln = io.get("_pln") or PostLN(p, cm, nm, pb[3], pb[7], pb[5], pb[4],
                                   R[3], R[7], R[5], R[4])

    if io.get("wr_view") is not None:
        wr_sb = io["wr_view"]
    else:
        wr_sb = nc.alloc_sbuf_tensor(nm + "_wr", [128, KC * 4], BF16)[:]
    amat = cm.iota0[:].bitcast(BF16)[0:128, 0:4 * KC * 16]
    wq0f = io["ar_wq0"][:].bitcast(F32)
    wq1b = io["ar_wq1"][:]
    wo0f = io["ar_wo0"][:].bitcast(F32)
    wo1f = io["ar_wo1"][:].bitcast(F32)
    RQ0, RQ1, RO0, RO1 = io["ar_res"]
    identf = wq0f[0:128, 64:192]
    mid_sb = wq0f[0:16, 192:704]
    dsub = [wq0f[0:128, 704:1216], wq0f[0:128, 1216:1728]]
    ms_sb = wq1b[0:16, 0:4 * HALF]
    ls_sb = wo0f[0:4, 0:TO]
    g_sb = wo0f[0:4, TO:2 * TO]
    lt_sb = wo1f[0:128, 0:32]
    rowm = wo1f[0:128, 32:40]
    tmp4 = wo1f[0:128, 40:56]
    ge_sb = wo1f[0:1, 56:56 + HALF]

    p.emit("gpsimd", lambda e: e.memset(identf, 1.0), writes=[RQ0], tag="midf0")
    p.emit("gpsimd", lambda e: e.affine_select(
        identf, identf, [[-1, 128]], ALU.is_equal, 0.0, base=0, channel_multiplier=1),
        reads=[RQ0], writes=[RQ0], tag="midf")
    p.dma("gpsimd", wr_sb.rearrange("p (a m) -> p a m", a=KC),
          io["wr_t"].rearrange("(a p) m -> p a m", p=128), writes=[nm + "wr"],
          tag="ldwr", role="wq")
    p.dma("gpsimd", amat, io["a_t"], writes=["iota0"], tag="lda", role="wq")
    bmat = scr[0:16, 0:4 * D]
    p.dma("gpsimd", bmat.rearrange("p (e m) -> p e m", e=4),
          io["b_t"].rearrange("e r m -> r e m"),
          reads=[scr_res], writes=[scr_res], tag="ldb", role="wq")

    if io.get("dbg_ls") is not None:
        pass
    for hf in range(2):
        dst = xn[:].rearrange("p (a s) -> p a s", a=KC)
        srcx = io["x_in"][:, HALF * hf:HALF * (hf + 1)].rearrange("(a p) s -> p a s", p=128)
        p.dma("gpsimd", dst, srcx, reads=io.get("x_in_res", []), writes=[xn_res],
              tag=f"mxn{hf}", role="xs")
        for kc in range(KC):
            p.emit("tensor", lambda e, kc=kc: e.matmul(
                pb[0][0:4, :], wr_sb[:, 4 * kc:4 * kc + 4],
                xn[:, HALF * kc:HALF * (kc + 1)],
                start=(kc == 0), stop=(kc == KC - 1)),
                reads=[nm + "wr", xn_res], writes=[R[0]], tag=f"lg{hf}_{kc}")
        p.emit("scalar", lambda e, hf=hf: e.copy(ls_sb[0:4, HALF * hf:HALF * (hf + 1)],
                                                 pb[0][0:4, :]),
               reads=[R[0]], writes=[RO0], tag=f"lge{hf}")
        for tt in range(4):
            t0 = HALF * hf + 128 * tt
            p.emit("tensor", lambda e, t0=t0: e.transpose(
                pb[1][:].bitcast(F32)[0:128, 0:4], ls_sb[0:4, t0:t0 + 128], identf[0:4, 0:4]),
                reads=[RO0, RQ0], writes=[R[1]], tag=f"gt{hf}_{tt}")
            p.emit("scalar", lambda e, tt=tt: e.copy(lt_sb[:, 4 * tt:4 * tt + 4],
                                                     pb[1][0:128, 0:4]),
                   reads=[R[1]], writes=[RO1], tag=f"gte{hf}_{tt}")
            L4 = lambda tt=tt: lt_sb[:, 4 * tt:4 * tt + 4]
            p.emit("vector", lambda e, tt=tt: e.reduce_max(
                rowm[:, 0:1], lt_sb[:, 4 * tt:4 * tt + 4], AX.X, negate=True),
                reads=[RO1], writes=[RO1], tag=f"m1{hf}_{tt}")
            p.emit("vector", lambda e, tt=tt: e.reduce_max(
                rowm[:, 4:5], lt_sb[:, 4 * tt:4 * tt + 4], AX.X),
                reads=[RO1], writes=[RO1], tag=f"m1p{hf}_{tt}")
            p.emit("vector", lambda e, tt=tt: e.tensor_scalar(
                tmp4[:, 0:4], lt_sb[:, 4 * tt:4 * tt + 4], 0.0, rowm[:, 4:5],
                ALU.add, ALU.is_ge),
                reads=[RO1, RO1], writes=[RO1], tag=f"im{hf}_{tt}")
            p.emit("vector", lambda e, tt=tt: e.scalar_tensor_tensor(
                tmp4[:, 4:8], tmp4[:, 0:4], -1e30, lt_sb[:, 4 * tt:4 * tt + 4],
                ALU.mult, ALU.add),
                reads=[RO1, RO1], writes=[RO1], tag=f"msk{hf}_{tt}")
            p.emit("vector", lambda e: e.reduce_max(
                rowm[:, 1:2], tmp4[:, 4:8], AX.X),
                reads=[RO1], writes=[RO1], tag=f"m2{hf}_{tt}")
            p.emit("vector", lambda e, tt=tt: e.tensor_scalar(
                tmp4[:, 8:12], lt_sb[:, 4 * tt:4 * tt + 4], 0.0, rowm[:, 1:2],
                ALU.add, ALU.is_ge),
                reads=[RO1, RO1], writes=[RO1], tag=f"kp{hf}_{tt}")
            p.emit("scalar", lambda e, tt=tt: e.activation(
                tmp4[:, 12:16], lt_sb[:, 4 * tt:4 * tt + 4], AF.Exp, bias=rowm[:, 0:1]),
                reads=[RO1, RO1], writes=[RO1], tag=f"ex{hf}_{tt}")
            p.emit("vector", lambda e: e.tensor_mul(
                tmp4[:, 12:16], tmp4[:, 12:16], tmp4[:, 8:12]),
                reads=[RO1], writes=[RO1], tag=f"ek{hf}_{tt}")
            p.emit("vector", lambda e: e.reduce_sum(
                rowm[:, 2:3], tmp4[:, 12:16], AX.X),
                reads=[RO1], writes=[RO1], tag=f"sm{hf}_{tt}")
            p.emit("vector", lambda e: e.reciprocal(rowm[:, 3:4], rowm[:, 2:3]),
                   reads=[RO1], writes=[RO1], tag=f"rc{hf}_{tt}")
            p.emit("vector", lambda e: e.tensor_scalar(
                tmp4[:, 12:16], tmp4[:, 12:16], rowm[:, 3:4], 4.0,
                ALU.mult, ALU.mult),
                reads=[RO1, RO1], writes=[RO1], tag=f"gs{hf}_{tt}")
            p.emit("tensor", lambda e: e.transpose(
                pb[1][:].bitcast(F32)[0:4, 0:128], tmp4[:, 12:16], identf),
                reads=[RO1, RQ0], writes=[R[1]], tag=f"gtb{hf}_{tt}")
            p.emit("scalar", lambda e, t0=t0: e.copy(g_sb[0:4, t0:t0 + 128],
                                                     pb[1][0:4, 0:128]),
                   reads=[R[1]], writes=[RO0], tag=f"gbe{hf}_{tt}")
    if io.get("dbg_ls") is not None:
        p.dma("sync", io["dbg_ls"], ls_sb[0:4, :], reads=[RO0], writes=["dbgls"],
              tag="dbgls", role="misc")
        p.dma("sync", io["dbg_g"], g_sb[0:4, :], reads=[RO0], writes=["dbgg"],
              tag="dbgg", role="misc")
    for hf in range(2):
        dst = xn[:].rearrange("p (a s) -> p a s", a=KC)
        srcx = io["x_in"][:, HALF * hf:HALF * (hf + 1)].rearrange("(a p) s -> p a s", p=128)
        p.dma("gpsimd", dst, srcx, reads=io.get("x_in_res", []), writes=[xn_res],
              tag=f"mxn2_{hf}", role="xs")
        for e_ in range(4):
            for kc in range(KC):
                p.emit("tensor", lambda e, e_=e_, kc=kc: e.matmul(
                    pb[0][0:16, :], amat[:, 16 * (KC * e_ + kc):16 * (KC * e_ + kc) + 16],
                    xn[:, HALF * kc:HALF * (kc + 1)],
                    start=(kc == 0), stop=(kc == KC - 1)),
                    reads=["iota0", xn_res], writes=[R[0]], tag=f"md{hf}_{e_}_{kc}")
            p.emit("scalar", lambda e: e.copy(mid_sb, pb[0][0:16, :]),
                   reads=[R[0]], writes=[RQ0], tag=f"mde{hf}_{e_}")
            p.dma("sync", ge_sb, g_sb[e_:e_ + 1, HALF * hf:HALF * (hf + 1)],
                  reads=[RO0], writes=[RO1], tag=f"gex{hf}_{e_}", role="misc")
            p.emit("tensor", lambda e, e_=e_, hf=hf: e.matmul(
                pb[1][0:16, :], cm.ones_row_f32[0:1, 0:16],
                ge_sb, start=True, stop=True),
                reads=["ones_row_f32", RO1], writes=[R[1]], tag=f"gb{hf}_{e_}")
            p.emit("vector", lambda e, e_=e_: e.tensor_mul(
                ms_sb[0:16, HALF * e_:HALF * (e_ + 1)], mid_sb, pb[1][0:16, :]),
                reads=[RQ0, R[1]], writes=[RQ1], tag=f"msx{hf}_{e_}")
        mpending = []
        for dt in range(KC):
            dbank = pb[2] if dt % 2 == 0 else pb[0]
            dR = R[2] if dt % 2 == 0 else R[0]
            for e_ in range(4):
                p.emit("tensor", lambda e, e_=e_, dt=dt, dbank=dbank: e.matmul(
                    dbank[:], bmat[0:16, D * e_ + 128 * dt: D * e_ + 128 * dt + 128],
                    ms_sb[0:16, HALF * e_:HALF * (e_ + 1)],
                    start=(e_ == 0), stop=(e_ == 3)),
                    reads=[scr_res, RQ1], writes=[dR], tag=f"dl{hf}_{dt}_{e_}")
            pbuf = dt % 2
            p.emit("scalar", lambda e, pbuf=pbuf, dbank=dbank: e.copy(dsub[pbuf], dbank[:]),
                   reads=[dR], writes=[RQ0], tag=f"dle{hf}_{dt}")
            mpending.append((dt, pbuf))
            if len(mpending) > 1:
                d0, pb0_ = mpending.pop(0)
                pln.add_chunk(io, hf, d0, dsub[pb0_], RQ0,
                              first=(d0 == 0), last=(d0 == KC - 1))
        for d0, pb0_ in mpending:
            pln.add_chunk(io, hf, d0, dsub[pb0_], RQ0,
                          first=(d0 == 0), last=(d0 == KC - 1))
        pln.finalize(io, hf)


def build_launch3(debug_x3=False):
    nc = bass.Bass("TRN2", target_bir_lowering=False, detect_race_conditions=False)
    x2 = nc.dram_tensor("x2", [D, TO], F32, kind="ExternalInput").ap()
    gathered = nc.dram_tensor("gathered", [2 * TO * NCORE, TO], BF16, kind="ExternalInput").ap()
    wq = nc.dram_tensor("wq", [NH, D, 128], F32, kind="ExternalInput").ap()
    wo = nc.dram_tensor("wo", [KC, D, 128], F32, kind="ExternalInput").ap()
    wr_t = nc.dram_tensor("wr_t", [D, 4], F32, kind="ExternalInput").ap()
    a_t = nc.dram_tensor("a_t", [128, 4 * KC * 16], F32, kind="ExternalInput").ap()
    b_t = nc.dram_tensor("b_t", [4, 16, D], F32, kind="ExternalInput").ap()
    c_in = nc.dram_tensor("c_in", [128, 1], F32, kind="ExternalInput").ap()
    delta_in = nc.dram_tensor("delta_in", [128, 8], F32, kind="ExternalInput").ap()
    out_d = nc.dram_tensor("out_d", [D, TO], F32, kind="ExternalOutput").ap()
    x3_dram = nc.dram_tensor("x3_dram", [D, TO], F32,
                             kind="ExternalOutput" if debug_x3 else "Internal").ap()
    v_dram = nc.dram_tensor("v_dram", [D, TO], F32, kind="Internal").ap()

    p = Prog(nc)
    cm = Common(p, want_l1_masks=True)
    cm.emit_setup(c_in, delta_in)
    io = dict(x_in=x2, res_in=x2, gathered=gathered, gathered_res="__in_g",
              wq=[wq[h] for h in range(NH)], wo=[wo[i] for i in range(KC)],
              v_dram=v_dram, x_out=x3_dram, x_out_res="x3d")
    L = dict(kv_mode="l1", nm="at")
    emit_attention_layer(p, cm, L, io)
    at_xn = None
    dbg_ls = dbg_g = None
    if debug_x3:
        dbg_ls = nc.dram_tensor("dbg_ls", [4, TO], F32, kind="ExternalOutput").ap()
        dbg_g = nc.dram_tensor("dbg_g", [4, TO], F32, kind="ExternalOutput").ap()
    x3res = [f"x3d_{hf}_{dt}" for hf in range(2) for dt in range(KC)]
    moe_io = dict(x_in=x3_dram, x_in_res=x3res, res_in=x3_dram, res_in_res=x3res,
                  dbg_ls=dbg_ls, dbg_g=dbg_g,
                  v_dram=v_dram, x_out=out_d, x_out_res="out",
                  wr_t=wr_t, a_t=a_t, b_t=b_t,
                  moe_xn=io["_xn_t"], moe_xn_res="atxn",
                  moe_scr=io["_attn_t"], moe_scr_res="atattn",
                  ar_wq0=io["_wq"][0], ar_wq1=io["_wq"][1],
                  ar_wo0=io["_wo"][0], ar_wo1=io["_wo"][1],
                  ar_res=["atwq0", "atwq1", "atwo0", "atwo1"],
                  pb=io["_pb"], pb_res=io["_pb_res"], _pln=io["_pln"])
    emit_moe_layer(p, cm, "moe", moe_io)
    p.barrier("sync", [f"out_{hf}_{dt}" for hf in range(2) for dt in range(KC)])
    p.build()
    nc._prog = p
    return nc


def host_prep_l1(inputs):
    wq_l = np.asarray(inputs["L1_attn_wq"], dtype=np.float32)[PI]
    wq_t = np.ascontiguousarray(
        wq_l.reshape(KC, 128, NH, 128).transpose(2, 1, 0, 3)
        .reshape(NH, 128, KC * 128)).astype(BF16_NP)
    wo_l = np.asarray(inputs["L1_attn_wo"], dtype=np.float32)[:, PI]
    wo_t = np.ascontiguousarray(
        wo_l.reshape(NH, 128, KC, 128).transpose(2, 1, 0, 3)
        .reshape(KC, 128, NH * 128)).astype(BF16_NP)
    wk_l = np.asarray(inputs["L1_attn_wk"], dtype=np.float32)[PI]
    wk_t = np.ascontiguousarray(
        wk_l.reshape(KC, 128, 8, 128).transpose(2, 1, 0, 3)
        .reshape(8, 128, KC * 128)).astype(BF16_NP)
    wv_l = np.asarray(inputs["L1_attn_wv"], dtype=np.float32)[PI]
    wv_t = np.ascontiguousarray(
        wv_l.reshape(KC, 128, 8, 128).transpose(2, 1, 0, 3)
        .reshape(8, 128, KC * 128)).astype(BF16_NP)
    wr_t = np.ascontiguousarray(np.asarray(inputs["L1_ffn_Wr"], np.float32).T[PI])
    a_l = np.asarray(inputs["L1_ffn_A"], np.float32).transpose(0, 2, 1)[:, PI, :]
    a_t = np.ascontiguousarray(a_l.reshape(4, KC, 128, 16).transpose(2, 0, 1, 3).reshape(128, 4 * KC * 16))
    b_t = np.ascontiguousarray(np.asarray(inputs["L1_ffn_B"], np.float32)[:, PI, :].transpose(0, 2, 1))
    per_core = []
    for c in range(NCORE):
        delta = np.zeros((128, 8), np.float32)
        for r in range(8):
            delta[:, r] = 0.0 if r <= c else -1.0
        per_core.append(dict(c_in=np.full((128, 1), float(c), dtype=np.float32),
                             delta_in=delta))
    shared = dict(wq=wq_t, wo=wo_t, wr_t=wr_t, a_t=a_t, b_t=b_t)
    return shared, per_core, wk_t, wv_t

def build_launch1():
    nc = bass.Bass("TRN2", target_bir_lowering=False, detect_race_conditions=False)
    xT = nc.dram_tensor("xT", [D, T], F32, kind="ExternalInput").ap()
    xTown = nc.dram_tensor("xTown", [D, TO], F32, kind="ExternalInput").ap()
    wq = nc.dram_tensor("wq", [NH, D, 128], F32, kind="ExternalInput").ap()
    wo = nc.dram_tensor("wo", [KC, D, 128], F32, kind="ExternalInput").ap()
    wk = nc.dram_tensor("wk", [D, 128], F32, kind="ExternalInput").ap()
    wv = nc.dram_tensor("wv", [D, 128], F32, kind="ExternalInput").ap()
    qb = nc.dram_tensor("qb", [128, NH], F32, kind="ExternalInput").ap()
    kb = nc.dram_tensor("kb", [128, 1], F32, kind="ExternalInput").ap()
    vbias = nc.dram_tensor("vbias", [1, 128], F32, kind="ExternalInput").ap()
    ob = nc.dram_tensor("ob", [128, KC], F32, kind="ExternalInput").ap()
    c_in = nc.dram_tensor("c_in", [128, 1], F32, kind="ExternalInput").ap()
    x1_out = nc.dram_tensor("x1_out", [D, TO], F32, kind="ExternalOutput").ap()
    v_dram = nc.dram_tensor("v_dram", [D, TO], F32, kind="Internal").ap()

    p = Prog(nc)
    cm = Common(p, scratch_view=None)
    cm.emit_setup(c_in)
    io = dict(xT=xT, x_in=xTown, res_in=xTown, wq=[wq[h] for h in range(NH)],
              wo=[wo[i] for i in range(KC)], wk=wk, wv=wv, qb=qb, kb=kb,
              vbias=vbias, ob=ob, v_dram=v_dram, x_out=x1_out, x_out_res="x1_out")
    emit_attention_layer(p, cm, dict(kv_mode="l0", nm="l0a"), io)
    p.barrier("sync", [f"x1_out_{hf}_{dt}" for hf in range(2) for dt in range(KC)])
    p.build()
    return nc


def host_prep_l0attn(inputs):
    x = np.asarray(inputs["L0_attn_x"], dtype=np.float32).reshape(T, D)
    xT = np.ascontiguousarray(x.T[PI])
    NS = T // HALF
    xTs = np.ascontiguousarray(
        xT.reshape(KC, 128, NS, HALF).transpose(1, 2, 0, 3)
        .reshape(128, NS * KC * HALF)).astype(BF16_NP)
    wq_l = np.asarray(inputs["L0_attn_wq_w"], dtype=np.float32).T[PI]
    wq_t = np.ascontiguousarray(
        wq_l.reshape(KC, 128, NH, 128).transpose(2, 1, 0, 3)
        .reshape(NH, 128, KC * 128)).astype(BF16_NP)
    wo_l = np.asarray(inputs["L0_attn_wo_w"], dtype=np.float32).T[:, PI]
    wo_t = np.ascontiguousarray(
        wo_l.reshape(NH, 128, KC, 128).transpose(2, 1, 0, 3)
        .reshape(KC, 128, NH * 128)).astype(BF16_NP)
    wk_t = np.ascontiguousarray(
        np.asarray(inputs["L0_attn_wk_w"], dtype=np.float32).T[PI]
        .reshape(KC, 128, 128).transpose(1, 0, 2).reshape(128, KC * 128)).astype(BF16_NP)
    wv_t = np.ascontiguousarray(
        np.asarray(inputs["L0_attn_wv_w"], dtype=np.float32).T[PI]
        .reshape(KC, 128, 128).transpose(1, 0, 2).reshape(128, KC * 128)).astype(BF16_NP)
    qb = np.ascontiguousarray(np.asarray(inputs["L0_attn_wq_b"], np.float32).reshape(NH, 128).T)
    kb = np.asarray(inputs["L0_attn_wk_b"], np.float32).reshape(128, 1).copy()
    vbias = np.asarray(inputs["L0_attn_wv_b"], np.float32).reshape(1, 128).copy()
    ob = np.ascontiguousarray(np.asarray(inputs["L0_attn_wo_b"], np.float32)[PI].reshape(KC, 128).T)
    shared = dict(xT=xTs, wq=wq_t, wo=wo_t, wk=wk_t, wv=wv_t, qb=qb, kb=kb,
                  vbias=vbias, ob=ob)
    per_core = []
    for c in range(NCORE):
        xTown = np.ascontiguousarray(xT[:, c::8])
        xno = np.ascontiguousarray(
            xTown.reshape(KC, 128, 2, HALF).transpose(1, 2, 0, 3)
            .reshape(128, 2 * KC * HALF)).astype(BF16_NP)
        per_core.append(dict(xTown=xTown, xno=xno,
                             c_in=np.full((128, 1), float(c), dtype=np.float32)))
    return shared, per_core



def build_launch2():
    nc = bass.Bass("TRN2", target_bir_lowering=False, detect_race_conditions=False)
    x1 = nc.dram_tensor("x1", [D, TO], F32, kind="ExternalInput").ap()
    qw_g = nc.dram_tensor("qw_g", [512, INTER_PAD], I32, kind="ExternalInput").ap()
    qw_u = nc.dram_tensor("qw_u", [512, INTER_PAD], I32, kind="ExternalInput").ap()
    qw_d = nc.dram_tensor("qw_d", [1376, D], I32, kind="ExternalInput").ap()
    sbc_g = nc.dram_tensor("sbc_g", [4, 128, INTER_PAD], F16, kind="ExternalInput").ap()
    sbc_u = nc.dram_tensor("sbc_u", [4, 128, INTER_PAD], F16, kind="ExternalInput").ap()
    sbc_d = nc.dram_tensor("sbc_d", [11, 128, D], F16, kind="ExternalInput").ap()
    gate_zeros = nc.dram_tensor("gate_zeros", [32, 1376], I32, kind="ExternalInput").ap()
    up_zeros = nc.dram_tensor("up_zeros", [32, 1376], I32, kind="ExternalInput").ap()
    down_zeros = nc.dram_tensor("down_zeros", [86, 512], I32, kind="ExternalInput").ap()
    e8 = nc.dram_tensor("e8", [8, 128], F32, kind="ExternalInput").ap()
    wk_t = nc.dram_tensor("wk_t", [8, D, 128], F32, kind="ExternalInput").ap()
    wv_t = nc.dram_tensor("wv_t", [8, D, 128], F32, kind="ExternalInput").ap()
    x2_out = nc.dram_tensor("x2_out", [D, TO], F32, kind="ExternalOutput").ap()
    contrib_out = nc.dram_tensor("contrib_out", [2 * TO, TO], BF16, kind="ExternalOutput").ap()
    v_dram = nc.dram_tensor("v_dram", [D, TO], F32, kind="Internal").ap()
    z1g = nc.dram_tensor("z1g", [32, INTER_PAD], BF16, kind="Internal").ap()
    z1u = nc.dram_tensor("z1u", [32, INTER_PAD], BF16, kind="Internal").ap()
    z1d = nc.dram_tensor("z1d", [86, D], BF16, kind="Internal").ap()
    zbc_g = nc.dram_tensor("zbc_g", [4, 128, INTER_PAD], BF16, kind="Internal").ap()
    zbc_u = nc.dram_tensor("zbc_u", [4, 128, INTER_PAD], BF16, kind="Internal").ap()
    zbc_d = nc.dram_tensor("zbc_d", [11, 128, D], BF16, kind="Internal").ap()

    p = Prog(nc)
    pb = [nc.alloc_psum_tensor(f"ffn_pb{i}", [128, 512], F32) for i in range(8)]
    inter_arena = nc.alloc_sbuf_tensor("inter_arena", [128, NT_I * HALF], BF16)
    cm = Common(p, want_masks=False, want_ffn=True, scratch_view=inter_arena[:, 0:20992])
    cm.emit_setup_consts()
    io = dict(x_in=x1, res_in=x1, x_out=x2_out, x_out_res="x2_out", v_dram=v_dram,
              inter_t=inter_arena, pb=pb, pb_res=[f"ffn_pb{i}" for i in range(8)],
              qw_g=qw_g, qw_u=qw_u, qw_d=qw_d,
              sbc_g=[sbc_g[i] for i in range(4)], sbc_u=[sbc_u[i] for i in range(4)],
              sbc_d=[sbc_d[i] for i in range(11)],
              gate_zeros=gate_zeros, up_zeros=up_zeros, down_zeros=down_zeros,
              e8=e8, z1g=z1g, z1u=z1u, z1d=z1d,
              zbc_g=[zbc_g[i] for i in range(4)], zbc_u=[zbc_u[i] for i in range(4)],
              zbc_d=[zbc_d[i] for i in range(11)])
    emit_ffn_setup(p, cm, io)
    emit_ffn_layer(p, cm, dict(nm="l0f"), io)
    cio = dict(x2=x2_out, x2_res=[f"x2_out_{hf}_{dt}" for hf in range(2) for dt in range(KC)],
               wk_t=[wk_t[i] for i in range(8)], wv_t=[wv_t[i] for i in range(8)],
               contrib=contrib_out, kvbuf=inter_arena[:, 0:8192], kvbuf_res="l0finter",
               cxn=io["_ffn_xn"], cxn_res=io["_ffn_xn_res"],
               cwkv=[inter_arena[:, 8192 + 4096 * i:8192 + 4096 * (i + 1)] for i in range(2)],
               cwkv_res=["l0finter", "l0finter"],
               pb=pb, pb_res=[f"ffn_pb{i}" for i in range(8)])
    emit_contrib(p, cm, "ctr", cio)
    p.barrier("sync", [f"x2_out_{hf}_{dt}" for hf in range(2) for dt in range(KC)]
              + [f"ctr_k{mt}_{hf}" for mt in range(8) for hf in range(2)]
              + [f"ctr_v{hf}_{tt}" for hf in range(2) for tt in range(4)])
    p.build()
    return nc


def host_prep_l0ffn(inputs):
    gq = np.asarray(inputs["L0_ffn_gate_qweight"])
    uq = np.asarray(inputs["L0_ffn_up_qweight"])
    dq = np.asarray(inputs["L0_ffn_down_qweight"])
    gs = np.asarray(inputs["L0_ffn_gate_scales"])
    us = np.asarray(inputs["L0_ffn_up_scales"])
    dsc = np.asarray(inputs["L0_ffn_down_scales"])
    gz = np.asarray(inputs["L0_ffn_gate_zeros"])
    uz = np.asarray(inputs["L0_ffn_up_zeros"])
    dz = np.asarray(inputs["L0_ffn_down_zeros"])
    qw_g = np.ascontiguousarray(gq[:, PIP])
    qw_u = np.ascontiguousarray(uq[:, PIP])
    qw_d = np.ascontiguousarray(dq[:, PI])
    def sbc_table(s, RC, W, perm):
        sp = s[:, perm]
        out = np.empty((RC, 128, W), dtype=s.dtype)
        for R in range(RC):
            g0 = 8 * R
            gr = min(8, sp.shape[0] - g0)
            blk = np.repeat(sp[g0:g0 + gr], 16, axis=0)
            out[R, :16 * gr] = blk
            if 16 * gr < 128:
                out[R, 16 * gr:] = 0
        return np.ascontiguousarray(out)
    sbc_g = sbc_table(gs, 4, INTER_PAD, PIP)
    sbc_u = sbc_table(us, 4, INTER_PAD, PIP)
    sbc_d = sbc_table(dsc, 11, D, PI)
    e8 = np.zeros((8, 128), dtype=np.float32)
    for g in range(8):
        e8[g, 16 * g:16 * g + 16] = 1.0
    shared = dict(qw_g=qw_g, qw_u=qw_u, qw_d=qw_d, sbc_g=sbc_g, sbc_u=sbc_u,
                  sbc_d=sbc_d, gate_zeros=gz, up_zeros=uz, down_zeros=dz, e8=e8)
    return shared




def build_fused(stop_after=None, no_ea_qbf=False, dbg=False):
    nc = bass.Bass("TRN2", target_bir_lowering=False, detect_race_conditions=False)
    xT = nc.dram_tensor("xT", [128, (T // HALF) * KC * HALF], BF16, kind="ExternalInput").ap()
    xTown = nc.dram_tensor("xTown", [D, TO], F32, kind="ExternalInput").ap()
    xno = nc.dram_tensor("xno", [128, 2 * KC * HALF], BF16, kind="ExternalInput").ap()
    wq0 = nc.dram_tensor("wq0", [NH, 128, KC * 128], BF16, kind="ExternalInput").ap()
    wo0 = nc.dram_tensor("wo0", [KC, 128, NH * 128], BF16, kind="ExternalInput").ap()
    wk0 = nc.dram_tensor("wk0", [128, KC * 128], BF16, kind="ExternalInput").ap()
    wv0 = nc.dram_tensor("wv0", [128, KC * 128], BF16, kind="ExternalInput").ap()
    qb = nc.dram_tensor("qb", [128, NH], F32, kind="ExternalInput").ap()
    kb = nc.dram_tensor("kb", [128, 1], F32, kind="ExternalInput").ap()
    vbias = nc.dram_tensor("vbias", [1, 128], F32, kind="ExternalInput").ap()
    ob = nc.dram_tensor("ob", [128, KC], F32, kind="ExternalInput").ap()
    c_in = nc.dram_tensor("c_in", [128, 1], F32, kind="ExternalInput").ap()
    delta_in = nc.dram_tensor("delta_in", [128, 8], F32, kind="ExternalInput").ap()
    qw_g = nc.dram_tensor("qw_g", [512, INTER_PAD], I32, kind="ExternalInput").ap()
    qw_u = nc.dram_tensor("qw_u", [512, INTER_PAD], I32, kind="ExternalInput").ap()
    qw_d = nc.dram_tensor("qw_d", [1376, D], I32, kind="ExternalInput").ap()
    sbc_g = nc.dram_tensor("sbc_g", [4, 128, INTER_PAD], F16, kind="ExternalInput").ap()
    sbc_u = nc.dram_tensor("sbc_u", [4, 128, INTER_PAD], F16, kind="ExternalInput").ap()
    sbc_d = nc.dram_tensor("sbc_d", [11, 128, D], F16, kind="ExternalInput").ap()
    gate_zeros = nc.dram_tensor("gate_zeros", [32, 1376], I32, kind="ExternalInput").ap()
    up_zeros = nc.dram_tensor("up_zeros", [32, 1376], I32, kind="ExternalInput").ap()
    down_zeros = nc.dram_tensor("down_zeros", [86, 512], I32, kind="ExternalInput").ap()
    e8 = nc.dram_tensor("e8", [8, 128], F32, kind="ExternalInput").ap()
    wk_t = nc.dram_tensor("wk_t", [8, 128, KC * 128], BF16, kind="ExternalInput").ap()
    wv_t = nc.dram_tensor("wv_t", [8, 128, KC * 128], BF16, kind="ExternalInput").ap()
    wq1 = nc.dram_tensor("wq1", [NH, 128, KC * 128], BF16, kind="ExternalInput").ap()
    wo1 = nc.dram_tensor("wo1", [KC, 128, NH * 128], BF16, kind="ExternalInput").ap()
    wr_t = nc.dram_tensor("wr_t", [D, 4], F32, kind="ExternalInput").ap()
    a_t = nc.dram_tensor("a_t", [128, 4 * KC * 16], F32, kind="ExternalInput").ap()
    b_t = nc.dram_tensor("b_t", [4, 16, D], F32, kind="ExternalInput").ap()
    out_d = nc.dram_tensor("out_d", [D, TO], F32, kind="ExternalOutput").ap()
    KI = "ExternalOutput" if dbg else "Internal"
    dbg_inter = nc.dram_tensor("dbg_inter", [128, 4096], BF16, kind=KI).ap()
    dbg_ps = nc.dram_tensor("dbg_ps", [128, 512], F32, kind=KI).ap()
    dbg_ps2 = nc.dram_tensor("dbg_ps2", [128, 512], F32, kind=KI).ap()
    dbg_wg = nc.dram_tensor("dbg_wg", [128, 512], BF16, kind=KI).ap()
    dbg_sbc = nc.dram_tensor("dbg_sbc", [128, 512], BF16, kind=KI).ap()
    dbg_zbc = nc.dram_tensor("dbg_zbc", [128, 512], BF16, kind=KI).ap()
    dbg_xn = nc.dram_tensor("dbg_xn", [128, 4096], BF16, kind=KI).ap()
    x1_dram = nc.dram_tensor("x1_dram", [D, TO], F32, kind=KI).ap()
    x2_dram = nc.dram_tensor("x2_dram", [D, TO], F32, kind=KI).ap()
    x3_dram = nc.dram_tensor("x3_dram", [D, TO], F32, kind="Internal").ap()
    v_dram = nc.dram_tensor("v_dram", [D, TO], F32, kind="Internal").ap()
    z1g = nc.dram_tensor("z1g", [32, INTER_PAD], BF16, kind="Internal").ap()
    z1u = nc.dram_tensor("z1u", [32, INTER_PAD], BF16, kind="Internal").ap()
    z1d = nc.dram_tensor("z1d", [86, D], BF16, kind="Internal").ap()
    zbc_g = nc.dram_tensor("zbc_g", [4, 128, INTER_PAD], BF16, kind="Internal").ap()
    zbc_u = nc.dram_tensor("zbc_u", [4, 128, INTER_PAD], BF16, kind="Internal").ap()
    zbc_d = nc.dram_tensor("zbc_d", [11, 128, D], BF16, kind="Internal").ap()
    contrib_d = nc.dram_tensor("contrib_d", [2 * TO, TO], BF16, kind="Internal").ap()
    gathered_d = nc.dram_tensor("gathered_d", [2 * TO * NCORE, TO], BF16, kind="Internal").ap()

    p = Prog(nc)
    pb = [nc.alloc_psum_tensor(f"pb{i}", [128, 512], F32) for i in range(8)]
    pb_res = [f"pb{i}" for i in range(8)]
    AR1 = nc.alloc_sbuf_tensor("AR1", [128, 16384], BF16)
    AR2 = nc.alloc_sbuf_tensor("AR2", [128, 45056], BF16)
    WO1 = nc.alloc_sbuf_tensor("WO1", [128, 4096], BF16)
    EA = nc.alloc_sbuf_tensor("EA", [128, 18432], BF16)
    arena = dict(
        xn=AR1[:],
        attn=AR2[:, 0:16384],
        k=AR2[:, 16384:24576],
        v=AR2[:, 24576:32768],
        wq=[AR2[:, 32768:36864], AR2[:, 36864:40960]],
        wo=[AR2[:, 40960:45056], WO1[:]],
    )
    at_res = ["atxn", "atattn", "atk", "atv", "atwq0", "atwq1", "atwo0", "atwo1",
              "atwkv"] + [f"atk{r}" for r in range(8)] + [f"atv{r}" for r in range(8)]
    ar2_res = [r for r in at_res if r not in ("atxn", "atwo1", "atwkv")]

    cm = Common(p, want_l1_masks=True, want_masks=True, want_ffn=True,
                scratch_view=AR2[:, 0:20992])
    cm.emit_setup(c_in, delta_in)

    ffn_io = dict(x_in=x1_dram, res_in=x1_dram,
                  x_in_res=[f"x1_{hf}_{dt}" for hf in range(2) for dt in range(KC)],
                  res_in_res=[f"x1_{hf}_{dt}" for hf in range(2) for dt in range(KC)],
                  x_out=x2_dram, x_out_res="x2", v_dram=v_dram,
                  qw_g=qw_g, qw_u=qw_u, qw_d=qw_d,
                  sbc_g=[sbc_g[i] for i in range(4)], sbc_u=[sbc_u[i] for i in range(4)],
                  sbc_d=[sbc_d[i] for i in range(11)],
                  gate_zeros=gate_zeros, up_zeros=up_zeros, down_zeros=down_zeros,
                  e8=e8, z1g=z1g, z1u=z1u, z1d=z1d,
                  zbc_g=[zbc_g[i] for i in range(4)], zbc_u=[zbc_u[i] for i in range(4)],
                  zbc_d=[zbc_d[i] for i in range(11)],
                  pb=pb, pb_res=pb_res,
                  xn_t=AR1[:], xn_res="atxn",
                  inter_t=AR2[:],
                  inter_claims=ar2_res + ["scratch", "stage"],
                  ea=EA[:], dbg_inter=dbg_inter, dbg_xn=dbg_xn, dbg_ps=dbg_ps,
                  dbg_ps2=dbg_ps2,
                  dbg_wg=dbg_wg, dbg_sbc=dbg_sbc, dbg_zbc=dbg_zbc)
    emit_ffn_setup(p, cm, ffn_io)

    p.claim(reads=[], writes=ar2_res + ["scratch", "stage"], tag="cl_setup_at")

    io0 = dict(xT=xT, x_in=xTown, x_in_c=xno, res_in=xTown,
               wq=[wq0[h] for h in range(NH)], wo=[wo0[i] for i in range(KC)],
               wk=wk0, wv=wv0, qb=qb, kb=kb, vbias=vbias, ob=ob,
               v_dram=v_dram, x_out=x1_dram, x_out_res="x1",
               arena=arena, pb=pb, pb_res=pb_res, wkv_t=EA[:, 0:8192],
               proj_t=[EA[:].bitcast(F32)[:, 7936 + 512 * i:8448 + 512 * i] for i in range(2)],
               proj_res=["l0ffsub0", "l0ffsub1"])
    emit_attention_layer(p, cm, dict(kv_mode="l0", nm="at"), io0)
    pln = io0["_pln"]

    p.claim(reads=[], writes=["atwkv", "l0fqwg0", "l0fqwg1", "l0fqwu0", "l0fqwu1",
                              "l0fqwd0", "l0fqwd1", "l0fqig0", "l0fqig1"],
            tag="cl_wkv_ea")
    ffn_io["_pln"] = pln
    zrow = p.buf("at_rec", [1, HALF], F32)
    p.emit("gpsimd", lambda e: e.memset(zrow[:], 0.0), writes=["zrow_s", "atrec"],
           tag="zrow_s")
    for i in range(8):
        p.emit("tensor", lambda e, i=i: e.matmul(
            pb[i][:], cm.ones_row_f32[:], zrow[0:1, 0:512], start=True, stop=True),
            reads=["ones_row_f32", "zrow_s", "atrec"], writes=[pb_res[i]], tag=f"scrub{i}")
    if no_ea_qbf:
        ffn_io["no_ea_qbf"] = True
    import os
    if os.environ.get("SWAP_BANKS"):
        ffn_io["swap_banks"] = True
    if os.environ.get("NT_REV"):
        ffn_io["nt_order"] = [3, 2, 1, 0]
    emit_ffn_layer(p, cm, dict(nm="l0f"), ffn_io)
    if stop_after == "ffn":
        p.barrier("sync", [f"x2_{hf}_{dt}" for hf in range(2) for dt in range(KC)])
        p.build()
        nc._prog = p
        return nc

    x2res = [f"x2_{hf}_{dt}" for hf in range(2) for dt in range(KC)]
    cio = dict(x2=x2_dram, x2_res=x2res,
               wk_t=[wk_t[i] for i in range(8)], wv_t=[wv_t[i] for i in range(8)],
               contrib=contrib_d, kvbuf=AR2[:, 0:8192], kvbuf_res="l0finter",
               cxn=AR1[:], cxn_res="atxn",
               cwkv=[AR2[:, 8192 + 4096 * i:8192 + 4096 * (i + 1)] for i in range(2)],
               cwkv_res=["l0finter", "l0finter"],
               pb=pb, pb_res=pb_res)
    emit_contrib(p, cm, "ctr", cio)
    ctr_res = [f"ctr_k{mt}_{hf}" for mt in range(8) for hf in range(2)] + \
              [f"ctr_v{hf}_{tt}" for hf in range(2) for tt in range(4)]
    p.emit("gpsimd", lambda e: e.collective_compute(
        "AllGather", ALU.bypass, replica_groups=[list(range(NCORE))],
        ins=[contrib_d.opt()], outs=[gathered_d.opt()]),
        reads=ctr_res, writes=["gathered"], tag="allgather")

    p.claim(reads=[], writes=ar2_res + ["l0finter"], tag="cl_ffn_at")

    io1 = dict(x_in=x2_dram, x_in_res=x2res, res_in=x2_dram, res_in_res=x2res,
               gathered=gathered_d, gathered_res="gathered",
               wq=[wq1[h] for h in range(NH)], wo=[wo1[i] for i in range(KC)],
               v_dram=v_dram, x_out=x3_dram, x_out_res="x3",
               arena=arena, pb=pb, pb_res=pb_res, _pln=pln,
               proj_t=[EA[:].bitcast(F32)[:, 7936 + 512 * i:8448 + 512 * i] for i in range(2)],
               proj_res=["l0ffsub0", "l0ffsub1"])
    emit_attention_layer(p, cm, dict(kv_mode="l1", nm="at"), io1)

    x3res = [f"x3_{hf}_{dt}" for hf in range(2) for dt in range(KC)]
    moe_io = dict(x_in=x3_dram, x_in_res=x3res, res_in=x3_dram, res_in_res=x3res,
                  v_dram=v_dram, x_out=out_d, x_out_res="out",
                  wr_t=wr_t, a_t=a_t, b_t=b_t,
                  moe_xn=AR1, moe_xn_res="atxn",
                  moe_scr=AR2[:, 0:16384], moe_scr_res="atattn",
                  ar_wq0=arena["wq"][0], ar_wq1=arena["wq"][1],
                  ar_wo0=arena["wo"][0], ar_wo1=arena["wo"][1],
                  ar_res=["atwq0", "atwq1", "atwo0", "atwo1"],
                  wr_view=EA[:, 15360:15488],
                  pb=pb, pb_res=pb_res, _pln=pln)
    emit_moe_layer(p, cm, "moe", moe_io)

    p.barrier("sync", [f"out_{hf}_{dt}" for hf in range(2) for dt in range(KC)])
    p.build()
    nc._prog = p
    return nc

def _reassemble(outs):
    out = np.empty((T, D), dtype=np.float32)
    for c in range(NCORE):
        nat = np.empty((D, TO), np.float32)
        nat[PI] = outs[c]
        out[c::8, :] = nat.T
    return out.reshape(1, T, D)


KERNEL_MODE = "fused"


def kernel(**inputs):
    if KERNEL_MODE == "fused":
        return kernel_fused(**inputs)
    return kernel_3launch(**inputs)


def kernel_fused(**inputs):
    shared1, per_core1 = host_prep_l0attn(inputs)
    shared2 = host_prep_l0ffn(inputs)
    shared3, per_core3, wk_t, wv_t = host_prep_l1(inputs)
    nc = build_fused()
    in_maps = []
    for c in range(NCORE):
        m = dict(shared2, wk_t=wk_t, wv_t=wv_t)
        m.update(xT=shared1["xT"], wq0=shared1["wq"], wo0=shared1["wo"],
                 wk0=shared1["wk"], wv0=shared1["wv"], qb=shared1["qb"],
                 kb=shared1["kb"], vbias=shared1["vbias"], ob=shared1["ob"],
                 wq1=shared3["wq"], wo1=shared3["wo"], wr_t=shared3["wr_t"],
                 a_t=shared3["a_t"], b_t=shared3["b_t"])
        m.update(per_core1[c])
        m.update(per_core3[c])
        in_maps.append(m)
    res = run_bass_kernel_spmd(nc, in_maps, core_ids=list(range(NCORE)))
    return _reassemble([res.results[c]["out_d"] for c in range(NCORE)])


def kernel_3launch(**inputs):
    shared1, per_core1 = host_prep_l0attn(inputs)
    shared2 = host_prep_l0ffn(inputs)
    shared3, per_core3, wk_t, wv_t = host_prep_l1(inputs)

    nc1 = build_launch1()
    in_maps = [dict(shared1, **per_core1[c]) for c in range(NCORE)]
    res1 = run_bass_kernel_spmd(nc1, in_maps, core_ids=list(range(NCORE)))
    x1s = [res1.results[c]["x1_out"] for c in range(NCORE)]

    nc2 = build_launch2()
    in_maps = [dict(shared2, x1=x1s[c], wk_t=wk_t, wv_t=wv_t) for c in range(NCORE)]
    res2 = run_bass_kernel_spmd(nc2, in_maps, core_ids=list(range(NCORE)))
    x2s = [res2.results[c]["x2_out"] for c in range(NCORE)]
    gathered = np.concatenate([res2.results[c]["contrib_out"] for c in range(NCORE)],
                              axis=0)

    nc3 = build_launch3()
    in_maps = [dict(shared3, **per_core3[c], gathered=gathered, x2=x2s[c])
               for c in range(NCORE)]
    res3 = run_bass_kernel_spmd(nc3, in_maps, core_ids=list(range(NCORE)))
    return _reassemble([res3.results[c]["out_d"] for c in range(NCORE)])


# version 11
# speedup vs baseline: 1.7926x; 1.0005x over previous
import numpy as np
import ml_dtypes
import concourse.bass as bass
import concourse.mybir as mybir
from concourse.bass_utils import run_bass_kernel_spmd

BF16_NP = ml_dtypes.bfloat16

F32 = mybir.dt.float32
BF16 = mybir.dt.bfloat16
F16 = mybir.dt.float16
I32 = mybir.dt.int32
AF = mybir.ActivationFunctionType
ALU = mybir.AluOpType
AX = mybir.AxisListType

D = 4096
T = 8192
NCORE = 8
TO = T // NCORE
HALF = 512
NH = 32
HD = 128
KC = D // 128
INTER = 11008
INTER_PAD = 11264
NT_I = INTER_PAD // 128
SM_SCALE = float(128 ** -0.5)


def _pi_perm():
    pos = np.arange(D)
    tau, p = pos // 128, pos % 128
    R, j = tau // 8, tau % 8
    return (1024 * R + 8 * p + j).astype(np.int64)

def _pip_perm():
    pos = np.arange(INTER_PAD)
    tau, p = pos // 128, pos % 128
    Rp, j = tau // 8, tau % 8
    n = 8 * (128 * Rp + p) + j
    valid = n < INTER
    return np.where(valid, n, 0).astype(np.int64), valid

PI = _pi_perm()
PIP, PIP_VALID = _pip_perm()


class Ev:
    __slots__ = ("idx", "engine", "fn", "reads", "writes", "is_dma", "deps",
                 "sem", "incval", "tokval", "tag", "role", "alldeps")

class Prog:
    def __init__(self, nc):
        self.nc = nc
        self.events = []
        self.res = {}
        self.sems = {}
        for en in ("tensor", "vector", "scalar", "gpsimd", "sync"):
            self.sems[(en, False)] = nc.alloc_semaphore("sc_" + en)
        self.dma_sems = {}
        self.dma_last = {}

    def dma_sem(self, role):
        if role not in self.dma_sems:
            self.dma_sems[role] = self.nc.alloc_semaphore("sd_" + role)
        return self.dma_sems[role]

    def emit(self, engine, fn, reads=(), writes=(), deps=(), is_dma=False, tag="",
             role=None):
        ev = Ev()
        ev.idx = len(self.events)
        ev.engine = engine
        ev.fn = fn
        ev.reads = list(reads)
        ev.writes = list(writes)
        ev.is_dma = is_dma
        ev.deps = [d for d in deps if d is not None]
        ev.sem = None
        ev.tag = tag
        if is_dma:
            base = role or engine
            if not hasattr(self, "role_rr"):
                self.role_rr = {}
            depth = 6 if base in ("sbc", "zbc", "qw") else 3
            n = self.role_rr[base] = (self.role_rr.get(base, -1) + 1) % depth
            ev.role = f"{base}.{n}"
            prev = self.dma_last.get(ev.role)
            if prev is not None:
                ev.deps.append(prev)
            self.dma_last[ev.role] = ev
        for r in ev.reads:
            st = self.res.get(r)
            if st is None:
                continue
            if st["writer"] is not None:
                ev.deps.append(st["writer"])
            st["readers"].append(ev)
        for w in ev.writes:
            st = self.res.setdefault(w, {"writer": None, "readers": []})
            if st["writer"] is not None:
                ev.deps.append(st["writer"])
            for rd in st["readers"]:
                ev.deps.append(rd)
            st["writer"] = ev
            st["readers"] = []
        self.events.append(ev)
        return ev

    def dma(self, engine, dst, src, reads=(), writes=(), deps=(), tag="", role=None, **kw):
        return self.emit(engine, lambda e: e.dma_start(dst, src, **kw),
                         reads=reads, writes=writes, deps=deps, is_dma=True, tag=tag,
                         role=role)

    def barrier(self, engine, reads, tag="end"):
        return self.emit(engine, lambda e: None, reads=reads, tag=tag)

    def claim(self, reads, writes, tag="claim"):
        return self.emit("gpsimd", lambda e: None, reads=reads, writes=writes, tag=tag)

    def buf(self, name, shape, dtype):
        if not hasattr(self, "_bufs"):
            self._bufs = {}
        if name not in self._bufs:
            self._bufs[name] = self.nc.alloc_sbuf_tensor(name, shape, dtype)
        return self._bufs[name]

    def build(self):
        for ev in self.events:
            ev.deps = [d for d in ev.deps if d is not ev]
            ev.alldeps = list(ev.deps)
        needed = set()
        for ev in self.events:
            kept = []
            seen = set()
            for d in ev.deps:
                if d.idx in seen:
                    continue
                seen.add(d.idx)
                if (d.engine == ev.engine) and not d.is_dma:
                    if ev.engine == "tensor" and d.engine == "tensor":
                        continue
                kept.append(d)
                needed.add(d.idx)
            ev.deps = kept
        counts = {}
        for ev in self.events:
            if ev.is_dma:
                ev.sem = self.dma_sem(ev.role)
            elif ev.idx in needed:
                ev.sem = self.sems[(ev.engine, False)]
            if ev.sem is not None:
                ev.incval = 16 if ev.is_dma else 1
                counts[ev.sem] = counts.get(ev.sem, 0) + ev.incval
                ev.tokval = counts[ev.sem]
        events = self.events
        with self.nc.Block() as block:
            def make(engine_name):
                def body(eng):
                    last_wait = {}
                    for ev in events:
                        if ev.engine != engine_name:
                            continue
                        wmax = {}
                        for d in ev.deps:
                            assert d.sem is not None, f"dep without token: {d.tag} <- {ev.tag}"
                            if wmax.get(d.sem, -1) < d.tokval:
                                wmax[d.sem] = d.tokval
                        for sem, val in wmax.items():
                            if last_wait.get(sem, -1) >= val:
                                continue
                            eng.wait_ge(sem, val)
                            last_wait[sem] = val
                        ins = ev.fn(eng)
                        if ev.sem is not None:
                            if ins is None:
                                ins = eng.nop()
                            ins.then_inc(ev.sem, ev.incval)
                return body
            block.tensor(make("tensor"))
            block.vector(make("vector"))
            block.scalar(make("scalar"))
            block.gpsimd(make("gpsimd"))
            block.sync(make("sync"))


class Common:
    def __init__(self, p, want_l1_masks=False, want_masks=True, want_ffn=False,
                 scratch_view=None):
        nc = p.nc
        self.p = p
        self.ones_col_bf = nc.alloc_sbuf_tensor("ones_col_bf", [128, 1], BF16)
        self.ones_row_bf = nc.alloc_sbuf_tensor("ones_row_bf", [1, 128], BF16)
        self.ones_row_f32 = nc.alloc_sbuf_tensor("ones_row_f32", [1, 128], F32)
        self.want_masks = want_masks
        if want_masks:
            self.c_sb = nc.alloc_sbuf_tensor("c_sb", [128, 1], F32)
            self.mask0 = nc.alloc_sbuf_tensor("mask0", [128, 2032], BF16)
            self.iota0 = nc.alloc_sbuf_tensor("iota0", [128, 2032], F32)
        self.want_l1_masks = want_l1_masks
        if want_l1_masks:
            self.delta_sb = nc.alloc_sbuf_tensor("delta_sb", [128, 8], F32)
            self.mask1 = nc.alloc_sbuf_tensor("mask1", [128, 8 * 128], BF16)
        self.want_ffn = want_ffn
        if want_ffn:
            self.e8_f32 = nc.alloc_sbuf_tensor("e8_f32", [8, 128], F32)
            if scratch_view is not None:
                self.scratch = scratch_view
            else:
                self.scratch = nc.alloc_sbuf_tensor("cm_scratch", [128, 20480], BF16)[:]

    def emit_setup_consts(self):
        p = self.p
        p.emit("gpsimd", lambda e: e.memset(self.ones_col_bf[:], 1.0),
               writes=["ones_col"], tag="ones_col")
        p.emit("gpsimd", lambda e: e.memset(self.ones_row_bf[:], 1.0),
               writes=["ones_row_bf"], tag="ones_row_bf")
        p.emit("gpsimd", lambda e: e.memset(self.ones_row_f32[:], 1.0),
               writes=["ones_row_f32"], tag="ones_row_f32")

    def emit_setup(self, c_in, delta_in=None):
        p = self.p
        self.emit_setup_consts()
        p.dma("sync", self.c_sb[:], c_in, writes=["c_sb"], tag="ld_c")
        p.emit("gpsimd", lambda e: e.iota(
            self.iota0[:], [[-8, 2032]], base=4096, channel_multiplier=1,
            allow_small_or_imprecise_dtypes=True), writes=["iota0"], tag="iota0")
        p.emit("vector", lambda e: e.tensor_scalar(
            self.mask0[:], self.iota0[:], 0.0, self.c_sb[:], ALU.add, ALU.is_le),
            reads=["iota0", "c_sb"], writes=["mask0"], tag="mask0")
        if self.want_l1_masks:
            p.dma("sync", self.delta_sb[:], delta_in, writes=["delta"], tag="ld_delta")
            p.emit("gpsimd", lambda e: e.iota(
                self.iota0[:, 0:128], [[-1, 128]], base=0, channel_multiplier=1,
                allow_small_or_imprecise_dtypes=True),
                reads=["mask0"], writes=["iota0"], tag="iota1")
            for r in range(8):
                p.emit("vector", lambda e, r=r: e.tensor_scalar(
                    self.mask1[:, 128 * r:128 * r + 128], self.iota0[:, 0:128],
                    0.0, self.delta_sb[:, r:r + 1], ALU.add, ALU.is_le),
                    reads=["iota0", "delta"], writes=["mask1"], tag=f"mask1_{r}")



class PostLN:
    def __init__(self, p, cm, nm, ps_sv, ps_svv, ps_a, ps_b, rsv, rsvv, rsa, rsb):
        nc = p.nc
        self.p, self.cm, self.nm = p, cm, nm
        self.ps_sv, self.ps_svv, self.ps_a, self.ps_b = ps_sv, ps_svv, ps_a, ps_b
        self.rsv, self.rsvv, self.rsa, self.rsb = rsv, rsvv, rsa, rsb
        self.xres = [nc.alloc_sbuf_tensor(nm + f"_lnxr{i}", [128, HALF], F32) for i in range(2)]
        self.vsum = [nc.alloc_sbuf_tensor(nm + f"_lnvs{i}", [128, HALF], F32) for i in range(2)]
        self.vb = nc.alloc_sbuf_tensor(nm + "_lnvb", [128, HALF], BF16)
        self.sq = nc.alloc_sbuf_tensor(nm + "_lnsq", [128, HALF], BF16)
        self.abc = nc.alloc_sbuf_tensor(nm + "_lnabc", [128, 2 * HALF], F32)
        self.xnew = [nc.alloc_sbuf_tensor(nm + f"_lnxn{i}", [128, HALF], F32) for i in range(2)]

    def add_chunk(self, io, hf, dt, src_ap, src_res, first, last):
        p, cm, nm = self.p, self.cm, self.nm
        vpre = io["x_out_res"] + "_v"
        pbuf = dt % 2
        rr = io.get("res_in_res", "__none__")
        rr = rr if isinstance(rr, list) else [rr]
        p.dma("sync", self.xres[pbuf][:],
              io["res_in"][128 * dt:128 * (dt + 1), HALF * hf:HALF * (hf + 1)],
              reads=rr, writes=[nm + f"lnxr{pbuf}"],
              tag=f"ldxr{hf}_{dt}", role=f"xres{pbuf}")
        p.emit("gpsimd", lambda e: e.tensor_add(
            self.vsum[pbuf][:], src_ap, self.xres[pbuf][:]),
            reads=[src_res, nm + f"lnxr{pbuf}"],
            writes=[nm + f"lnvs{pbuf}"], tag=f"radd{hf}_{dt}")
        p.dma("sync", io["v_dram"][128 * dt:128 * (dt + 1), HALF * hf:HALF * (hf + 1)],
              self.vsum[pbuf][:], reads=[nm + f"lnvs{pbuf}"], writes=[vpre + f"{hf}_{dt}"],
              tag=f"stv{hf}_{dt}", role=f"st{pbuf}")
        p.emit("vector", lambda e: e.tensor_copy(self.vb[:], self.vsum[pbuf][:]),
               reads=[nm + f"lnvs{pbuf}"], writes=[nm + "lnvb"], tag=f"vb{hf}_{dt}")
        p.emit("scalar", lambda e: e.activation(self.sq[:], self.vsum[pbuf][:], AF.Square),
               reads=[nm + f"lnvs{pbuf}"], writes=[nm + "lnsq"], tag=f"sq{hf}_{dt}")
        p.emit("tensor", lambda e, first=first, last=last: e.matmul(
            self.ps_sv[0:1, :], cm.ones_col_bf[:], self.vb[:], start=first, stop=last),
            reads=["ones_col", nm + "lnvb"], writes=[self.rsv], tag=f"sv{hf}_{dt}")
        p.emit("tensor", lambda e, first=first, last=last: e.matmul(
            self.ps_svv[0:1, :], cm.ones_col_bf[:], self.sq[:], start=first, stop=last),
            reads=["ones_col", nm + "lnsq"], writes=[self.rsvv], tag=f"svv{hf}_{dt}")

    def finalize(self, io, hf):
        p, cm, nm = self.p, self.cm, self.nm
        vpre = io["x_out_res"] + "_v"
        R0 = self.xnew[0][0:1, :]
        R1 = self.xnew[0][64:65, :]
        R2 = self.xnew[1][0:1, :]
        R3 = self.xres[0][0:1, :]
        R4 = self.xnew[1][64:65, :]
        rn = [nm + "lnxn0", nm + "lnxn1", nm + "lnxr0"]
        p.emit("vector", lambda e: e.tensor_scalar_mul(R0, self.ps_sv[0:1, :], 1.0 / D),
               reads=[self.rsv] + rn, writes=rn, tag=f"mean{hf}")
        p.emit("vector", lambda e: e.tensor_scalar_mul(R1, self.ps_svv[0:1, :], 1.0 / D),
               reads=[self.rsvv] + rn, writes=rn, tag=f"evv{hf}")
        p.emit("vector", lambda e: e.tensor_mul(R4, R0, R0),
               reads=rn, writes=rn, tag=f"m2{hf}")
        p.emit("vector", lambda e: e.tensor_sub(R1, R1, R4),
               reads=rn, writes=rn, tag=f"var{hf}")
        p.emit("vector", lambda e: e.tensor_scalar_add(R1, R1, 1e-5),
               reads=rn, writes=rn, tag=f"vareps{hf}")
        p.emit("scalar", lambda e: e.activation(R2, R1, AF.Ln),
               reads=rn, writes=rn, tag=f"ln{hf}")
        p.emit("scalar", lambda e: e.activation(R2, R2, AF.Exp, scale=-0.5),
               reads=rn, writes=rn, tag=f"rstd{hf}")
        p.emit("vector", lambda e: e.scalar_tensor_tensor(
            R3, R0, -1.0, R2, ALU.mult, ALU.mult),
            reads=rn, writes=rn, tag=f"brow{hf}")
        p.emit("tensor", lambda e: e.matmul(
            self.ps_a[:], cm.ones_row_f32[:], R2, start=True, stop=True),
            reads=["ones_row_f32"] + rn, writes=[self.rsa], tag=f"bca{hf}")
        p.emit("tensor", lambda e: e.matmul(
            self.ps_b[:], cm.ones_row_f32[:], R3, start=True, stop=True),
            reads=["ones_row_f32"] + rn, writes=[self.rsb], tag=f"bcb{hf}")
        p.emit("scalar", lambda e: e.copy(self.abc[:, 0:HALF], self.ps_a[:]),
               reads=[self.rsa], writes=[nm + "lnabc"], tag=f"ea{hf}")
        p.emit("scalar", lambda e: e.copy(self.abc[:, HALF:2 * HALF], self.ps_b[:]),
               reads=[self.rsb, nm + "lnabc"], writes=[nm + "lnabc"], tag=f"eb{hf}")
        for dt in range(KC):
            pbuf = dt % 2
            p.dma("sync", self.xres[pbuf][:],
                  io["v_dram"][128 * dt:128 * (dt + 1), HALF * hf:HALF * (hf + 1)],
                  reads=[vpre + f"{hf}_{dt}"], writes=[nm + f"lnxr{pbuf}"],
                  tag=f"ldv{hf}_{dt}", role=f"xres{pbuf}")
            p.emit("gpsimd", lambda e, pbuf=pbuf: e.tensor_mul(
                self.xnew[pbuf][:], self.xres[pbuf][:], self.abc[:, 0:HALF]),
                reads=[nm + f"lnxr{pbuf}", nm + "lnabc"], writes=[nm + f"lnxn{pbuf}"],
                tag=f"ap1_{hf}_{dt}")
            p.emit("gpsimd", lambda e, pbuf=pbuf: e.tensor_add(
                self.xnew[pbuf][:], self.xnew[pbuf][:], self.abc[:, HALF:2 * HALF]),
                reads=[nm + f"lnxn{pbuf}", nm + "lnabc"], writes=[nm + f"lnxn{pbuf}"],
                tag=f"ap2_{hf}_{dt}")
            p.dma("sync", io["x_out"][128 * dt:128 * (dt + 1), HALF * hf:HALF * (hf + 1)],
                  self.xnew[pbuf][:], reads=[nm + f"lnxn{pbuf}"],
                  writes=[io["x_out_res"] + f"_{hf}_{dt}"],
                  tag=f"stx{hf}_{dt}", role=f"st{pbuf}")


def emit_attention_layer(p, cm, L, io):
    nc = p.nc
    nm = L["nm"]
    l0 = L["kv_mode"] == "l0"

    ar = io.get("arena")
    if ar is None:
        xn = p.buf(nm + "_xn", [128, KC * HALF], BF16)
        attn = p.buf(nm + "_attn", [128, NH * HALF], BF16)
        k_sb = p.buf(nm + "_k", [128, T], BF16)
        v_sb = p.buf(nm + "_v", [128, T], BF16)
        wq_sb = [p.buf(nm + f"_wq{i}", [128, KC * 128], BF16) for i in range(2)]
        wo_sb = [p.buf(nm + f"_wo{i}", [128, NH * 128], BF16) for i in range(2)]
    else:
        xn, attn, k_sb, v_sb = ar["xn"], ar["attn"], ar["k"], ar["v"]
        wq_sb, wo_sb = ar["wq"], ar["wo"]
    q_sb = [p.buf(nm + f"_q{i}", [128, HALF], BF16) for i in range(2)]
    exp_sb = [p.buf(nm + f"_exp{i}", [128, HALF], BF16) for i in range(4)]
    av_sb = p.buf(nm + "_av", [128, HALF], BF16)
    rec_sb = p.buf(nm + "_rec", [1, HALF], F32)
    if io.get("proj_t") is not None:
        proj_sb = io["proj_t"]
        proj_res = io["proj_res"]
    else:
        proj_sb = [p.buf(nm + f"_proj{i}", [128, HALF], F32)[:] for i in range(2)]
        proj_res = [nm + f"proj{i}" for i in range(2)]
    if l0:
        qb_sb = p.buf(nm + "_qb", [128, NH], F32)
        kb_sb = p.buf(nm + "_kb", [128, 1], F32)
        vbias_sb = p.buf(nm + "_vbias", [1, 128], F32)
        vbc_sb = p.buf(nm + "_vbc", [128, 128], F32)
        ob_sb = p.buf(nm + "_ob", [128, KC], F32)
        wkv_sb = io.get("wkv_t")
        if wkv_sb is None:
            wkv_sb = p.buf(nm + "_wkv", [128, 2 * KC * 128], BF16)[:]
        xs_sb = [xn, attn]
        xs_res = [nm + "xn", nm + "attn"]

    if io.get("pb") is not None:
        pb = io["pb"]
    else:
        pb = [nc.alloc_psum_tensor(nm + f"_pb{i}", [128, 512], F32) for i in range(8)]
        io["pb"] = pb
        io["pb_res"] = [nm + f"_pb{i}" for i in range(8)]
    ps_sc = [pb[0], pb[1], pb[6], pb[4]]
    sc_R = [0, 1, 6, 4]
    ps_av, ps_sum, ps_q, ps_bc, ps_proj, ps_misc = pb[2], pb[3], pb[4], pb[5], pb[6], pb[7]
    R = {i: io["pb_res"][i] for i in range(8)}
    pln = io.get("_pln") or PostLN(p, cm, nm, ps_sum, ps_misc, ps_bc, ps_q,
                                   R[3], R[7], R[5], R[4])
    io["_xn_t"] = xn
    io["_pln"] = pln
    io["_wq"] = wq_sb
    io["_wo"] = wo_sb
    io["_attn_t"] = attn
    io["_pb"] = pb
    io["_pb_res"] = [R[i] for i in range(8)]

    if l0:
        p.dma("sync", qb_sb[:], io["qb"], writes=[nm + "qb"], tag="ld_qb")
        p.dma("sync", kb_sb[:], io["kb"], writes=[nm + "kb"], tag="ld_kb")
        p.dma("sync", vbias_sb[:], io["vbias"], writes=[nm + "vbias"], tag="ld_vbias")
        p.dma("sync", ob_sb[:], io["ob"], writes=[nm + "ob"], tag="ld_ob")
        for idx, nmw in enumerate(("wk", "wv")):
            dst = wkv_sb[:, idx * KC * 128:(idx + 1) * KC * 128]
            p.dma("gpsimd", dst, io[nmw], writes=[nm + "wkv"], tag="ld_" + nmw, role="wq")
        p.emit("tensor", lambda e: e.matmul(
            ps_bc[:, 0:128], cm.ones_row_f32[:], vbias_sb[:], start=True, stop=True),
            reads=["ones_row_f32", nm + "vbias"], writes=[R[5]], tag="vb_bc")
        p.emit("scalar", lambda e: e.copy(vbc_sb[:], ps_bc[:, 0:128]),
               reads=[R[5]], writes=[nm + "vbc"], tag="vb_evac")

    if l0:
        NS = T // HALF
        for ss in range(NS):
            buf = ss % 2
            dst = xs_sb[buf][:, 0:KC * HALF]
            src = io["xT"][:, KC * HALF * ss:KC * HALF * (ss + 1)]
            p.dma("gpsimd", dst, src, writes=[xs_res[buf]], tag=f"xs{ss}", role="xs")
            for kc in range(KC):
                p.emit("tensor", lambda e, kc=kc, buf=buf: e.matmul(
                    ps_misc[:], wkv_sb[:, 128 * kc:128 * kc + 128],
                    xs_sb[buf][:, HALF * kc:HALF * (kc + 1)],
                    start=(kc == 0), stop=(kc == KC - 1)),
                    reads=[nm + "wkv", xs_res[buf]], writes=[R[7]], tag=f"kmm{ss}_{kc}")
            p.emit("scalar", lambda e, ss=ss: e.add(
                k_sb[:, HALF * ss:HALF * (ss + 1)], ps_misc[:], kb_sb[:, 0:1]),
                reads=[R[7], nm + "kb"], writes=[nm + "k"], tag=f"kev{ss}")
            for st in range(4):
                for kc in range(KC):
                    p.emit("tensor", lambda e, kc=kc, st=st, buf=buf: e.matmul(
                        ps_q[:, 0:128],
                        xs_sb[buf][:, HALF * kc + 128 * st: HALF * kc + 128 * st + 128],
                        wkv_sb[:, KC * 128 + 128 * kc: KC * 128 + 128 * kc + 128],
                        start=(kc == 0), stop=(kc == KC - 1)),
                        reads=[nm + "wkv", xs_res[buf]], writes=[R[4]],
                        tag=f"vmm{ss}_{st}_{kc}")
                col = 128 * (4 * ss + st)
                p.emit("vector", lambda e, col=col: e.tensor_add(
                    v_sb[:, col:col + 128], ps_q[:, 0:128], vbc_sb[:]),
                    reads=[R[4], nm + "vbc"], writes=[nm + "v"], tag=f"vev{ss}_{st}")

    def tile_cfg_l0(K, hf):
        gK8 = 16 * K - HALF * hf
        i0 = max(0, -(-(8 * gK8 - 7) // 8))
        if i0 >= HALF:
            return None
        i1 = min(HALF, max(i0, -(-(8 * gK8 + 128) // 8)))
        off = 512 - gK8 + i0
        return (i0, i1, lambda w: cm.mask0[:, off:off + w])

    def tile_cfg_l1(r, B, hf):
        w0 = 128 * B - HALF * hf
        i0 = max(0, w0)
        if i0 >= HALF:
            return None
        i1 = min(HALF, w0 + 128)
        if i1 <= i0:
            return (i0, i0, None)
        moff = 128 * r + (i0 - w0)
        return (i0, i1, lambda w: cm.mask1[:, moff:moff + w])

    for hf in range(2):
        if io.get("x_in_c") is not None:
            p.dma("gpsimd", xn[:], io["x_in_c"][:, KC * HALF * hf:KC * HALF * (hf + 1)],
                  reads=io.get("x_in_res", []), writes=[nm + "xn"],
                  tag=f"xn{hf}", role="xs")
        else:
            dst = xn[:].rearrange("p (a s) -> p a s", a=KC)
            src = io["x_in"][:, HALF * hf:HALF * (hf + 1)].rearrange("(a p) s -> p a s", p=128)
            p.dma("gpsimd", dst, src, reads=io.get("x_in_res", []), writes=[nm + "xn"],
                  tag=f"xn{hf}", role="xs")

        def make_tail(h, hf=hf):
            def tail():
                p.emit("vector", lambda e: e.reciprocal(rec_sb[:], ps_sum[0:1, :]),
                       reads=[R[3]], writes=[nm + "rec"], tag=f"rec{hf}_{h}")
                p.emit("tensor", lambda e: e.matmul(
                    ps_bc[:], cm.ones_row_f32[:], rec_sb[:], start=True, stop=True),
                    reads=["ones_row_f32", nm + "rec"], writes=[R[5]], tag=f"rbc{hf}_{h}")
                p.emit("scalar", lambda e: e.copy(av_sb[:], ps_av[:]),
                       reads=[R[2]], writes=[nm + "av"], tag=f"avev{hf}_{h}")
                p.emit("vector", lambda e, h=h: e.tensor_mul(
                    attn[:, HALF * h:HALF * (h + 1)], av_sb[:], ps_bc[:]),
                    reads=[nm + "av", R[5]], writes=[nm + "attn"], tag=f"at{hf}_{h}")
            return tail

        pend_tail = None
        for h in range(NH):
            wbuf = h % 2
            if hf == 0 or True:
                pass
            if h == 0 and hf == 0:
                p.dma("gpsimd", wq_sb[0][:], io["wq"][0],
                      writes=[nm + "wq0"], tag=f"ldwq0_{hf}", role="wq")
            nh_ = h + 1 if h + 1 < NH else (0 if hf == 0 else None)
            if nh_ is not None:
                nbuf = nh_ % 2
                p.dma("gpsimd", wq_sb[nbuf][:], io["wq"][nh_],
                      writes=[nm + f"wq{nbuf}"], tag=f"ldwq{nh_}_{hf}", role="wq")
            if (not l0) and h % 4 == 0:
                kh = h // 4
                for r in range(8):
                    p.dma("sync", k_sb[:, 1024 * r:1024 * (r + 1)],
                          io["gathered"][2048 * r + 128 * kh: 2048 * r + 128 * kh + 128, :],
                          reads=[io["gathered_res"]], writes=[nm + f"k{r}"], tag=f"ldk{kh}_{r}", role="kv")
                    dstv = v_sb[:, 1024 * r:1024 * (r + 1)].rearrange("p (b m) -> p b m", b=8)
                    srcv = io["gathered"][2048 * r + 1024: 2048 * (r + 1),
                                          128 * kh:128 * kh + 128].rearrange(
                        "(b p) m -> p b m", p=128)
                    p.dma("sync", dstv, srcv, reads=[io["gathered_res"]],
                          writes=[nm + f"v{r}"], tag=f"ldv{kh}_{r}", role="kv")
            for kc in range(KC):
                p.emit("tensor", lambda e, kc=kc, wbuf=wbuf: e.matmul(
                    ps_q[:], wq_sb[wbuf][:, 128 * kc:128 * kc + 128],
                    xn[:, HALF * kc:HALF * (kc + 1)],
                    start=(kc == 0), stop=(kc == KC - 1)),
                    reads=[nm + f"wq{wbuf}", nm + "xn"], writes=[R[4]],
                    tag=f"qmm{hf}_{h}_{kc}")
            qbuf = h % 2
            if l0:
                p.emit("scalar", lambda e, qbuf=qbuf, h=h: e.add(
                    q_sb[qbuf][:], ps_q[:], qb_sb[:, h:h + 1]),
                    reads=[R[4], nm + "qb"], writes=[nm + f"q{qbuf}"], tag=f"qev{hf}_{h}")
            else:
                p.emit("scalar", lambda e, qbuf=qbuf: e.copy(q_sb[qbuf][:], ps_q[:]),
                       reads=[R[4]], writes=[nm + f"q{qbuf}"], tag=f"qev{hf}_{h}")
            if pend_tail is not None:
                pend_tail()
                pend_tail = None

            if l0:
                tiles = []
                for K in range(T // 128):
                    cfg = tile_cfg_l0(K, hf)
                    if cfg is not None:
                        tiles.append((k_sb[:, 128 * K:128 * K + 128], nm + "k",
                                      v_sb[:, 128 * K:128 * K + 128], nm + "v", cfg))
            else:
                tiles = []
                for B in range(8):
                    for r in range(8):
                        cfg = tile_cfg_l1(r, B, hf)
                        if cfg is not None:
                            o = 1024 * r + 128 * B
                            tiles.append((k_sb[:, o:o + 128], nm + f"k{r}",
                                          v_sb[:, o:o + 128], nm + f"v{r}", cfg))
            n_tiles = len(tiles)

            def rest_stage(idx, h=h, hf=hf, qbuf=qbuf, tiles=tiles, n_tiles=n_tiles):
                k_ap, k_res, v_ap, v_res, cfg = tiles[idx]
                i0, i1, mfn = cfg
                sb = idx % 4
                rb = sc_R[sb]
                first = idx == 0
                last = idx == n_tiles - 1
                p.emit("scalar", lambda e, sb=sb, i0=i0: e.activation(
                    exp_sb[sb][:, i0:HALF], ps_sc[sb][:, i0:HALF], AF.Exp,
                    scale=SM_SCALE),
                    reads=[R[rb]], writes=[nm + f"exp{sb}"], tag=f"ex{hf}_{h}_{idx}")
                if i1 > i0 and mfn is not None:
                    p.emit("vector", lambda e, sb=sb, i0=i0, i1=i1, mfn=mfn: e.tensor_mul(
                        exp_sb[sb][:, i0:i1], exp_sb[sb][:, i0:i1], mfn(i1 - i0)),
                        reads=["mask0" if l0 else "mask1", nm + f"exp{sb}"],
                        writes=[nm + f"exp{sb}"], tag=f"mk{hf}_{h}_{idx}")
                p.emit("tensor", lambda e, sb=sb, i0=i0, v_ap=v_ap, first=first, last=last: e.matmul(
                    ps_av[:, i0:HALF], v_ap, exp_sb[sb][:, i0:HALF],
                    start=first, stop=last),
                    reads=[v_res, nm + f"exp{sb}"], writes=[R[2]],
                    tag=f"av{hf}_{h}_{idx}")
                p.emit("tensor", lambda e, sb=sb, i0=i0, first=first, last=last: e.matmul(
                    ps_sum[0:1, i0:HALF], cm.ones_col_bf[:], exp_sb[sb][:, i0:HALF],
                    start=first, stop=last),
                    reads=["ones_col", nm + f"exp{sb}"], writes=[R[3]],
                    tag=f"su{hf}_{h}_{idx}")

            LA = 2
            for idx, (k_ap, k_res, v_ap, v_res, cfg) in enumerate(tiles):
                i0 = cfg[0]
                sb = idx % 4
                rb = sc_R[sb]
                p.emit("tensor", lambda e, sb=sb, k_ap=k_ap, i0=i0, qbuf=qbuf: e.matmul(
                    ps_sc[sb][:, i0:HALF], k_ap, q_sb[qbuf][:, i0:HALF],
                    start=True, stop=True),
                    reads=[k_res, nm + f"q{qbuf}"], writes=[R[rb]],
                    tag=f"sc{hf}_{h}_{idx}")
                if idx >= LA:
                    rest_stage(idx - LA)
            for j in range(max(0, n_tiles - LA), n_tiles):
                rest_stage(j)
            pend_tail = make_tail(h)
        if pend_tail is not None:
            pend_tail()
            pend_tail = None

        if hf == 0:
            p.dma("gpsimd", wo_sb[0][:], io["wo"][0],
                  writes=[nm + "wo0"], tag=f"ldwo0_{hf}", role="wo")
        pending = []
        for dt in range(KC):
            wbuf = dt % 2
            ndt = dt + 1 if dt + 1 < KC else (0 if hf == 0 else None)
            if ndt is not None:
                nbuf = ndt % 2
                p.dma("gpsimd", wo_sb[nbuf][:], io["wo"][ndt],
                      writes=[nm + f"wo{nbuf}"], tag=f"ldwo{ndt}_{hf}", role="wo")
            pbank = ps_proj if dt % 2 == 0 else ps_av
            pR = R[6] if dt % 2 == 0 else R[2]
            for h in range(NH):
                p.emit("tensor", lambda e, h=h, wbuf=wbuf, pbank=pbank: e.matmul(
                    pbank[:], wo_sb[wbuf][:, 128 * h:128 * h + 128],
                    attn[:, HALF * h:HALF * (h + 1)],
                    start=(h == 0), stop=(h == NH - 1)),
                    reads=[nm + f"wo{wbuf}", nm + "attn"], writes=[pR],
                    tag=f"pj{hf}_{dt}_{h}")
            pbuf = dt % 2
            if l0:
                p.emit("scalar", lambda e, pbuf=pbuf, dt=dt, pbank=pbank: e.add(
                    proj_sb[pbuf], pbank[:], ob_sb[:, dt:dt + 1]),
                    reads=[pR, nm + "ob"], writes=[proj_res[pbuf]],
                    tag=f"pjev{hf}_{dt}")
            else:
                p.emit("scalar", lambda e, pbuf=pbuf, pbank=pbank: e.copy(
                    proj_sb[pbuf], pbank[:]),
                    reads=[pR], writes=[proj_res[pbuf]], tag=f"pjev{hf}_{dt}")
            pending.append((dt, pbuf))
            if len(pending) > 1:
                d0, pb0_ = pending.pop(0)
                pln.add_chunk(io, hf, d0, proj_sb[pb0_], proj_res[pb0_],
                              first=(d0 == 0), last=(d0 == KC - 1))
        for d0, pb0_ in pending:
            pln.add_chunk(io, hf, d0, proj_sb[pb0_], proj_res[pb0_],
                          first=(d0 == 0), last=(d0 == KC - 1))
        pln.finalize(io, hf)



def emit_ffn_setup(p, cm, io):
    nc = p.nc
    e8_bf = nc.alloc_sbuf_tensor("e8_bf", [8, 128], BF16)
    p.dma("sync", cm.e8_f32[:], io["e8"], writes=["e8f"], tag="ld_e8", role="misc")
    p.emit("vector", lambda e: e.tensor_copy(e8_bf[:], cm.e8_f32[:]),
           reads=["e8f"], writes=["e8"], tag="cv_e8")
    ps = io["pb"][0]
    zsb = cm.scratch.bitcast(I32)[0:96, 0:1376]
    tmp = cm.scratch.bitcast(I32)[0:96, 1376:2752]
    zunp = cm.scratch[0:96, 5504:5504 + 1408]
    zrow = cm.scratch[0:8, 6912:6912 + INTER_PAD]
    stage = cm.scratch[0:128, 18176:18176 + 2816]
    SCR = "scratch"
    specs = [
        ("g", io["gate_zeros"], 32, 1376, 10, io["z1g"], INTER_PAD, 4, io["zbc_g"]),
        ("u", io["up_zeros"], 32, 1376, 10, io["z1u"], INTER_PAD, 4, io["zbc_u"]),
        ("d", io["down_zeros"], 86, 512, 4, io["z1d"], D, 11, io["zbc_d"]),
    ]
    for (mn, zin, G, KW, NB, z1d, W, RC, zbcd) in specs:
        p.dma("sync", zsb[0:G, 0:KW], zin, reads=[SCR], writes=[SCR],
              tag=f"ldz_{mn}", role="misc")
        p.emit("gpsimd", lambda e, G=G, KW=KW: e.memset(zunp[0:G, KW:KW + 32], 0.0),
               reads=[SCR], writes=[SCR], tag=f"zpad_{mn}")
        for j2 in range(8):
            p.emit("vector", lambda e, j2=j2, G=G, KW=KW: e.tensor_scalar(
                tmp[0:G, 0:KW], zsb[0:G, 0:KW], 4 * j2, 15,
                ALU.logical_shift_right, ALU.bitwise_and),
                reads=[SCR], writes=[SCR], tag=f"zs_{mn}_{j2}")
            p.emit("vector", lambda e, G=G, KW=KW: e.tensor_scalar_add(
                tmp[0:G, 0:KW], tmp[0:G, 0:KW], 1),
                reads=[SCR], writes=[SCR], tag=f"z1_{mn}_{j2}")
            p.emit("vector", lambda e, j2=j2, G=G, KW=KW: e.tensor_copy(
                zunp[0:G, 0:KW], tmp[0:G, 0:KW]),
                reads=[SCR], writes=[SCR], tag=f"zc_{mn}_{j2}")
            nfull = (KW // 128) * 128
            from concourse.ap import AP as _AP
            if nfull > 0:
                dst = _AP(z1d.tensor, 128 * j2,
                          [[W, G], [1024, nfull // 128], [1, 128]])
                p.dma("sync", dst, zunp[0:G, 0:nfull].rearrange(
                    "g (a k) -> g a k", k=128),
                    reads=[SCR], writes=[f"z1_{mn}"], tag=f"zst_{mn}_{j2}", role="misc")
            if KW > nfull:
                dst = _AP(z1d.tensor, 128 * j2 + 1024 * (KW // 128),
                          [[W, G], [1, KW - nfull + 32]])
                p.dma("sync", dst, zunp[0:G, nfull:KW + 32],
                    reads=[SCR], writes=[f"z1_{mn}"], tag=f"zstt_{mn}_{j2}", role="misc")
        for Rr in range(RC):
            gr = min(8, G - 8 * Rr)
            p.dma("sync", zrow[0:gr, 0:W], z1d[8 * Rr:8 * Rr + gr, :],
                  reads=[f"z1_{mn}", SCR], writes=[SCR], tag=f"zrow_{mn}_{Rr}", role="misc")
            nseg = (W + 2815) // 2816
            for seg in range(nseg):
                w0 = 2816 * seg
                w1 = min(W, w0 + 2816)
                for ns in range(w0, w1, 512):
                    nw = min(512, w1 - ns)
                    p.emit("tensor", lambda e, gr=gr, ns=ns, nw=nw: e.matmul(
                        ps[:, 0:nw], e8_bf[0:gr, :], zrow[0:gr, ns:ns + nw],
                        start=True, stop=True),
                        reads=["e8", SCR], writes=[io["pb_res"][0]], tag=f"zbm_{mn}_{Rr}_{ns}")
                    p.emit("scalar", lambda e, ns=ns, w0=w0, nw=nw: e.copy(
                        stage[:, ns - w0:ns - w0 + nw], ps[:, 0:nw]),
                        reads=[io["pb_res"][0]], writes=["stage"], tag=f"zbe_{mn}_{Rr}_{ns}")
                p.dma("sync", zbcd[Rr][:, w0:w1], stage[:, 0:w1 - w0],
                      reads=["stage"], writes=[f"zbc_{mn}"], tag=f"zbs_{mn}_{Rr}_{seg}",
                      role="misc") if ns + 512 >= w1 else None
    return


def emit_ffn_layer(p, cm, L, io):
    nc = p.nc
    nm = L["nm"]
    if io.get("xn_t") is not None:
        xn = io["xn_t"]
        xn_res = io["xn_res"]
    else:
        xn = nc.alloc_sbuf_tensor(nm + "_xn", [128, KC * HALF], BF16)
        xn_res = nm + "xn"
    io["_ffn_xn"] = xn
    io["_ffn_xn_res"] = xn_res
    inter = io["inter_t"]
    p.emit("vector", lambda e: None, reads=[],
           writes=[nm + "inter"] + io.get("inter_claims", ["scratch", "stage"]),
           tag="inter_claim")
    ea = io.get("ea")
    if ea is None:
        qw = {m: [nc.alloc_sbuf_tensor(nm + f"_qw{m}{i}", [128, 512], I32)[:] for i in range(2)]
              for m in "gud"}
        sbc = {m: [nc.alloc_sbuf_tensor(nm + f"_sbc{m}{i}", [128, 512], BF16)[:] for i in range(2)]
               for m in "gud"}
        zbc = {m: [nc.alloc_sbuf_tensor(nm + f"_zbc{m}{i}", [128, 512], BF16)[:] for i in range(2)]
               for m in "gud"}
        qi = {m: [nc.alloc_sbuf_tensor(nm + f"_qi{m}{i}", [128, 512], I32)[:] for i in range(2)]
              for m in "gu"}
        qbf = {m: [nc.alloc_sbuf_tensor(nm + f"_qbf{m}{i}", [128, 512], BF16)[:] for i in range(2)]
               for m in "gu"}
        sg_sb = nc.alloc_sbuf_tensor(nm + "_sg", [128, HALF], BF16)[:]
        f16_sb = nc.alloc_sbuf_tensor(nm + "_f16", [128, HALF], F16)[:]
        fsub = [nc.alloc_sbuf_tensor(nm + f"_fsub{i}", [128, HALF], F32)[:] for i in range(2)]
    else:
        eai = ea.bitcast(I32)
        eaf = ea.bitcast(F32)
        eah = ea.bitcast(F16)
        qw = {m: [eai[:, 512 * (2 * mi + i):512 * (2 * mi + i) + 512] for i in range(2)]
              for mi, m in enumerate("gud")}
        qi = {m: [eai[:, 3072 + 512 * i:3584 + 512 * i] for i in range(2)] for m in "gu"}
        sbc = {m: [ea[:, 8192 + 512 * (2 * mi + i):8192 + 512 * (2 * mi + i) + 512]
                   for i in range(2)] for mi, m in enumerate("gu")}
        zbc = {m: [ea[:, 10240 + 512 * (2 * mi + i):10240 + 512 * (2 * mi + i) + 512]
                   for i in range(2)] for mi, m in enumerate("gu")}
        sbc["d"] = sbc["g"]
        zbc["d"] = zbc["g"]
        qbf = {"g": [ea[:, 12288 + 512 * i:12800 + 512 * i] for i in range(2)],
               "u": [ea[:, 13312 + 512 * i:13824 + 512 * i] for i in range(2)]}
        sg_sb = ea[:, 14336:14848]
        fsub = [eaf[:, 7936 + 512 * i:8448 + 512 * i] for i in range(2)]
        f16_sb = eah[:, 14848:15360]
    qi["d"] = qi["g"]
    qbf["d"] = qbf["g"]
    wt = qbf
    mres = {"g": "g", "u": "u", "d": "g"}
    if ea is None:
        sres = {"g": "g", "u": "u", "d": "d"}
        qires = {"g": "g", "u": "u", "d": "g"}
    else:
        sres = {"g": "g", "u": "u", "d": "g"}
        qires = {"g": "g", "u": "g", "d": "g"}

    pb = io["pb"]
    R = {i: io["pb_res"][i] for i in range(8)}
    if io.get("swap_banks"):
        gate_ps, up_ps = pb[4:8], pb[0:4]
        gate_R = [4, 5, 6, 7]
        up_R = [0, 1, 2, 3]
    else:
        gate_ps, up_ps = pb[0:4], pb[4:8]
        gate_R = [0, 1, 2, 3]
        up_R = [4, 5, 6, 7]
    down_ps = [pb[0], pb[1], pb[2], pb[6]]
    down_R = [R[0], R[1], R[2], R[6]]
    pln = io.get("_pln") or PostLN(p, cm, nm, pb[3], pb[7], pb[5], pb[4],
                                   R[3], R[7], R[5], R[4])

    def dequant(m, par, j, Rr, kcols, tagp):
        jb = j % 2
        mr = mres[m]
        sr = sres[m]
        qr = qires[m]
        p.emit("vector", lambda e: e.tensor_scalar(
            qi[m][jb][0:kcols, :], qw[m][par][0:kcols, :], 4 * j, 15,
            ALU.logical_shift_right, ALU.bitwise_and),
            reads=[nm + f"qw{m}{par}"], writes=[nm + f"qi{qr}{jb}"], tag=f"dq1{tagp}")
        p.emit("scalar", lambda e: e.copy(
            qbf[m][jb][0:kcols, :], qi[m][jb][0:kcols, :]),
            reads=[nm + f"qi{qr}{jb}"], writes=[nm + f"qbf{mr}{jb}"], tag=f"dq2{tagp}")
        p.emit("vector", lambda e: e.tensor_sub(
            qbf[m][jb][0:kcols, :], qbf[m][jb][0:kcols, :], zbc[m][par][0:kcols, :]),
            reads=[nm + f"qbf{mr}{jb}", nm + f"zbc{sr}{par}"],
            writes=[nm + f"qbf{mr}{jb}"], tag=f"dq3{tagp}")
        p.emit("vector", lambda e: e.tensor_mul(
            qbf[m][jb][0:kcols, :], qbf[m][jb][0:kcols, :], sbc[m][par][0:kcols, :]),
            reads=[nm + f"qbf{mr}{jb}", nm + f"sbc{sr}{par}"],
            writes=[nm + f"qbf{mr}{jb}"], tag=f"dq4{tagp}")
        return qbf[m][jb], nm + f"qbf{mr}{jb}"

    def load_tiles(m, par, Rr, cols, c0, kcols, tagp):
        p.dma("sync", qw[m][par][0:kcols, :], io[f"qw_{m}"][128 * Rr:128 * Rr + kcols,
                                                           c0:c0 + cols],
              reads=[], writes=[nm + f"qw{m}{par}"], tag=f"lq{tagp}", role="qw")
        sr = sres[m]
        p.dma("gpsimd", sbc[m][par][:, 0:cols], io[f"sbc_{m}"][Rr][:, c0:c0 + cols],
              reads=[], writes=[nm + f"sbc{sr}{par}"], tag=f"ls{tagp}", role="sbc")
        p.dma("sync", zbc[m][par][:, 0:cols], io[f"zbc_{m}"][Rr][:, c0:c0 + cols],
              reads=[f"zbc_{m}", "__none"], writes=[nm + f"zbc{sr}{par}"], tag=f"lz{tagp}",
              role="zbc")

    for hf in range(2):
        dst = xn[:].rearrange("p (a s) -> p a s", a=KC)
        srcx = io["x_in"][:, HALF * hf:HALF * (hf + 1)].rearrange("(a p) s -> p a s", p=128)
        p.dma("gpsimd", dst, srcx, reads=io.get("x_in_res", []), writes=[xn_res],
              tag=f"fxn{hf}", role="xs")
        for ns in range(INTER_PAD // 512):
            for Rr in range(4):
                par = Rr % 2
                load_tiles("g", par, Rr, 512, 512 * ns, 128, f"g{hf}_{ns}_{Rr}")
                load_tiles("u", par, Rr, 512, 512 * ns, 128, f"u{hf}_{ns}_{Rr}")
                for j in range(8):
                    wg, wg_res = dequant("g", par, j, Rr, 128, f"g{hf}_{ns}_{Rr}_{j}")
                    NTO = io.get("nt_order", [0, 1, 2, 3])
                    if io.get("dbg_wg") is not None and hf == 0 and ns == 0 and Rr == 0 and j == 1:
                        p.dma("sync", io["dbg_wg"], wg, reads=[wg_res],
                              writes=["dbgwg"], tag="dbgwg", role="misc")
                        p.dma("sync", io["dbg_sbc"], sbc["g"][par][:, 0:512], reads=[nm + "sbcg" + str(par)],
                              writes=["dbgsbc"], tag="dbgsbc", role="misc")
                        p.dma("sync", io["dbg_zbc"], zbc["g"][par][:, 0:512], reads=[nm + "zbcg" + str(par)],
                              writes=["dbgzbc"], tag="dbgzbc", role="misc")
                    tau = 8 * Rr + j
                    for nt in NTO:
                        p.emit("tensor", lambda e, nt=nt, wg=wg, tau=tau, Rr=Rr, j=j: e.matmul(
                            gate_ps[nt][:], wg[:, 128 * nt:128 * nt + 128],
                            xn[:, HALF * tau:HALF * (tau + 1)],
                            start=(Rr == 0 and j == 0), stop=(Rr == 3 and j == 7)),
                            reads=[wg_res, xn_res], writes=[R[gate_R[nt]]],
                            tag=f"gm{hf}_{ns}_{Rr}_{j}_{nt}")
                    wu, wu_res = dequant("u", par, j, Rr, 128, f"u{hf}_{ns}_{Rr}_{j}")
                    for nt in NTO:
                        p.emit("tensor", lambda e, nt=nt, wu=wu, tau=tau, Rr=Rr, j=j: e.matmul(
                            up_ps[nt][:], wu[:, 128 * nt:128 * nt + 128],
                            xn[:, HALF * tau:HALF * (tau + 1)],
                            start=(Rr == 0 and j == 0), stop=(Rr == 3 and j == 7)),
                            reads=[wu_res, xn_res], writes=[R[up_R[nt]]],
                            tag=f"um{hf}_{ns}_{Rr}_{j}_{nt}")
            if io.get("dbg_ps") is not None and hf == 0 and ns == 0:
                p.emit("scalar", lambda e: e.copy(fsub[0], gate_ps[3][:]),
                       reads=[R[gate_R[3]]], writes=[nm + "fsub0"], tag="dbgpse")
                p.dma("sync", io["dbg_ps"], fsub[0], reads=[nm + "fsub0"],
                      writes=["dbgps"], tag="dbgps", role="misc")
                p.emit("scalar", lambda e: e.copy(fsub[0], up_ps[3][:]),
                       reads=[R[up_R[3]], "dbgps"], writes=[nm + "fsub0"], tag="dbgpse2")
                p.dma("sync", io["dbg_ps2"], fsub[0], reads=[nm + "fsub0"],
                      writes=["dbgps2"], tag="dbgps2", role="misc")
            for nt in range(4):
                p.emit("scalar", lambda e, nt=nt: e.activation(
                    sg_sb, gate_ps[nt][:], AF.Silu),
                    reads=[R[gate_R[nt]]], writes=[nm + "sg"], tag=f"sl{hf}_{ns}_{nt}")
                slot = 4 * ns + nt
                p.emit("vector", lambda e, nt=nt, slot=slot: e.tensor_mul(
                    inter[:, HALF * slot:HALF * (slot + 1)], sg_sb, up_ps[nt][:]),
                    reads=[nm + "sg", R[up_R[nt]]], writes=[nm + "inter"],
                    tag=f"im{hf}_{ns}_{nt}")
        if io.get("dbg_inter") is not None and hf == 0:
            p.dma("sync", io["dbg_inter"], inter[:, 0:4096], reads=[nm + "inter"],
                  writes=["dbgi"], tag="dbgi", role="misc")
            p.dma("sync", io["dbg_xn"], xn[:, 0:4096], reads=[xn_res],
                  writes=["dbgx"], tag="dbgx", role="misc")
        dpending = []
        for ds in range(8):
            for Rr in range(11):
                par = Rr % 2
                kcols = 96 if Rr == 10 else 128
                load_tiles("d", par, Rr, 512, 512 * ds, kcols, f"d{hf}_{ds}_{Rr}")
                for j in range(8):
                    wd, wd_res = dequant("d", par, j, Rr, kcols, f"d{hf}_{ds}_{Rr}_{j}")
                    tau = 8 * Rr + j
                    for dtl in range(4):
                        p.emit("tensor", lambda e, dtl=dtl, wd=wd, tau=tau, kcols=kcols, Rr=Rr, j=j: e.matmul(
                            down_ps[dtl][:], wd[0:kcols, 128 * dtl:128 * dtl + 128],
                            inter[0:kcols, HALF * tau:HALF * (tau + 1)],
                            start=(Rr == 0 and j == 0), stop=(Rr == 10 and j == 7)),
                            reads=[wd_res, nm + "inter"], writes=[down_R[dtl]],
                            tag=f"dm{hf}_{ds}_{Rr}_{j}_{dtl}")
            for dtl in range(4):
                dt = 4 * ds + dtl
                pbuf = dt % 2
                p.emit("scalar", lambda e, dtl=dtl: e.copy(f16_sb, down_ps[dtl][:]),
                       reads=[down_R[dtl]], writes=[nm + "f16"], tag=f"de1{hf}_{dt}")
                p.emit("scalar", lambda e, pbuf=pbuf: e.copy(fsub[pbuf], f16_sb),
                       reads=[nm + "f16"], writes=[nm + f"fsub{pbuf}"], tag=f"de2{hf}_{dt}")
                dpending.append((dt, pbuf))
                if len(dpending) > 1:
                    d0, pb0_ = dpending.pop(0)
                    pln.add_chunk(io, hf, d0, fsub[pb0_], nm + f"fsub{pb0_}",
                                  first=(d0 == 0), last=(d0 == KC - 1))
        for d0, pb0_ in dpending:
            pln.add_chunk(io, hf, d0, fsub[pb0_], nm + f"fsub{pb0_}",
                          first=(d0 == 0), last=(d0 == KC - 1))
        pln.finalize(io, hf)


def emit_contrib(p, cm, nm, io):
    nc = p.nc
    xn = io["cxn"]
    wkv = io["cwkv"]
    wkv_res = io["cwkv_res"]
    kb = io["kvbuf"]
    kb_res = io["kvbuf_res"]
    ps = io["pb"]
    R = io["pb_res"]
    for hf in range(2):
        dst = xn[:].rearrange("p (a s) -> p a s", a=KC)
        srcx = io["x2"][:, HALF * hf:HALF * (hf + 1)].rearrange("(a p) s -> p a s", p=128)
        p.dma("gpsimd", dst, srcx, reads=io["x2_res"], writes=[io["cxn_res"]],
              tag=f"cxn{hf}", role="xs")
        for mt in range(8):
            par = mt % 2
            p.dma("gpsimd", wkv[par], io["wk_t"][mt],
                  writes=[wkv_res[par]], tag=f"cwk{hf}_{mt}", role="wq")
            for kc in range(KC):
                p.emit("tensor", lambda e, kc=kc, par=par: e.matmul(
                    ps[0][:], wkv[par][:, 128 * kc:128 * kc + 128],
                    xn[:, HALF * kc:HALF * (kc + 1)],
                    start=(kc == 0), stop=(kc == KC - 1)),
                    reads=[wkv_res[par], io["cxn_res"]], writes=[R[0]],
                    tag=f"ck{hf}_{mt}_{kc}")
            p.emit("scalar", lambda e, mt=mt, hf=hf: e.copy(
                kb[:, 1024 * (mt % 4) + 0:1024 * (mt % 4) + HALF] if False else
                kb[:, 1024 * (mt % 4) + HALF * hf:1024 * (mt % 4) + HALF * hf + HALF],
                ps[0][:]),
                reads=[R[0]], writes=[kb_res], tag=f"cke{hf}_{mt}")
            if mt % 4 == 3 or True:
                pass
            p.dma("sync", io["contrib"][128 * mt:128 * (mt + 1), HALF * hf:HALF * (hf + 1)],
                  kb[:, 1024 * (mt % 4) + HALF * hf:1024 * (mt % 4) + HALF * hf + HALF],
                  reads=[kb_res], writes=[f"ctr_k{mt}_{hf}"], tag=f"cks{hf}_{mt}",
                  role="st0")
        for mt in range(8):
            par = mt % 2
            p.dma("gpsimd", wkv[par], io["wv_t"][mt],
                  writes=[wkv_res[par]], tag=f"cwv{hf}_{mt}", role="wq")
            for tt in range(4):
                for kc in range(KC):
                    p.emit("tensor", lambda e, kc=kc, par=par, tt=tt: e.matmul(
                        ps[1][:, 0:128], xn[:, HALF * kc + 128 * tt:HALF * kc + 128 * tt + 128],
                        wkv[par][:, 128 * kc:128 * kc + 128],
                        start=(kc == 0), stop=(kc == KC - 1)),
                        reads=[wkv_res[par], io["cxn_res"]], writes=[R[1]],
                        tag=f"cv{hf}_{mt}_{tt}_{kc}")
                p.emit("scalar", lambda e, tt=tt, mt=mt: e.copy(
                    kb[:, 4096 + 1024 * tt + 128 * mt: 4096 + 1024 * tt + 128 * mt + 128],
                    ps[1][:, 0:128]),
                    reads=[R[1]], writes=[kb_res], tag=f"cve{hf}_{mt}_{tt}")
        for tt in range(4):
            gt = 128 * (4 * hf + tt)
            p.dma("sync", io["contrib"][1024 + gt:1024 + gt + 128, :],
                  kb[:, 4096 + 1024 * tt: 4096 + 1024 * tt + 1024],
                  reads=[kb_res], writes=[f"ctr_v{hf}_{tt}"], tag=f"cvs{hf}_{tt}",
                  role="st1")


def emit_moe_layer(p, cm, nm, io):
    nc = p.nc
    xn = io["moe_xn"]
    xn_res = io["moe_xn_res"]
    scr = io["moe_scr"]
    scr_res = io["moe_scr_res"]
    pb = io["pb"]
    R = io["pb_res"]
    pln = io.get("_pln") or PostLN(p, cm, nm, pb[3], pb[7], pb[5], pb[4],
                                   R[3], R[7], R[5], R[4])

    if io.get("wr_view") is not None:
        wr_sb = io["wr_view"]
    else:
        wr_sb = nc.alloc_sbuf_tensor(nm + "_wr", [128, KC * 4], BF16)[:]
    amat = cm.iota0[:].bitcast(BF16)[0:128, 0:4 * KC * 16]
    wq0f = io["ar_wq0"][:].bitcast(F32)
    wq1b = io["ar_wq1"][:]
    wo0f = io["ar_wo0"][:].bitcast(F32)
    wo1f = io["ar_wo1"][:].bitcast(F32)
    RQ0, RQ1, RO0, RO1 = io["ar_res"]
    identf = wq0f[0:128, 64:192]
    mid_sb = wq0f[0:16, 192:704]
    dsub = [wq0f[0:128, 704:1216], wq0f[0:128, 1216:1728]]
    ms_sb = wq1b[0:16, 0:4 * HALF]
    ls_sb = wo0f[0:4, 0:TO]
    g_sb = wo0f[0:4, TO:2 * TO]
    lt_sb = wo1f[0:128, 0:32]
    rowm = wo1f[0:128, 32:40]
    tmp4 = wo1f[0:128, 40:56]
    ge_sb = wo1f[0:1, 56:56 + HALF]

    p.emit("gpsimd", lambda e: e.memset(identf, 1.0), writes=[RQ0], tag="midf0")
    p.emit("gpsimd", lambda e: e.affine_select(
        identf, identf, [[-1, 128]], ALU.is_equal, 0.0, base=0, channel_multiplier=1),
        reads=[RQ0], writes=[RQ0], tag="midf")
    p.dma("gpsimd", wr_sb.rearrange("p (a m) -> p a m", a=KC),
          io["wr_t"].rearrange("(a p) m -> p a m", p=128), writes=[nm + "wr"],
          tag="ldwr", role="wq")
    p.dma("gpsimd", amat, io["a_t"], writes=["iota0"], tag="lda", role="wq")
    bmat = scr[0:16, 0:4 * D]
    p.dma("gpsimd", bmat.rearrange("p (e m) -> p e m", e=4),
          io["b_t"].rearrange("e r m -> r e m"),
          reads=[scr_res], writes=[scr_res], tag="ldb", role="wq")

    if io.get("dbg_ls") is not None:
        pass
    for hf in range(2):
        dst = xn[:].rearrange("p (a s) -> p a s", a=KC)
        srcx = io["x_in"][:, HALF * hf:HALF * (hf + 1)].rearrange("(a p) s -> p a s", p=128)
        p.dma("gpsimd", dst, srcx, reads=io.get("x_in_res", []), writes=[xn_res],
              tag=f"mxn{hf}", role="xs")
        for kc in range(KC):
            p.emit("tensor", lambda e, kc=kc: e.matmul(
                pb[0][0:4, :], wr_sb[:, 4 * kc:4 * kc + 4],
                xn[:, HALF * kc:HALF * (kc + 1)],
                start=(kc == 0), stop=(kc == KC - 1)),
                reads=[nm + "wr", xn_res], writes=[R[0]], tag=f"lg{hf}_{kc}")
        p.emit("scalar", lambda e, hf=hf: e.copy(ls_sb[0:4, HALF * hf:HALF * (hf + 1)],
                                                 pb[0][0:4, :]),
               reads=[R[0]], writes=[RO0], tag=f"lge{hf}")
        for tt in range(4):
            t0 = HALF * hf + 128 * tt
            p.emit("tensor", lambda e, t0=t0: e.transpose(
                pb[1][:].bitcast(F32)[0:128, 0:4], ls_sb[0:4, t0:t0 + 128], identf[0:4, 0:4]),
                reads=[RO0, RQ0], writes=[R[1]], tag=f"gt{hf}_{tt}")
            p.emit("scalar", lambda e, tt=tt: e.copy(lt_sb[:, 4 * tt:4 * tt + 4],
                                                     pb[1][0:128, 0:4]),
                   reads=[R[1]], writes=[RO1], tag=f"gte{hf}_{tt}")
            L4 = lambda tt=tt: lt_sb[:, 4 * tt:4 * tt + 4]
            p.emit("vector", lambda e, tt=tt: e.reduce_max(
                rowm[:, 0:1], lt_sb[:, 4 * tt:4 * tt + 4], AX.X, negate=True),
                reads=[RO1], writes=[RO1], tag=f"m1{hf}_{tt}")
            p.emit("vector", lambda e, tt=tt: e.reduce_max(
                rowm[:, 4:5], lt_sb[:, 4 * tt:4 * tt + 4], AX.X),
                reads=[RO1], writes=[RO1], tag=f"m1p{hf}_{tt}")
            p.emit("vector", lambda e, tt=tt: e.tensor_scalar(
                tmp4[:, 0:4], lt_sb[:, 4 * tt:4 * tt + 4], 0.0, rowm[:, 4:5],
                ALU.add, ALU.is_ge),
                reads=[RO1, RO1], writes=[RO1], tag=f"im{hf}_{tt}")
            p.emit("vector", lambda e, tt=tt: e.scalar_tensor_tensor(
                tmp4[:, 4:8], tmp4[:, 0:4], -1e30, lt_sb[:, 4 * tt:4 * tt + 4],
                ALU.mult, ALU.add),
                reads=[RO1, RO1], writes=[RO1], tag=f"msk{hf}_{tt}")
            p.emit("vector", lambda e: e.reduce_max(
                rowm[:, 1:2], tmp4[:, 4:8], AX.X),
                reads=[RO1], writes=[RO1], tag=f"m2{hf}_{tt}")
            p.emit("vector", lambda e, tt=tt: e.tensor_scalar(
                tmp4[:, 8:12], lt_sb[:, 4 * tt:4 * tt + 4], 0.0, rowm[:, 1:2],
                ALU.add, ALU.is_ge),
                reads=[RO1, RO1], writes=[RO1], tag=f"kp{hf}_{tt}")
            p.emit("scalar", lambda e, tt=tt: e.activation(
                tmp4[:, 12:16], lt_sb[:, 4 * tt:4 * tt + 4], AF.Exp, bias=rowm[:, 0:1]),
                reads=[RO1, RO1], writes=[RO1], tag=f"ex{hf}_{tt}")
            p.emit("vector", lambda e: e.tensor_mul(
                tmp4[:, 12:16], tmp4[:, 12:16], tmp4[:, 8:12]),
                reads=[RO1], writes=[RO1], tag=f"ek{hf}_{tt}")
            p.emit("vector", lambda e: e.reduce_sum(
                rowm[:, 2:3], tmp4[:, 12:16], AX.X),
                reads=[RO1], writes=[RO1], tag=f"sm{hf}_{tt}")
            p.emit("vector", lambda e: e.reciprocal(rowm[:, 3:4], rowm[:, 2:3]),
                   reads=[RO1], writes=[RO1], tag=f"rc{hf}_{tt}")
            p.emit("vector", lambda e: e.tensor_scalar(
                tmp4[:, 12:16], tmp4[:, 12:16], rowm[:, 3:4], 4.0,
                ALU.mult, ALU.mult),
                reads=[RO1, RO1], writes=[RO1], tag=f"gs{hf}_{tt}")
            p.emit("tensor", lambda e: e.transpose(
                pb[1][:].bitcast(F32)[0:4, 0:128], tmp4[:, 12:16], identf),
                reads=[RO1, RQ0], writes=[R[1]], tag=f"gtb{hf}_{tt}")
            p.emit("scalar", lambda e, t0=t0: e.copy(g_sb[0:4, t0:t0 + 128],
                                                     pb[1][0:4, 0:128]),
                   reads=[R[1]], writes=[RO0], tag=f"gbe{hf}_{tt}")
    if io.get("dbg_ls") is not None:
        p.dma("sync", io["dbg_ls"], ls_sb[0:4, :], reads=[RO0], writes=["dbgls"],
              tag="dbgls", role="misc")
        p.dma("sync", io["dbg_g"], g_sb[0:4, :], reads=[RO0], writes=["dbgg"],
              tag="dbgg", role="misc")
    for hf in range(2):
        dst = xn[:].rearrange("p (a s) -> p a s", a=KC)
        srcx = io["x_in"][:, HALF * hf:HALF * (hf + 1)].rearrange("(a p) s -> p a s", p=128)
        p.dma("gpsimd", dst, srcx, reads=io.get("x_in_res", []), writes=[xn_res],
              tag=f"mxn2_{hf}", role="xs")
        for e_ in range(4):
            for kc in range(KC):
                p.emit("tensor", lambda e, e_=e_, kc=kc: e.matmul(
                    pb[0][0:16, :], amat[:, 16 * (KC * e_ + kc):16 * (KC * e_ + kc) + 16],
                    xn[:, HALF * kc:HALF * (kc + 1)],
                    start=(kc == 0), stop=(kc == KC - 1)),
                    reads=["iota0", xn_res], writes=[R[0]], tag=f"md{hf}_{e_}_{kc}")
            p.emit("scalar", lambda e: e.copy(mid_sb, pb[0][0:16, :]),
                   reads=[R[0]], writes=[RQ0], tag=f"mde{hf}_{e_}")
            p.dma("sync", ge_sb, g_sb[e_:e_ + 1, HALF * hf:HALF * (hf + 1)],
                  reads=[RO0], writes=[RO1], tag=f"gex{hf}_{e_}", role="misc")
            p.emit("tensor", lambda e, e_=e_, hf=hf: e.matmul(
                pb[1][0:16, :], cm.ones_row_f32[0:1, 0:16],
                ge_sb, start=True, stop=True),
                reads=["ones_row_f32", RO1], writes=[R[1]], tag=f"gb{hf}_{e_}")
            p.emit("vector", lambda e, e_=e_: e.tensor_mul(
                ms_sb[0:16, HALF * e_:HALF * (e_ + 1)], mid_sb, pb[1][0:16, :]),
                reads=[RQ0, R[1]], writes=[RQ1], tag=f"msx{hf}_{e_}")
        mpending = []
        for dt in range(KC):
            dbank = pb[2] if dt % 2 == 0 else pb[0]
            dR = R[2] if dt % 2 == 0 else R[0]
            for e_ in range(4):
                p.emit("tensor", lambda e, e_=e_, dt=dt, dbank=dbank: e.matmul(
                    dbank[:], bmat[0:16, D * e_ + 128 * dt: D * e_ + 128 * dt + 128],
                    ms_sb[0:16, HALF * e_:HALF * (e_ + 1)],
                    start=(e_ == 0), stop=(e_ == 3)),
                    reads=[scr_res, RQ1], writes=[dR], tag=f"dl{hf}_{dt}_{e_}")
            pbuf = dt % 2
            p.emit("scalar", lambda e, pbuf=pbuf, dbank=dbank: e.copy(dsub[pbuf], dbank[:]),
                   reads=[dR], writes=[RQ0], tag=f"dle{hf}_{dt}")
            mpending.append((dt, pbuf))
            if len(mpending) > 1:
                d0, pb0_ = mpending.pop(0)
                pln.add_chunk(io, hf, d0, dsub[pb0_], RQ0,
                              first=(d0 == 0), last=(d0 == KC - 1))
        for d0, pb0_ in mpending:
            pln.add_chunk(io, hf, d0, dsub[pb0_], RQ0,
                          first=(d0 == 0), last=(d0 == KC - 1))
        pln.finalize(io, hf)


def build_launch3(debug_x3=False):
    nc = bass.Bass("TRN2", target_bir_lowering=False, detect_race_conditions=False)
    x2 = nc.dram_tensor("x2", [D, TO], F32, kind="ExternalInput").ap()
    gathered = nc.dram_tensor("gathered", [2 * TO * NCORE, TO], BF16, kind="ExternalInput").ap()
    wq = nc.dram_tensor("wq", [NH, D, 128], F32, kind="ExternalInput").ap()
    wo = nc.dram_tensor("wo", [KC, D, 128], F32, kind="ExternalInput").ap()
    wr_t = nc.dram_tensor("wr_t", [D, 4], F32, kind="ExternalInput").ap()
    a_t = nc.dram_tensor("a_t", [128, 4 * KC * 16], F32, kind="ExternalInput").ap()
    b_t = nc.dram_tensor("b_t", [4, 16, D], F32, kind="ExternalInput").ap()
    c_in = nc.dram_tensor("c_in", [128, 1], F32, kind="ExternalInput").ap()
    delta_in = nc.dram_tensor("delta_in", [128, 8], F32, kind="ExternalInput").ap()
    out_d = nc.dram_tensor("out_d", [D, TO], F32, kind="ExternalOutput").ap()
    x3_dram = nc.dram_tensor("x3_dram", [D, TO], F32,
                             kind="ExternalOutput" if debug_x3 else "Internal").ap()
    v_dram = nc.dram_tensor("v_dram", [D, TO], F32, kind="Internal").ap()

    p = Prog(nc)
    cm = Common(p, want_l1_masks=True)
    cm.emit_setup(c_in, delta_in)
    io = dict(x_in=x2, res_in=x2, gathered=gathered, gathered_res="__in_g",
              wq=[wq[h] for h in range(NH)], wo=[wo[i] for i in range(KC)],
              v_dram=v_dram, x_out=x3_dram, x_out_res="x3d")
    L = dict(kv_mode="l1", nm="at")
    emit_attention_layer(p, cm, L, io)
    at_xn = None
    dbg_ls = dbg_g = None
    if debug_x3:
        dbg_ls = nc.dram_tensor("dbg_ls", [4, TO], F32, kind="ExternalOutput").ap()
        dbg_g = nc.dram_tensor("dbg_g", [4, TO], F32, kind="ExternalOutput").ap()
    x3res = [f"x3d_{hf}_{dt}" for hf in range(2) for dt in range(KC)]
    moe_io = dict(x_in=x3_dram, x_in_res=x3res, res_in=x3_dram, res_in_res=x3res,
                  dbg_ls=dbg_ls, dbg_g=dbg_g,
                  v_dram=v_dram, x_out=out_d, x_out_res="out",
                  wr_t=wr_t, a_t=a_t, b_t=b_t,
                  moe_xn=io["_xn_t"], moe_xn_res="atxn",
                  moe_scr=io["_attn_t"], moe_scr_res="atattn",
                  ar_wq0=io["_wq"][0], ar_wq1=io["_wq"][1],
                  ar_wo0=io["_wo"][0], ar_wo1=io["_wo"][1],
                  ar_res=["atwq0", "atwq1", "atwo0", "atwo1"],
                  pb=io["_pb"], pb_res=io["_pb_res"], _pln=io["_pln"])
    emit_moe_layer(p, cm, "moe", moe_io)
    p.barrier("sync", [f"out_{hf}_{dt}" for hf in range(2) for dt in range(KC)])
    p.build()
    nc._prog = p
    return nc


def host_prep_l1(inputs):
    wq_l = np.asarray(inputs["L1_attn_wq"], dtype=np.float32)[PI]
    wq_t = np.ascontiguousarray(
        wq_l.reshape(KC, 128, NH, 128).transpose(2, 1, 0, 3)
        .reshape(NH, 128, KC * 128)).astype(BF16_NP)
    wo_l = np.asarray(inputs["L1_attn_wo"], dtype=np.float32)[:, PI]
    wo_t = np.ascontiguousarray(
        wo_l.reshape(NH, 128, KC, 128).transpose(2, 1, 0, 3)
        .reshape(KC, 128, NH * 128)).astype(BF16_NP)
    wk_l = np.asarray(inputs["L1_attn_wk"], dtype=np.float32)[PI]
    wk_t = np.ascontiguousarray(
        wk_l.reshape(KC, 128, 8, 128).transpose(2, 1, 0, 3)
        .reshape(8, 128, KC * 128)).astype(BF16_NP)
    wv_l = np.asarray(inputs["L1_attn_wv"], dtype=np.float32)[PI]
    wv_t = np.ascontiguousarray(
        wv_l.reshape(KC, 128, 8, 128).transpose(2, 1, 0, 3)
        .reshape(8, 128, KC * 128)).astype(BF16_NP)
    wr_t = np.ascontiguousarray(np.asarray(inputs["L1_ffn_Wr"], np.float32).T[PI])
    a_l = np.asarray(inputs["L1_ffn_A"], np.float32).transpose(0, 2, 1)[:, PI, :]
    a_t = np.ascontiguousarray(a_l.reshape(4, KC, 128, 16).transpose(2, 0, 1, 3).reshape(128, 4 * KC * 16))
    b_t = np.ascontiguousarray(np.asarray(inputs["L1_ffn_B"], np.float32)[:, PI, :].transpose(0, 2, 1))
    per_core = []
    for c in range(NCORE):
        delta = np.zeros((128, 8), np.float32)
        for r in range(8):
            delta[:, r] = 0.0 if r <= c else -1.0
        per_core.append(dict(c_in=np.full((128, 1), float(c), dtype=np.float32),
                             delta_in=delta))
    shared = dict(wq=wq_t, wo=wo_t, wr_t=wr_t, a_t=a_t, b_t=b_t)
    return shared, per_core, wk_t, wv_t

def build_launch1():
    nc = bass.Bass("TRN2", target_bir_lowering=False, detect_race_conditions=False)
    xT = nc.dram_tensor("xT", [D, T], F32, kind="ExternalInput").ap()
    xTown = nc.dram_tensor("xTown", [D, TO], F32, kind="ExternalInput").ap()
    wq = nc.dram_tensor("wq", [NH, D, 128], F32, kind="ExternalInput").ap()
    wo = nc.dram_tensor("wo", [KC, D, 128], F32, kind="ExternalInput").ap()
    wk = nc.dram_tensor("wk", [D, 128], F32, kind="ExternalInput").ap()
    wv = nc.dram_tensor("wv", [D, 128], F32, kind="ExternalInput").ap()
    qb = nc.dram_tensor("qb", [128, NH], F32, kind="ExternalInput").ap()
    kb = nc.dram_tensor("kb", [128, 1], F32, kind="ExternalInput").ap()
    vbias = nc.dram_tensor("vbias", [1, 128], F32, kind="ExternalInput").ap()
    ob = nc.dram_tensor("ob", [128, KC], F32, kind="ExternalInput").ap()
    c_in = nc.dram_tensor("c_in", [128, 1], F32, kind="ExternalInput").ap()
    x1_out = nc.dram_tensor("x1_out", [D, TO], F32, kind="ExternalOutput").ap()
    v_dram = nc.dram_tensor("v_dram", [D, TO], F32, kind="Internal").ap()

    p = Prog(nc)
    cm = Common(p, scratch_view=None)
    cm.emit_setup(c_in)
    io = dict(xT=xT, x_in=xTown, res_in=xTown, wq=[wq[h] for h in range(NH)],
              wo=[wo[i] for i in range(KC)], wk=wk, wv=wv, qb=qb, kb=kb,
              vbias=vbias, ob=ob, v_dram=v_dram, x_out=x1_out, x_out_res="x1_out")
    emit_attention_layer(p, cm, dict(kv_mode="l0", nm="l0a"), io)
    p.barrier("sync", [f"x1_out_{hf}_{dt}" for hf in range(2) for dt in range(KC)])
    p.build()
    return nc


def host_prep_l0attn(inputs):
    x = np.asarray(inputs["L0_attn_x"], dtype=np.float32).reshape(T, D)
    xT = np.ascontiguousarray(x.T[PI])
    NS = T // HALF
    xTs = np.ascontiguousarray(
        xT.reshape(KC, 128, NS, HALF).transpose(1, 2, 0, 3)
        .reshape(128, NS * KC * HALF)).astype(BF16_NP)
    wq_l = np.asarray(inputs["L0_attn_wq_w"], dtype=np.float32).T[PI]
    wq_t = np.ascontiguousarray(
        wq_l.reshape(KC, 128, NH, 128).transpose(2, 1, 0, 3)
        .reshape(NH, 128, KC * 128)).astype(BF16_NP)
    wo_l = np.asarray(inputs["L0_attn_wo_w"], dtype=np.float32).T[:, PI]
    wo_t = np.ascontiguousarray(
        wo_l.reshape(NH, 128, KC, 128).transpose(2, 1, 0, 3)
        .reshape(KC, 128, NH * 128)).astype(BF16_NP)
    wk_t = np.ascontiguousarray(
        np.asarray(inputs["L0_attn_wk_w"], dtype=np.float32).T[PI]
        .reshape(KC, 128, 128).transpose(1, 0, 2).reshape(128, KC * 128)).astype(BF16_NP)
    wv_t = np.ascontiguousarray(
        np.asarray(inputs["L0_attn_wv_w"], dtype=np.float32).T[PI]
        .reshape(KC, 128, 128).transpose(1, 0, 2).reshape(128, KC * 128)).astype(BF16_NP)
    qb = np.ascontiguousarray(np.asarray(inputs["L0_attn_wq_b"], np.float32).reshape(NH, 128).T)
    kb = np.asarray(inputs["L0_attn_wk_b"], np.float32).reshape(128, 1).copy()
    vbias = np.asarray(inputs["L0_attn_wv_b"], np.float32).reshape(1, 128).copy()
    ob = np.ascontiguousarray(np.asarray(inputs["L0_attn_wo_b"], np.float32)[PI].reshape(KC, 128).T)
    shared = dict(xT=xTs, wq=wq_t, wo=wo_t, wk=wk_t, wv=wv_t, qb=qb, kb=kb,
                  vbias=vbias, ob=ob)
    per_core = []
    for c in range(NCORE):
        xTown = np.ascontiguousarray(xT[:, c::8])
        xno = np.ascontiguousarray(
            xTown.reshape(KC, 128, 2, HALF).transpose(1, 2, 0, 3)
            .reshape(128, 2 * KC * HALF)).astype(BF16_NP)
        per_core.append(dict(xTown=xTown, xno=xno,
                             c_in=np.full((128, 1), float(c), dtype=np.float32)))
    return shared, per_core



def build_launch2():
    nc = bass.Bass("TRN2", target_bir_lowering=False, detect_race_conditions=False)
    x1 = nc.dram_tensor("x1", [D, TO], F32, kind="ExternalInput").ap()
    qw_g = nc.dram_tensor("qw_g", [512, INTER_PAD], I32, kind="ExternalInput").ap()
    qw_u = nc.dram_tensor("qw_u", [512, INTER_PAD], I32, kind="ExternalInput").ap()
    qw_d = nc.dram_tensor("qw_d", [1376, D], I32, kind="ExternalInput").ap()
    sbc_g = nc.dram_tensor("sbc_g", [4, 128, INTER_PAD], F16, kind="ExternalInput").ap()
    sbc_u = nc.dram_tensor("sbc_u", [4, 128, INTER_PAD], F16, kind="ExternalInput").ap()
    sbc_d = nc.dram_tensor("sbc_d", [11, 128, D], F16, kind="ExternalInput").ap()
    gate_zeros = nc.dram_tensor("gate_zeros", [32, 1376], I32, kind="ExternalInput").ap()
    up_zeros = nc.dram_tensor("up_zeros", [32, 1376], I32, kind="ExternalInput").ap()
    down_zeros = nc.dram_tensor("down_zeros", [86, 512], I32, kind="ExternalInput").ap()
    e8 = nc.dram_tensor("e8", [8, 128], F32, kind="ExternalInput").ap()
    wk_t = nc.dram_tensor("wk_t", [8, D, 128], F32, kind="ExternalInput").ap()
    wv_t = nc.dram_tensor("wv_t", [8, D, 128], F32, kind="ExternalInput").ap()
    x2_out = nc.dram_tensor("x2_out", [D, TO], F32, kind="ExternalOutput").ap()
    contrib_out = nc.dram_tensor("contrib_out", [2 * TO, TO], BF16, kind="ExternalOutput").ap()
    v_dram = nc.dram_tensor("v_dram", [D, TO], F32, kind="Internal").ap()
    z1g = nc.dram_tensor("z1g", [32, INTER_PAD], BF16, kind="Internal").ap()
    z1u = nc.dram_tensor("z1u", [32, INTER_PAD], BF16, kind="Internal").ap()
    z1d = nc.dram_tensor("z1d", [86, D], BF16, kind="Internal").ap()
    zbc_g = nc.dram_tensor("zbc_g", [4, 128, INTER_PAD], BF16, kind="Internal").ap()
    zbc_u = nc.dram_tensor("zbc_u", [4, 128, INTER_PAD], BF16, kind="Internal").ap()
    zbc_d = nc.dram_tensor("zbc_d", [11, 128, D], BF16, kind="Internal").ap()

    p = Prog(nc)
    pb = [nc.alloc_psum_tensor(f"ffn_pb{i}", [128, 512], F32) for i in range(8)]
    inter_arena = nc.alloc_sbuf_tensor("inter_arena", [128, NT_I * HALF], BF16)
    cm = Common(p, want_masks=False, want_ffn=True, scratch_view=inter_arena[:, 0:20992])
    cm.emit_setup_consts()
    io = dict(x_in=x1, res_in=x1, x_out=x2_out, x_out_res="x2_out", v_dram=v_dram,
              inter_t=inter_arena, pb=pb, pb_res=[f"ffn_pb{i}" for i in range(8)],
              qw_g=qw_g, qw_u=qw_u, qw_d=qw_d,
              sbc_g=[sbc_g[i] for i in range(4)], sbc_u=[sbc_u[i] for i in range(4)],
              sbc_d=[sbc_d[i] for i in range(11)],
              gate_zeros=gate_zeros, up_zeros=up_zeros, down_zeros=down_zeros,
              e8=e8, z1g=z1g, z1u=z1u, z1d=z1d,
              zbc_g=[zbc_g[i] for i in range(4)], zbc_u=[zbc_u[i] for i in range(4)],
              zbc_d=[zbc_d[i] for i in range(11)])
    emit_ffn_setup(p, cm, io)
    emit_ffn_layer(p, cm, dict(nm="l0f"), io)
    cio = dict(x2=x2_out, x2_res=[f"x2_out_{hf}_{dt}" for hf in range(2) for dt in range(KC)],
               wk_t=[wk_t[i] for i in range(8)], wv_t=[wv_t[i] for i in range(8)],
               contrib=contrib_out, kvbuf=inter_arena[:, 0:8192], kvbuf_res="l0finter",
               cxn=io["_ffn_xn"], cxn_res=io["_ffn_xn_res"],
               cwkv=[inter_arena[:, 8192 + 4096 * i:8192 + 4096 * (i + 1)] for i in range(2)],
               cwkv_res=["l0finter", "l0finter"],
               pb=pb, pb_res=[f"ffn_pb{i}" for i in range(8)])
    emit_contrib(p, cm, "ctr", cio)
    p.barrier("sync", [f"x2_out_{hf}_{dt}" for hf in range(2) for dt in range(KC)]
              + [f"ctr_k{mt}_{hf}" for mt in range(8) for hf in range(2)]
              + [f"ctr_v{hf}_{tt}" for hf in range(2) for tt in range(4)])
    p.build()
    return nc


def host_prep_l0ffn(inputs):
    gq = np.asarray(inputs["L0_ffn_gate_qweight"])
    uq = np.asarray(inputs["L0_ffn_up_qweight"])
    dq = np.asarray(inputs["L0_ffn_down_qweight"])
    gs = np.asarray(inputs["L0_ffn_gate_scales"])
    us = np.asarray(inputs["L0_ffn_up_scales"])
    dsc = np.asarray(inputs["L0_ffn_down_scales"])
    gz = np.asarray(inputs["L0_ffn_gate_zeros"])
    uz = np.asarray(inputs["L0_ffn_up_zeros"])
    dz = np.asarray(inputs["L0_ffn_down_zeros"])
    qw_g = np.ascontiguousarray(gq[:, PIP])
    qw_u = np.ascontiguousarray(uq[:, PIP])
    qw_d = np.ascontiguousarray(dq[:, PI])
    def sbc_table(s, RC, W, perm):
        sp = s[:, perm]
        out = np.empty((RC, 128, W), dtype=s.dtype)
        for R in range(RC):
            g0 = 8 * R
            gr = min(8, sp.shape[0] - g0)
            blk = np.repeat(sp[g0:g0 + gr], 16, axis=0)
            out[R, :16 * gr] = blk
            if 16 * gr < 128:
                out[R, 16 * gr:] = 0
        return np.ascontiguousarray(out)
    sbc_g = sbc_table(gs, 4, INTER_PAD, PIP)
    sbc_u = sbc_table(us, 4, INTER_PAD, PIP)
    sbc_d = sbc_table(dsc, 11, D, PI)
    e8 = np.zeros((8, 128), dtype=np.float32)
    for g in range(8):
        e8[g, 16 * g:16 * g + 16] = 1.0
    shared = dict(qw_g=qw_g, qw_u=qw_u, qw_d=qw_d, sbc_g=sbc_g, sbc_u=sbc_u,
                  sbc_d=sbc_d, gate_zeros=gz, up_zeros=uz, down_zeros=dz, e8=e8)
    return shared




def build_fused(stop_after=None, no_ea_qbf=False, dbg=False):
    nc = bass.Bass("TRN2", target_bir_lowering=False, detect_race_conditions=False)
    xT = nc.dram_tensor("xT", [128, (T // HALF) * KC * HALF], BF16, kind="ExternalInput").ap()
    xTown = nc.dram_tensor("xTown", [D, TO], F32, kind="ExternalInput").ap()
    xno = nc.dram_tensor("xno", [128, 2 * KC * HALF], BF16, kind="ExternalInput").ap()
    wq0 = nc.dram_tensor("wq0", [NH, 128, KC * 128], BF16, kind="ExternalInput").ap()
    wo0 = nc.dram_tensor("wo0", [KC, 128, NH * 128], BF16, kind="ExternalInput").ap()
    wk0 = nc.dram_tensor("wk0", [128, KC * 128], BF16, kind="ExternalInput").ap()
    wv0 = nc.dram_tensor("wv0", [128, KC * 128], BF16, kind="ExternalInput").ap()
    qb = nc.dram_tensor("qb", [128, NH], F32, kind="ExternalInput").ap()
    kb = nc.dram_tensor("kb", [128, 1], F32, kind="ExternalInput").ap()
    vbias = nc.dram_tensor("vbias", [1, 128], F32, kind="ExternalInput").ap()
    ob = nc.dram_tensor("ob", [128, KC], F32, kind="ExternalInput").ap()
    c_in = nc.dram_tensor("c_in", [128, 1], F32, kind="ExternalInput").ap()
    delta_in = nc.dram_tensor("delta_in", [128, 8], F32, kind="ExternalInput").ap()
    qw_g = nc.dram_tensor("qw_g", [512, INTER_PAD], I32, kind="ExternalInput").ap()
    qw_u = nc.dram_tensor("qw_u", [512, INTER_PAD], I32, kind="ExternalInput").ap()
    qw_d = nc.dram_tensor("qw_d", [1376, D], I32, kind="ExternalInput").ap()
    sbc_g = nc.dram_tensor("sbc_g", [4, 128, INTER_PAD], F16, kind="ExternalInput").ap()
    sbc_u = nc.dram_tensor("sbc_u", [4, 128, INTER_PAD], F16, kind="ExternalInput").ap()
    sbc_d = nc.dram_tensor("sbc_d", [11, 128, D], F16, kind="ExternalInput").ap()
    gate_zeros = nc.dram_tensor("gate_zeros", [32, 1376], I32, kind="ExternalInput").ap()
    up_zeros = nc.dram_tensor("up_zeros", [32, 1376], I32, kind="ExternalInput").ap()
    down_zeros = nc.dram_tensor("down_zeros", [86, 512], I32, kind="ExternalInput").ap()
    e8 = nc.dram_tensor("e8", [8, 128], F32, kind="ExternalInput").ap()
    wk_t = nc.dram_tensor("wk_t", [8, 128, KC * 128], BF16, kind="ExternalInput").ap()
    wv_t = nc.dram_tensor("wv_t", [8, 128, KC * 128], BF16, kind="ExternalInput").ap()
    wq1 = nc.dram_tensor("wq1", [NH, 128, KC * 128], BF16, kind="ExternalInput").ap()
    wo1 = nc.dram_tensor("wo1", [KC, 128, NH * 128], BF16, kind="ExternalInput").ap()
    wr_t = nc.dram_tensor("wr_t", [D, 4], F32, kind="ExternalInput").ap()
    a_t = nc.dram_tensor("a_t", [128, 4 * KC * 16], F32, kind="ExternalInput").ap()
    b_t = nc.dram_tensor("b_t", [4, 16, D], F32, kind="ExternalInput").ap()
    out_d = nc.dram_tensor("out_d", [D, TO], F32, kind="ExternalOutput").ap()
    KI = "ExternalOutput" if dbg else "Internal"
    dbg_inter = nc.dram_tensor("dbg_inter", [128, 4096], BF16, kind=KI).ap()
    dbg_ps = nc.dram_tensor("dbg_ps", [128, 512], F32, kind=KI).ap()
    dbg_ps2 = nc.dram_tensor("dbg_ps2", [128, 512], F32, kind=KI).ap()
    dbg_wg = nc.dram_tensor("dbg_wg", [128, 512], BF16, kind=KI).ap()
    dbg_sbc = nc.dram_tensor("dbg_sbc", [128, 512], BF16, kind=KI).ap()
    dbg_zbc = nc.dram_tensor("dbg_zbc", [128, 512], BF16, kind=KI).ap()
    dbg_xn = nc.dram_tensor("dbg_xn", [128, 4096], BF16, kind=KI).ap()
    x1_dram = nc.dram_tensor("x1_dram", [D, TO], F32, kind=KI).ap()
    x2_dram = nc.dram_tensor("x2_dram", [D, TO], F32, kind=KI).ap()
    x3_dram = nc.dram_tensor("x3_dram", [D, TO], F32, kind="Internal").ap()
    v_dram = nc.dram_tensor("v_dram", [D, TO], F32, kind="Internal").ap()
    z1g = nc.dram_tensor("z1g", [32, INTER_PAD], BF16, kind="Internal").ap()
    z1u = nc.dram_tensor("z1u", [32, INTER_PAD], BF16, kind="Internal").ap()
    z1d = nc.dram_tensor("z1d", [86, D], BF16, kind="Internal").ap()
    zbc_g = nc.dram_tensor("zbc_g", [4, 128, INTER_PAD], BF16, kind="Internal").ap()
    zbc_u = nc.dram_tensor("zbc_u", [4, 128, INTER_PAD], BF16, kind="Internal").ap()
    zbc_d = nc.dram_tensor("zbc_d", [11, 128, D], BF16, kind="Internal").ap()
    contrib_d = nc.dram_tensor("contrib_d", [2 * TO, TO], BF16, kind="Internal").ap()
    gathered_d = nc.dram_tensor("gathered_d", [2 * TO * NCORE, TO], BF16, kind="Internal").ap()

    p = Prog(nc)
    pb = [nc.alloc_psum_tensor(f"pb{i}", [128, 512], F32) for i in range(8)]
    pb_res = [f"pb{i}" for i in range(8)]
    AR1 = nc.alloc_sbuf_tensor("AR1", [128, 16384], BF16)
    AR2 = nc.alloc_sbuf_tensor("AR2", [128, 45056], BF16)
    WO1 = nc.alloc_sbuf_tensor("WO1", [128, 4096], BF16)
    EA = nc.alloc_sbuf_tensor("EA", [128, 18432], BF16)
    arena = dict(
        xn=AR1[:],
        attn=AR2[:, 0:16384],
        k=AR2[:, 16384:24576],
        v=AR2[:, 24576:32768],
        wq=[AR2[:, 32768:36864], AR2[:, 36864:40960]],
        wo=[AR2[:, 40960:45056], WO1[:]],
    )
    at_res = ["atxn", "atattn", "atk", "atv", "atwq0", "atwq1", "atwo0", "atwo1",
              "atwkv"] + [f"atk{r}" for r in range(8)] + [f"atv{r}" for r in range(8)]
    ar2_res = [r for r in at_res if r not in ("atxn", "atwo1", "atwkv")]

    cm = Common(p, want_l1_masks=True, want_masks=True, want_ffn=True,
                scratch_view=AR2[:, 0:20992])
    cm.emit_setup(c_in, delta_in)

    ffn_io = dict(x_in=x1_dram, res_in=x1_dram,
                  x_in_res=[f"x1_{hf}_{dt}" for hf in range(2) for dt in range(KC)],
                  res_in_res=[f"x1_{hf}_{dt}" for hf in range(2) for dt in range(KC)],
                  x_out=x2_dram, x_out_res="x2", v_dram=v_dram,
                  qw_g=qw_g, qw_u=qw_u, qw_d=qw_d,
                  sbc_g=[sbc_g[i] for i in range(4)], sbc_u=[sbc_u[i] for i in range(4)],
                  sbc_d=[sbc_d[i] for i in range(11)],
                  gate_zeros=gate_zeros, up_zeros=up_zeros, down_zeros=down_zeros,
                  e8=e8, z1g=z1g, z1u=z1u, z1d=z1d,
                  zbc_g=[zbc_g[i] for i in range(4)], zbc_u=[zbc_u[i] for i in range(4)],
                  zbc_d=[zbc_d[i] for i in range(11)],
                  pb=pb, pb_res=pb_res,
                  xn_t=AR1[:], xn_res="atxn",
                  inter_t=AR2[:],
                  inter_claims=ar2_res + ["scratch", "stage"],
                  ea=EA[:], dbg_inter=dbg_inter, dbg_xn=dbg_xn, dbg_ps=dbg_ps,
                  dbg_ps2=dbg_ps2,
                  dbg_wg=dbg_wg, dbg_sbc=dbg_sbc, dbg_zbc=dbg_zbc)
    emit_ffn_setup(p, cm, ffn_io)

    p.claim(reads=[], writes=ar2_res + ["scratch", "stage"], tag="cl_setup_at")

    io0 = dict(xT=xT, x_in=xTown, x_in_c=xno, res_in=xTown,
               wq=[wq0[h] for h in range(NH)], wo=[wo0[i] for i in range(KC)],
               wk=wk0, wv=wv0, qb=qb, kb=kb, vbias=vbias, ob=ob,
               v_dram=v_dram, x_out=x1_dram, x_out_res="x1",
               arena=arena, pb=pb, pb_res=pb_res, wkv_t=EA[:, 0:8192],
               proj_t=[EA[:].bitcast(F32)[:, 7936 + 512 * i:8448 + 512 * i] for i in range(2)],
               proj_res=["l0ffsub0", "l0ffsub1"])
    emit_attention_layer(p, cm, dict(kv_mode="l0", nm="at"), io0)
    pln = io0["_pln"]

    p.claim(reads=[], writes=["atwkv", "l0fqwg0", "l0fqwg1", "l0fqwu0", "l0fqwu1",
                              "l0fqwd0", "l0fqwd1", "l0fqig0", "l0fqig1"],
            tag="cl_wkv_ea")
    ffn_io["_pln"] = pln
    zrow = p.buf("at_rec", [1, HALF], F32)
    p.emit("gpsimd", lambda e: e.memset(zrow[:], 0.0), writes=["zrow_s", "atrec"],
           tag="zrow_s")
    for i in range(8):
        p.emit("tensor", lambda e, i=i: e.matmul(
            pb[i][:], cm.ones_row_f32[:], zrow[0:1, 0:512], start=True, stop=True),
            reads=["ones_row_f32", "zrow_s", "atrec"], writes=[pb_res[i]], tag=f"scrub{i}")
    if no_ea_qbf:
        ffn_io["no_ea_qbf"] = True
    import os
    if os.environ.get("SWAP_BANKS"):
        ffn_io["swap_banks"] = True
    if os.environ.get("NT_REV"):
        ffn_io["nt_order"] = [3, 2, 1, 0]
    emit_ffn_layer(p, cm, dict(nm="l0f"), ffn_io)
    if stop_after == "ffn":
        p.barrier("sync", [f"x2_{hf}_{dt}" for hf in range(2) for dt in range(KC)])
        p.build()
        nc._prog = p
        return nc

    x2res = [f"x2_{hf}_{dt}" for hf in range(2) for dt in range(KC)]
    cio = dict(x2=x2_dram, x2_res=x2res,
               wk_t=[wk_t[i] for i in range(8)], wv_t=[wv_t[i] for i in range(8)],
               contrib=contrib_d, kvbuf=AR2[:, 0:8192], kvbuf_res="l0finter",
               cxn=AR1[:], cxn_res="atxn",
               cwkv=[AR2[:, 8192 + 4096 * i:8192 + 4096 * (i + 1)] for i in range(2)],
               cwkv_res=["l0finter", "l0finter"],
               pb=pb, pb_res=pb_res)
    emit_contrib(p, cm, "ctr", cio)
    ctr_res = [f"ctr_k{mt}_{hf}" for mt in range(8) for hf in range(2)] + \
              [f"ctr_v{hf}_{tt}" for hf in range(2) for tt in range(4)]
    p.emit("gpsimd", lambda e: e.collective_compute(
        "AllGather", ALU.bypass, replica_groups=[list(range(NCORE))],
        ins=[contrib_d.opt()], outs=[gathered_d.opt()]),
        reads=ctr_res, writes=["gathered"], tag="allgather")

    p.claim(reads=[], writes=ar2_res + ["l0finter"], tag="cl_ffn_at")

    io1 = dict(x_in=x2_dram, x_in_res=x2res, res_in=x2_dram, res_in_res=x2res,
               gathered=gathered_d, gathered_res="gathered",
               wq=[wq1[h] for h in range(NH)], wo=[wo1[i] for i in range(KC)],
               v_dram=v_dram, x_out=x3_dram, x_out_res="x3",
               arena=arena, pb=pb, pb_res=pb_res, _pln=pln,
               proj_t=[EA[:].bitcast(F32)[:, 7936 + 512 * i:8448 + 512 * i] for i in range(2)],
               proj_res=["l0ffsub0", "l0ffsub1"])
    emit_attention_layer(p, cm, dict(kv_mode="l1", nm="at"), io1)

    x3res = [f"x3_{hf}_{dt}" for hf in range(2) for dt in range(KC)]
    moe_io = dict(x_in=x3_dram, x_in_res=x3res, res_in=x3_dram, res_in_res=x3res,
                  v_dram=v_dram, x_out=out_d, x_out_res="out",
                  wr_t=wr_t, a_t=a_t, b_t=b_t,
                  moe_xn=AR1, moe_xn_res="atxn",
                  moe_scr=AR2[:, 0:16384], moe_scr_res="atattn",
                  ar_wq0=arena["wq"][0], ar_wq1=arena["wq"][1],
                  ar_wo0=arena["wo"][0], ar_wo1=arena["wo"][1],
                  ar_res=["atwq0", "atwq1", "atwo0", "atwo1"],
                  wr_view=EA[:, 15360:15488],
                  pb=pb, pb_res=pb_res, _pln=pln)
    emit_moe_layer(p, cm, "moe", moe_io)

    p.barrier("sync", [f"out_{hf}_{dt}" for hf in range(2) for dt in range(KC)])
    p.build()
    nc._prog = p
    return nc

def _reassemble(outs):
    out = np.empty((T, D), dtype=np.float32)
    for c in range(NCORE):
        nat = np.empty((D, TO), np.float32)
        nat[PI] = outs[c]
        out[c::8, :] = nat.T
    return out.reshape(1, T, D)


KERNEL_MODE = "fused"


def kernel(**inputs):
    if KERNEL_MODE == "fused":
        return kernel_fused(**inputs)
    return kernel_3launch(**inputs)


def kernel_fused(**inputs):
    shared1, per_core1 = host_prep_l0attn(inputs)
    shared2 = host_prep_l0ffn(inputs)
    shared3, per_core3, wk_t, wv_t = host_prep_l1(inputs)
    nc = build_fused()
    in_maps = []
    for c in range(NCORE):
        m = dict(shared2, wk_t=wk_t, wv_t=wv_t)
        m.update(xT=shared1["xT"], wq0=shared1["wq"], wo0=shared1["wo"],
                 wk0=shared1["wk"], wv0=shared1["wv"], qb=shared1["qb"],
                 kb=shared1["kb"], vbias=shared1["vbias"], ob=shared1["ob"],
                 wq1=shared3["wq"], wo1=shared3["wo"], wr_t=shared3["wr_t"],
                 a_t=shared3["a_t"], b_t=shared3["b_t"])
        m.update(per_core1[c])
        m.update(per_core3[c])
        in_maps.append(m)
    res = run_bass_kernel_spmd(nc, in_maps, core_ids=list(range(NCORE)))
    return _reassemble([res.results[c]["out_d"] for c in range(NCORE)])


def kernel_3launch(**inputs):
    shared1, per_core1 = host_prep_l0attn(inputs)
    shared2 = host_prep_l0ffn(inputs)
    shared3, per_core3, wk_t, wv_t = host_prep_l1(inputs)

    nc1 = build_launch1()
    in_maps = [dict(shared1, **per_core1[c]) for c in range(NCORE)]
    res1 = run_bass_kernel_spmd(nc1, in_maps, core_ids=list(range(NCORE)))
    x1s = [res1.results[c]["x1_out"] for c in range(NCORE)]

    nc2 = build_launch2()
    in_maps = [dict(shared2, x1=x1s[c], wk_t=wk_t, wv_t=wv_t) for c in range(NCORE)]
    res2 = run_bass_kernel_spmd(nc2, in_maps, core_ids=list(range(NCORE)))
    x2s = [res2.results[c]["x2_out"] for c in range(NCORE)]
    gathered = np.concatenate([res2.results[c]["contrib_out"] for c in range(NCORE)],
                              axis=0)

    nc3 = build_launch3()
    in_maps = [dict(shared3, **per_core3[c], gathered=gathered, x2=x2s[c])
               for c in range(NCORE)]
    res3 = run_bass_kernel_spmd(nc3, in_maps, core_ids=list(range(NCORE)))
    return _reassemble([res3.results[c]["out_d"] for c in range(NCORE)])
